# version 06
# baseline (speedup 1.0000x reference)
import numpy as np
import concourse.bass as bass
import concourse.mybir as mybir
from concourse.bass_utils import run_bass_kernel_spmd
from contextlib import ExitStack

F32 = mybir.dt.float32
BF16 = mybir.dt.bfloat16
U8 = mybir.dt.uint8
AF = mybir.ActivationFunctionType
ALU = mybir.AluOpType
AX = mybir.AxisListType.X

NCORES = 8
NB = 2
T = 2048
D = 1024
DI = 2048
DS = 16
DTR = 64
NE = 8
ISH = 2048
NT = 512
NCH = T // NT
PC_D = D // 128
PC_DI = DI // 128
TQ = 512
NTQ = T // TQ
EPS = 1e-5

ENGS = ("sync", "tensor", "vector", "scalar", "gpsimd")
ENG_SEM = {"vector": "sV", "scalar": "sA", "gpsimd": "sO", "tensor": "sP"}
ALL_SEMS = tuple(["qOut", "qC", "qC2", "sP", "sV", "sA", "sO",
                  "qG0", "qG1", "qP0", "qP1", "q_x", "q_x1s", "q_bt",
                  "q_ygl", "q_x1l"] +
                 [f"q_sl{i}" for i in range(4)] +
                 [f"{p}{i}" for p in ("q_zt", "q_xcv", "q_ir", "q_dts",
                                      "q_xcs", "q_szs", "q_yg") for i in range(2)])


class Sched:

    def __init__(self):
        self.q = {e: [] for e in ENGS}
        self.cnt = {s: 0 for s in ALL_SEMS}
        self.lastwait = {e: {} for e in ENGS}
        self.sems = {}

    def _wait_list(self, eng, waits):
        out = []
        for w in waits:
            if w is None:
                continue
            s, c = w
            if c <= 0:
                continue
            assert c <= self.cnt[s], f"wait on future count {s}={c} > {self.cnt[s]}"
            if self.lastwait[eng].get(s, 0) < c:
                self.lastwait[eng][s] = c
                out.append((s, c))
        return out

    def op(self, eng, fn, waits=(), inc=None, incn=1, selfsync=False):
        if selfsync and inc is None:
            inc = ENG_SEM[eng]
        w = self._wait_list(eng, list(waits))
        tok = None
        if inc is not None:
            self.cnt[inc] += incn
            tok = (inc, self.cnt[inc])
        self.q[eng].append((w, fn, tok, incn if inc else 0, selfsync))
        if selfsync:
            self.lastwait[eng][tok[0]] = tok[1]
        return tok

    def dma(self, eng, out_ap, in_ap, waits=(), sem="qOut"):
        def fn(e, out_ap=out_ap, in_ap=in_ap):
            return e.dma_start(out=out_ap, in_=in_ap)
        return self.op(eng, fn, waits=waits, inc=sem, incn=16)

    def barrier_all(self, exclude=()):
        toks = [(s, self.cnt[s]) for s in ALL_SEMS]
        for e in ENGS:
            if e in exclude:
                continue
            w = self._wait_list(e, toks)
            if w:
                self.q[e].append((w, None, None, 0, False))

    def emit(self, block):
        def make_body(eng_name):
            def body(e):
                for (w, fn, tok, incn, selfsync) in self.q[eng_name]:
                    for s, c in w:
                        e.wait_ge(self.sems[s], c)
                    if fn is None:
                        continue
                    instr = fn(e)
                    if tok is not None:
                        instr.then_inc(self.sems[tok[0]], incn)
                    if selfsync:
                        e.wait_ge(self.sems[tok[0]], tok[1])
            return body
        block.sync(make_body("sync"))
        block.tensor(make_body("tensor"))
        block.vector(make_body("vector"))
        block.scalar(make_body("scalar"))
        block.gpsimd(make_body("gpsimd"))


def build_program(debug=False):
    nc = bass.Bass("TRN2", target_bir_lowering=False)
    S = Sched()

    def din(name, shape):
        return nc.dram_tensor(name, shape, F32, kind="ExternalInput").ap()

    x_in_d = din("x", [NB, D, T])
    wg_d = din("wg", [D, NE])
    win_d = din("win", [D, 2 * DI])
    bin_d = din("binp", [128, 32])
    cw_d = din("cw", [128, 64])
    cb_d = din("cb", [128, 16])
    wx_d = din("wx", [DI, 96])
    bx_d = din("bx", [96, 1])
    bxc_d = din("bxc", [16, 1])
    wdt_d = din("wdt", [DTR, DI])
    bdt_d = din("bdt", [128, 16])
    alog_d = din("alog", [128, 256])
    dpar_d = din("dpar", [128, 16])
    wout_d = din("wout", [DI, D])
    bout_d = din("bout", [128, 8])
    w1_d = din("w1", [NE, D, D])
    b1_d = din("b1", [128, 64])
    w2_d = din("w2", [NE, D, D])
    b2_d = din("b2", [128, 64])
    w3_d = din("w3", [NE, D, D])
    b3_d = din("b3", [128, 64])
    sw1_d = din("sw1", [D, ISH])
    sw2_d = din("sw2", [D, ISH])
    sw3_d = din("sw3", [ISH, D])
    ident_d = din("ident", [128, 128])
    onesc_d = din("onesc", [128, 1])
    onesr_d = din("onesr", [1, 128])
    sel8_d = din("sel8", [NE, NE * 128])
    constv_d = din("constv", [128, 4])

    out_d = nc.dram_tensor("out", [NB, D, T], F32, kind="ExternalOutput").ap()

    def scr(name, shape, dt):
        kind = "ExternalOutput" if (debug and name.startswith(("s_x", "s_dbg"))
                                    or debug and name in ("s_sz", "s_dt", "s_bi",
                                                          "s_ci", "s_yg")) else "Internal"
        return nc.dram_tensor(name, shape, dt, kind=kind).ap()

    s_win = scr("s_win", [D, 2 * DI], BF16)
    s_w1 = scr("s_w1", [NE, D, D], BF16)
    s_w2 = scr("s_w2", [NE, D, D], BF16)
    s_w3 = scr("s_w3", [NE, D, D], BF16)
    s_sw1 = scr("s_sw1", [D, ISH], BF16)
    s_sw2 = scr("s_sw2", [D, ISH], BF16)
    s_sw3 = scr("s_sw3", [ISH, D], BF16)
    s_wout = scr("s_wout", [DI, D], BF16)
    s_wx = scr("s_wx", [DI, 96], BF16)
    s_wdt = scr("s_wdt", [DTR, DI], BF16)
    s_x1 = scr("s_x1", [NB, D, T], F32)
    s_xc = scr("s_xc", [NB, DI, T], F32)
    s_sz = scr("s_sz", [NB, DI, T], F32)
    s_dt = scr("s_dt", [NB, DI, T], F32)
    s_bi = scr("s_bi", [NB, T * DS], F32)
    s_ci = scr("s_ci", [NB, T * DS], F32)
    s_yg = scr("s_yg", [NB, DI, T], BF16)

    st = ExitStack()
    e = st.enter_context

    ident_sb = e(nc.sbuf_tensor("ident_sb", [128, 128], F32))
    onesc_sb = e(nc.sbuf_tensor("onesc_sb", [128, 1], F32))
    onesr_sb = e(nc.sbuf_tensor("onesr_sb", [1, 128], F32))
    sel8_sb = e(nc.sbuf_tensor("sel8_sb", [NE, NE * 128], F32))
    bin_sb = e(nc.sbuf_tensor("bin_sb", [128, 32], F32))
    cw_sb = e(nc.sbuf_tensor("cw_sb", [128, 64], F32))
    cb_sb = e(nc.sbuf_tensor("cb_sb", [128, 16], F32))
    bx_sb = e(nc.sbuf_tensor("bx_sb", [96, 1], F32))
    bxc_sb = e(nc.sbuf_tensor("bxc_sb", [16, 1], F32))
    bdt_sb = e(nc.sbuf_tensor("bdt_sb", [128, 16], F32))
    alog_sb = e(nc.sbuf_tensor("alog_sb", [128, 256], F32))
    aneg_sb = e(nc.sbuf_tensor("aneg_sb", [128, 256], F32))
    dpar_sb = e(nc.sbuf_tensor("dpar_sb", [128, 16], F32))
    bout_sb = e(nc.sbuf_tensor("bout_sb", [128, 8], F32))
    b1_sb = e(nc.sbuf_tensor("b1_sb", [128, 64], F32))
    b2_sb = e(nc.sbuf_tensor("b2_sb", [128, 64], F32))
    b3_sb = e(nc.sbuf_tensor("b3_sb", [128, 64], F32))
    ggf_sb = e(nc.sbuf_tensor("ggf_sb", [128, 64], F32))
    ghi_sb = e(nc.sbuf_tensor("ghi_sb", [128, 64], BF16))
    glo_sb = e(nc.sbuf_tensor("glo_sb", [128, 64], BF16))
    wx_sb = e(nc.sbuf_tensor("wx_sb", [128, 16 * 96], BF16))
    wdt_sb = e(nc.sbuf_tensor("wdt_sb", [DTR, DI], BF16))
    halo_sb = e(nc.sbuf_tensor("halo_sb", [128, 48], F32))
    state_sb = e(nc.sbuf_tensor("state_sb", [128, 256], F32))
    mean_r = e(nc.sbuf_tensor("mean_r", [1, NT], F32))
    rstd_r = e(nc.sbuf_tensor("rstd_r", [1, NT], F32))
    srow_r = e(nc.sbuf_tensor("srow_r", [1, NT], F32))
    scores_sb = e(nc.sbuf_tensor("scores_sb", [128, 8], F32))
    m8_sb = e(nc.sbuf_tensor("m8_sb", [128, 8], F32))
    m2_sb = e(nc.sbuf_tensor("m2_sb", [128, 8], F32))
    zap_sb = e(nc.sbuf_tensor("zap_sb", [128, 8], F32))
    wsel_sb = e(nc.sbuf_tensor("wsel_sb", [128, 8], F32))
    nmax_sb = e(nc.sbuf_tensor("nmax_sb", [128, 1], F32))
    ssum_sb = e(nc.sbuf_tensor("ssum_sb", [128, 1], F32))
    btile_sb = e(nc.sbuf_tensor("btile_sb", [128, 16], F32))
    dtlow_bf = e(nc.sbuf_tensor("dtlow_bf", [DTR, NT], BF16))
    constv_sb = e(nc.sbuf_tensor("constv_sb", [128, 4], F32))

    AW = e(nc.sbuf_tensor("AW", [128, 4 * 16384], U8))
    AACT = e(nc.sbuf_tensor("AACT", [128, 118784], U8))

    def aw_slot(i):
        return AW[:, i * 16384:(i + 1) * 16384].bitcast(BF16)

    def av(off_b, nbytes, dt):
        return AACT[:, off_b:off_b + nbytes].bitcast(dt)

    K = 1024
    x_in = av(0, 16 * K, F32)
    x1c_v = av(16 * K, 16 * K, F32)
    xh_bf = av(16 * K, 8 * K, BF16)
    xh_lo = av(24 * K, 8 * K, BF16)
    y_acc = av(32 * K, 16 * K, F32)
    xs_bf = av(48 * K, 8 * K, BF16)
    h_bf = av(56 * K, 8 * K, BF16)
    p_bf = av(64 * K, 16 * K, BF16)
    xf_tmp = av(80 * K, 2 * K, F32)
    g1s_bf = av(80 * K, K, BF16)
    xsq = [av(82 * K, 2 * K, F32), av(84 * K, 2 * K, F32)]
    x2m = av(86 * K, 2 * K, F32)
    tmp_f = av(88 * K, 2 * K, F32)
    wbc_bf = av(90 * K, K, BF16)
    mask_f = av(91 * K, 2 * K, F32)
    sp_a = av(93 * K, 2 * K, F32)
    sp_b = av(95 * K, 2 * K, F32)
    cvb = [av(97 * K, 2072, F32), av(97 * K + 2080, 2072, F32)]
    zt_sb = [av(102 * K, 2 * K, F32), av(104 * K, 2 * K, F32)]
    xcv_sb = [av(106 * K, 2 * K, F32), av(108 * K, 2 * K, F32)]
    dtv_sb = zt_sb
    brow_sb = av(110 * K, 2 * K, F32)[0:16, :].bitcast(F32)
    crow_sb = av(112 * K, 2 * K, F32)[0:16, :].bitcast(F32)
    wT_sb = av(114 * K, 2 * K, F32)[0:8, :].bitcast(F32)
    vh_bufs = [av(0, 32 * K, F32), av(70 * K, 32 * K, F32)]
    dts_sb = [av(32 * K, 2 * K, F32), av(34 * K, 2 * K, F32)]
    xcs_sb = [av(36 * K, 2 * K, F32), av(38 * K, 2 * K, F32)]
    szs_sb = [av(40 * K, 2 * K, F32), av(42 * K, 2 * K, F32)]
    u_sb = av(44 * K, 2 * K, F32)
    dA_sb = [av(46 * K, 2 * K, F32), av(48 * K, 2 * K, F32)]
    yt_sb = av(50 * K, 2 * K, F32)
    ygb_sb = [av(52 * K, K, BF16), av(53 * K, K, BF16)]
    irow_sb = [av(54 * K, 8 * K, F32)[0:1, :].bitcast(F32),
               av(62 * K, 8 * K, F32)[0:1, :].bitcast(F32)]
    Bint = AW[:, 0:32768].bitcast(F32)
    Cint = AW[:, 32768:65536].bitcast(F32)
    stg_sb = [av(0, 8 * K, BF16), av(8 * K, 8 * K, BF16)]

    PS = [e(nc.psum_tensor(f"PS{i}", [128, 512], F32)) for i in range(8)]
    bank_tok = [None] * 8

    for s in ALL_SEMS:
        S.sems[s] = e(nc.semaphore(s))
    block = e(nc.Block())

    def mm_group(out_view, bank, pieces, waits=(), tag=""):
        w = list(waits)
        if bank_tok[bank] is not None:
            w.append(bank_tok[bank])
        tok = None
        n = len(pieces)
        for i, (lh, rh) in enumerate(pieces):
            def fn(t, lh=lh, rh=rh, i=i, n=n, out_view=out_view):
                return t.matmul(out=out_view, lhsT=lh, rhs=rh,
                                start=(i == 0), stop=(i == n - 1))
            if i == n - 1:
                tok = S.op("tensor", fn, waits=w if i == 0 else (), inc="sP")
            else:
                S.op("tensor", fn, waits=w if i == 0 else ())
        return tok

    def act(out, in_, func, bias=None, scale=None, waits=()):
        def fn(s, out=out, in_=in_, func=func, bias=bias, scale=scale):
            kw = {}
            if bias is not None:
                kw["bias"] = bias
            if scale is not None:
                kw["scale"] = scale
            return s.activation(out=out, in_=in_, func=func, **kw)
        return S.op("scalar", fn, waits=waits, inc="sA")

    def vop(fn, waits=(), selfsync=False):
        return S.op("vector", fn, waits=waits, inc="sV", selfsync=selfsync)

    def pop_(fn, waits=()):
        return S.op("gpsimd", fn, waits=waits, inc="sO")

    slot_free = [None] * 4

    def load_slot(i, dram_ap_3d, waits=()):
        w = list(waits)
        if slot_free[i] is not None:
            w.append(slot_free[i])
        kc = dram_ap_3d.shape[1]
        m = dram_ap_3d.shape[2]
        sl = aw_slot(i)
        return S.dma("sync", sl[:, 0:kc * m].rearrange("p (kc m1) -> p kc m1", kc=kc),
                     dram_ap_3d, waits=w, sem=f"q_sl{i}")

    def expert_mat(dram, ei):
        return dram[ei].rearrange("(kc p) m -> p kc m", p=128)

    def win_view(sl):
        return s_win[sl * 256:(sl + 1) * 256].rearrange("(kc p) m -> p kc m", p=128)

    def xi_view(m):
        return AACT[:, 32768 + m * 2048: 32768 + (m + 1) * 2048].bitcast(F32)

    for dst, src in [(ident_sb, ident_d), (onesc_sb, onesc_d), (onesr_sb, onesr_d),
                     (sel8_sb, sel8_d), (bin_sb, bin_d), (cw_sb, cw_d),
                     (cb_sb, cb_d), (bx_sb, bx_d), (bxc_sb, bxc_d),
                     (bdt_sb, bdt_d), (alog_sb, alog_d), (dpar_sb, dpar_d),
                     (bout_sb, bout_d), (b1_sb, b1_d), (b2_sb, b2_d),
                     (b3_sb, b3_d), (constv_sb, constv_d)]:
        S.dma("sync", dst[:, :], src, sem="qC")
    for i_, v_ in enumerate([EPS, 1.0, -1.0, -0.5]):
        nc.const_aps.register(F32, v_, constv_sb[:, i_:i_ + 1])
    S.dma("sync", ggf_sb[:, :], wg_d.rearrange("(c p) e1 -> p c e1", p=128),
          sem="qC")
    tok_consts = ("qC", S.cnt["qC"])
    t_ghi = vop(lambda v: v.tensor_copy(out=ghi_sb[:, :], in_=ggf_sb[:, :]),
                waits=[tok_consts])
    t_glo = vop(lambda v: v.tensor_sub(out=glo_sb[:, :], in0=ggf_sb[:, :],
                                       in1=ghi_sb[:, :]), waits=[t_ghi])
    t_an = act(aneg_sb[:, :], alog_sb[:, :], AF.Exp, waits=[tok_consts])
    t_an = vop(lambda v: v.tensor_scalar(out=aneg_sb[:, :], in0=aneg_sb[:, :],
                                         scalar1=-1.0, scalar2=None, op0=ALU.mult),
               waits=[t_an])

    stg_free = [None, None]
    pctr = [0]

    def prep_one(df, ds_, r0, rows, cols):
        i = pctr[0] % 2
        pctr[0] += 1
        w = []
        if stg_free[i] is not None:
            w.append(stg_free[i])
        tok_c = S.dma("gpsimd", stg_sb[i][0:rows, 0:cols], df[r0:r0 + rows, :],
                      waits=w, sem=f"qG{i}")
        tok_s = S.dma("sync", ds_[r0:r0 + rows, :], stg_sb[i][0:rows, 0:cols],
                      waits=[tok_c], sem=f"qP{i}")
        stg_free[i] = tok_s

    ptok = {}

    def prep_done_tok():
        return [stg_free[0], stg_free[1]]

    for ei in range(NE):
        for nm, (df, ds_) in [("w1", (w1_d, s_w1)), ("w2", (w2_d, s_w2)),
                              ("w3", (w3_d, s_w3))]:
            for kc in range(8):
                prep_one(df[ei], ds_[ei], kc * 128, 128, D)
            ptok[(nm, ei)] = prep_done_tok()
    for nm, (df, ds_) in [("sw1", (sw1_d, s_sw1)), ("sw2", (sw2_d, s_sw2)),
                          ("win", (win_d, s_win)), ("sw3", (sw3_d, s_sw3)),
                          ("wout", (wout_d, s_wout)), ("wx", (wx_d, s_wx)),
                          ("wdt", (wdt_d, s_wdt))]:
        R = df.shape[0]
        r0 = 0
        while r0 < R:
            rows = min(128, R - r0)
            prep_one(df, ds_, r0, rows, df.shape[1])
            r0 += rows
        ptok[nm] = prep_done_tok()
    prep_toks = prep_done_tok()

    t_wx = S.dma("sync", wx_sb[:, :].rearrange("p (kc m1) -> p kc m1", kc=16),
                 s_wx.rearrange("(kc p) m -> p kc m", p=128), waits=ptok["wx"],
                 sem="qC2")
    S.dma("sync", wdt_sb[:, :], s_wdt, waits=ptok["wdt"], sem="qC2")
    t_wx = ("qC2", S.cnt["qC2"])
    t_wdt = t_wx

    def ln_full(src_view, src_toks, want_lo):
        ws = [t for t in src_toks if t is not None]
        w6 = list(ws)
        if bank_tok[6] is not None:
            w6.append(bank_tok[6])
        tok_sum = None
        for pc in range(PC_D):
            xb = src_view[:, pc * NT:(pc + 1) * NT]
            def fn(t, xb=xb, pc=pc):
                return t.matmul(out=PS[6][0:1, :], lhsT=onesc_sb[:, :], rhs=xb,
                                start=(pc == 0), stop=(pc == PC_D - 1))
            if pc == PC_D - 1:
                tok_sum = S.op("tensor", fn, waits=w6 if pc == 0 else (), inc="sP")
            else:
                S.op("tensor", fn, waits=w6 if pc == 0 else ())
        w7 = list(ws)
        if bank_tok[7] is not None:
            w7.append(bank_tok[7])
        sq_free = [None, None]
        tok_sq = None
        for pc in range(PC_D):
            xb = src_view[:, pc * NT:(pc + 1) * NT]
            wq = list(ws)
            if sq_free[pc % 2] is not None:
                wq.append(sq_free[pc % 2])
            t_s = act(xsq[pc % 2][:, :], xb, AF.Square, waits=wq)
            def fn(t, pc=pc):
                return t.matmul(out=PS[7][0:1, :], lhsT=onesc_sb[:, :],
                                rhs=xsq[pc % 2][:, :],
                                start=(pc == 0), stop=(pc == PC_D - 1))
            if pc == PC_D - 1:
                tok_sq = S.op("tensor", fn, waits=(w7 + [t_s]) if pc == 0 else [t_s],
                              inc="sP")
            else:
                tok_sq = S.op("tensor", fn, waits=(w7 + [t_s]) if pc == 0 else [t_s],
                              inc="sP")
            sq_free[pc % 2] = tok_sq
        t1 = vop(lambda v: v.tensor_scalar(out=mean_r[:, :], in0=PS[6][0:1, :],
                                           scalar1=1.0 / D, scalar2=None,
                                           op0=ALU.mult), waits=[tok_sum])
        t2 = vop(lambda v: v.tensor_scalar(out=srow_r[:, :], in0=PS[7][0:1, :],
                                           scalar1=1.0 / D, scalar2=None,
                                           op0=ALU.mult), waits=[tok_sq])
        t3 = vop(lambda v: v.tensor_mul(out=rstd_r[:, :], in0=mean_r[:, :],
                                        in1=mean_r[:, :]), waits=[t1])
        t4 = vop(lambda v: v.tensor_sub(out=srow_r[:, :], in0=srow_r[:, :],
                                        in1=rstd_r[:, :]), waits=[t2, t3])
        t5 = act(rstd_r[:, :], srow_r[:, :], AF.Ln, bias=float(EPS), waits=[t4])
        t6 = act(rstd_r[:, :], rstd_r[:, :], AF.Exp, scale=-0.5, waits=[t5])
        tok_mb = mm_group(PS[6][:, :], 6, [(onesr_sb[:, :], mean_r[:, :])],
                          waits=[t1])
        tok_rb = mm_group(PS[7][:, :], 7, [(onesr_sb[:, :], rstd_r[:, :])],
                          waits=[t6])
        toks_bf = []
        tok_lo = None
        for pc in range(PC_D):
            xb = src_view[:, pc * NT:(pc + 1) * NT]
            ta = vop(lambda v, xb=xb: v.tensor_sub(out=xf_tmp[:, :], in0=xb,
                                                   in1=PS[6][:, :]),
                     waits=[tok_mb])
            tb = vop(lambda v: v.tensor_mul(out=xf_tmp[:, :], in0=xf_tmp[:, :],
                                            in1=PS[7][:, :]), waits=[tok_rb, ta])
            tc = vop(lambda v, pc=pc: v.tensor_copy(
                out=xh_bf[:, pc * NT:(pc + 1) * NT], in_=xf_tmp[:, :]), waits=[tb])
            toks_bf.append(tc)
            if want_lo:
                tok_lo = vop(lambda v, pc=pc: v.tensor_sub(
                    out=xh_lo[:, pc * NT:(pc + 1) * NT], in0=xf_tmp[:, :],
                    in1=xh_bf[:, pc * NT:(pc + 1) * NT]), waits=[tc])
        bank_tok[6] = tok_lo if want_lo else toks_bf[-1]
        bank_tok[7] = bank_tok[6]
        return toks_bf, tok_lo

    def routing(bf_toks, tok_lo):
        tok_wT = None
        for tt in range(NT // 128):
            w = [bf_toks[-1], tok_lo, t_ghi, t_glo]
            if bank_tok[4] is not None:
                w.append(bank_tok[4])
            combos = []
            for pc in range(PC_D):
                lh_hi = xh_bf[:, pc * NT + tt * 128: pc * NT + (tt + 1) * 128]
                lh_lo = xh_lo[:, pc * NT + tt * 128: pc * NT + (tt + 1) * 128]
                combos.append((lh_hi, ghi_sb[:, pc * 8:(pc + 1) * 8]))
                combos.append((lh_hi, glo_sb[:, pc * 8:(pc + 1) * 8]))
                combos.append((lh_lo, ghi_sb[:, pc * 8:(pc + 1) * 8]))
            tokg = None
            for i, (lh, rh) in enumerate(combos):
                def fn(t, lh=lh, rh=rh, i=i, n=len(combos)):
                    return t.matmul(out=PS[4][:, 0:8], lhsT=lh, rhs=rh,
                                    start=(i == 0), stop=(i == n - 1))
                if i == len(combos) - 1:
                    tokg = S.op("tensor", fn, waits=w if i == 0 else (), inc="sP")
                else:
                    S.op("tensor", fn, waits=w if i == 0 else ())
            ta = vop(lambda v: v.reduce_max(out=nmax_sb[:, :], in_=PS[4][:, 0:8],
                                            axis=AX, negate=True),
                     waits=[tokg], selfsync=True)
            tb = act(scores_sb[:, :], PS[4][:, 0:8], AF.Exp, bias=nmax_sb[:, 0:1],
                     waits=[ta])
            bank_tok[4] = tb
            tc = vop(lambda v: v.reduce_sum(out=ssum_sb[:, :], in_=scores_sb[:, :],
                                            axis=AX), waits=[tb], selfsync=True)
            td = vop(lambda v: v.reciprocal(out=ssum_sb[:, :], in_=ssum_sb[:, :]),
                     waits=[tc], selfsync=True)
            te = vop(lambda v: v.tensor_scalar(out=scores_sb[:, :],
                                               in0=scores_sb[:, :],
                                               scalar1=ssum_sb[:, 0:1],
                                               scalar2=None, op0=ALU.mult),
                     waits=[td], selfsync=True)
            tf = vop(lambda v: v.max(out=m8_sb[:, :], in_=scores_sb[:, :]),
                     waits=[te], selfsync=True)
            tg = vop(lambda v: v.memset(m2_sb[:, :], -1.0), selfsync=True)
            th = vop(lambda v: v.tensor_copy(out=m2_sb[:, 0:2], in_=m8_sb[:, 0:2]),
                     waits=[tf, tg], selfsync=True)
            ti = vop(lambda v: v.match_replace(out=zap_sb[:, :],
                                               in_to_replace=m2_sb[:, :],
                                               in_values=scores_sb[:, :],
                                               imm_value=0.0),
                     waits=[th], selfsync=True)
            tj = vop(lambda v: v.tensor_sub(out=wsel_sb[:, :], in0=scores_sb[:, :],
                                            in1=zap_sb[:, :]),
                     waits=[ti], selfsync=True)
            w5 = [tj]
            if bank_tok[5] is not None:
                w5.append(bank_tok[5])
            def ftr(t):
                return t.transpose(out=PS[5][0:8, 0:128], in_=wsel_sb[:, :],
                                   identity=ident_sb[:, :])
            tk = S.op("tensor", ftr, waits=w5, inc="sP")
            tok_wT = vop(lambda v, tt=tt: v.tensor_copy(
                out=wT_sb[:, tt * 128:(tt + 1) * 128], in_=PS[5][0:8, 0:128]),
                waits=[tk])
            bank_tok[5] = tok_wT
        return tok_wT

    def ffn_prefetch():
        mats = []
        for ei in range(NE):
            mats += [("w1", ei), ("w2", ei), ("w3", ei)]
        dram_of = {"w1": s_w1, "w2": s_w2, "w3": s_w3}
        mat_tok = {}
        next_load = [0]

        def pump(n):
            while next_load[0] < len(mats) and next_load[0] < n:
                mi = next_load[0]
                kind, ei = mats[mi]
                sl = mi % 4
                mat_tok[(kind, ei)] = (sl, load_slot(
                    sl, expert_mat(dram_of[kind], ei), waits=ptok[(kind, ei)]))
                next_load[0] += 1
        return mat_tok, pump

    def ffn(bf_toks, tok_wT, mat_tok, pump):
        y_toks = [None] * PC_D
        prev_xs_last = None
        prev_x2_toks = []
        tok_mask = None
        for ei in range(NE):
            w6 = [tok_wT]
            if bank_tok[6] is not None:
                w6.append(bank_tok[6])
            def fbc(t, ei=ei):
                return t.matmul(out=PS[6][:, :],
                                lhsT=sel8_sb[:, ei * 128:(ei + 1) * 128],
                                rhs=wT_sb[:, :], start=True, stop=True)
            tbc = S.op("tensor", fbc, waits=w6, inc="sP")
            tw = act(wbc_bf[:, :], PS[6][:, :], AF.Identity,
                     waits=[tbc, prev_xs_last])
            bank_tok[6] = tw
            tok_mask = act(mask_f[:, :], wbc_bf[:, :], AF.Sign,
                           waits=[tw] + prev_x2_toks)
            xs_toks = []
            for pc in range(PC_D):
                xs_toks.append(vop(lambda v, pc=pc: v.tensor_mul(
                    out=xs_bf[:, pc * NT:(pc + 1) * NT],
                    in0=xh_bf[:, pc * NT:(pc + 1) * NT], in1=wbc_bf[:, :]),
                    waits=[tw, bf_toks[pc]]))
            prev_xs_last = xs_toks[-1]
            sl1, tok_w1 = mat_tok[("w1", ei)]
            h_toks = []
            for m in range(PC_D):
                bank = m % 2
                pieces = [(aw_slot(sl1).rearrange("p (kc m1) -> p kc m1", kc=8)
                           [:, kc, m * 128:(m + 1) * 128],
                           xs_bf[:, kc * NT:(kc + 1) * NT]) for kc in range(8)]
                tokm = mm_group(PS[bank][:, :], bank, pieces,
                                waits=[tok_w1] + xs_toks)
                t_h = act(h_bf[:, m * NT:(m + 1) * NT], PS[bank][:, :], AF.Silu,
                          bias=b1_sb[:, m * 8 + ei: m * 8 + ei + 1], waits=[tokm])
                bank_tok[bank] = t_h
                h_toks.append(t_h)
            sl2, tok_w2 = mat_tok[("w2", ei)]
            sl3, tok_w3 = mat_tok[("w3", ei)]
            prev_x2_toks = []
            for m in range(PC_D):
                b2k = 2 + (m % 2)
                pieces2 = [(aw_slot(sl2).rearrange("p (kc m1) -> p kc m1", kc=8)
                            [:, kc, m * 128:(m + 1) * 128],
                            h_bf[:, kc * NT:(kc + 1) * NT]) for kc in range(8)]
                tok2 = mm_group(PS[b2k][:, :], b2k, pieces2,
                                waits=[tok_w2] + h_toks)
                t_x2 = vop(lambda v, b2k=b2k, m=m, ei=ei: v.scalar_tensor_tensor(
                    out=x2m[:, :], in0=PS[b2k][:, :],
                    scalar=b2_sb[:, m * 8 + ei: m * 8 + ei + 1],
                    in1=mask_f[:, :], op0=ALU.add, op1=ALU.mult),
                    waits=[tok2, tok_mask])
                bank_tok[b2k] = t_x2
                prev_x2_toks.append(t_x2)
                b3k = 4 + (m % 2)
                pieces3 = [(aw_slot(sl3).rearrange("p (kc m1) -> p kc m1", kc=8)
                            [:, kc, m * 128:(m + 1) * 128],
                            xs_bf[:, kc * NT:(kc + 1) * NT]) for kc in range(8)]
                tok3 = mm_group(PS[b3k][:, :], b3k, pieces3,
                                waits=[tok_w3] + xs_toks)
                t_x3 = vop(lambda v, b3k=b3k, m=m, ei=ei: v.scalar_tensor_tensor(
                    out=tmp_f[:, :], in0=PS[b3k][:, :],
                    scalar=b3_sb[:, m * 8 + ei: m * 8 + ei + 1],
                    in1=x2m[:, :], op0=ALU.add, op1=ALU.mult),
                    waits=[tok3, t_x2] + ([y_toks[m - 1]] if m > 0 else
                                          ([y_toks[PC_D - 1]] if y_toks[PC_D - 1] else [])))
                bank_tok[b3k] = t_x3
                if ei == 0:
                    t_y = pop_(lambda g, m=m: g.tensor_copy(
                        out=y_acc[:, m * NT:(m + 1) * NT], in_=tmp_f[:, :]),
                        waits=[t_x3] + reg_free["y_acc"])
                else:
                    t_y = pop_(lambda g, m=m: g.tensor_add(
                        out=y_acc[:, m * NT:(m + 1) * NT],
                        in0=y_acc[:, m * NT:(m + 1) * NT], in1=tmp_f[:, :]),
                        waits=[t_x3, y_toks[m]])
                y_toks[m] = t_y
            done_tok = y_toks[PC_D - 1]
            for (sl, _) in (mat_tok[("w1", ei)], mat_tok[("w2", ei)],
                            mat_tok[("w3", ei)]):
                slot_free[sl] = done_tok
            pump((ei + 1) * 3 + 4)
        def sw_half(dram, half):
            return dram[half * 512:(half + 1) * 512].rearrange(
                "(kc p) m -> p kc m", p=128)
        t_sw1 = [load_slot(0, sw_half(s_sw1, 0), waits=ptok["sw1"]),
                 load_slot(1, sw_half(s_sw1, 1), waits=ptok["sw1"])]
        t_sw2 = [load_slot(2, sw_half(s_sw2, 0), waits=ptok["sw2"]),
                 load_slot(3, sw_half(s_sw2, 1), waits=ptok["sw2"])]
        p_toks = []
        prev_tp = None
        for m in range(PC_DI):
            bank = m % 2
            pieces = []
            for kc in range(8):
                sl = 0 if kc < 4 else 1
                pieces.append((aw_slot(sl).rearrange("p (kc m1) -> p kc m1", kc=4)
                               [:, kc % 4, m * 128:(m + 1) * 128],
                               xh_bf[:, kc * NT:(kc + 1) * NT]))
            tok1 = mm_group(PS[bank][:, :], bank, pieces,
                            waits=t_sw1 + [bf_toks[-1]])
            wg1 = [tok1]
            if prev_tp is not None:
                wg1.append(prev_tp)
            t_g1 = act(g1s_bf[:, :], PS[bank][:, :], AF.Silu, waits=wg1)
            bank_tok[bank] = t_g1
            bank2 = 2 + (m % 2)
            pieces = []
            for kc in range(8):
                sl = 2 if kc < 4 else 3
                pieces.append((aw_slot(sl).rearrange("p (kc m1) -> p kc m1", kc=4)
                               [:, kc % 4, m * 128:(m + 1) * 128],
                               xh_bf[:, kc * NT:(kc + 1) * NT]))
            tok2 = mm_group(PS[bank2][:, :], bank2, pieces,
                            waits=t_sw2 + [bf_toks[-1]])
            t_p = vop(lambda v, m=m, bank2=bank2: v.tensor_mul(
                out=p_bf[:, m * NT:(m + 1) * NT], in0=g1s_bf[:, :],
                in1=PS[bank2][:, :]),
                waits=[tok2, t_g1] + reg_free["p_bf"])
            bank_tok[bank2] = t_p
            prev_tp = t_p
            p_toks.append(t_p)
        slot_free[0] = p_toks[-1]
        slot_free[1] = p_toks[-1]
        t_sw3 = [load_slot(0, s_sw3[0:1024].rearrange("(kc p) m -> p kc m", p=128),
                           waits=ptok["sw3"]),
                 load_slot(1, s_sw3[1024:2048].rearrange("(kc p) m -> p kc m", p=128),
                           waits=ptok["sw3"])]
        for m in range(PC_D):
            bank = 4 + (m % 2)
            pieces = []
            for kc in range(16):
                sl = 0 if kc < 8 else 1
                pieces.append((aw_slot(sl).rearrange("p (kc m1) -> p kc m1", kc=8)
                               [:, kc % 8, m * 128:(m + 1) * 128],
                               p_bf[:, kc * NT:(kc + 1) * NT]))
            tokz = mm_group(PS[bank][:, :], bank, pieces, waits=t_sw3 + p_toks)
            t_y = vop(lambda v, m=m, bank=bank: v.tensor_add(
                out=y_acc[:, m * NT:(m + 1) * NT],
                in0=y_acc[:, m * NT:(m + 1) * NT], in1=PS[bank][:, :]),
                waits=[tokz, y_toks[m]])
            bank_tok[bank] = t_y
            y_toks[m] = t_y
        slot_free[0] = y_toks[-1]
        slot_free[1] = y_toks[-1]
        slot_free[2] = p_toks[-1]
        slot_free[3] = p_toks[-1]
        reg_free["p_bf"] = []
        reg_free["y_acc"] = []
        return y_toks

    reg_free = {"y_acc": [], "p_bf": []}
    halo_toks = [None] * PC_DI
    btile_free = [None]
    xin_free = [[]]
    c_free = [[]]
    dbg_braw = None
    if debug:
        dbg_braw = [scr("s_dbgB", [NB, 16, T], F32), scr("s_dbgC", [NB, 16, T], F32)]

    def mamba_pre(b, c, x1_toks):
        bf_toks, _ = ln_full(x_in[:, :], x1_toks, want_lo=False)
        ln2_bf = bf_toks
        t_win = [load_slot(sl, win_view(sl), waits=ptok["win"]) for sl in range(4)]
        xi_toks = [None] * PC_DI
        zt_free = [None, None]
        sl_of = [0, 0, 1, 1, 2, 2, 3, 3]
        for m in range(32):
            bank = m % 2
            pieces = []
            for kc in range(8):
                sl = sl_of[kc]
                pieces.append((aw_slot(sl).rearrange("p (kc m1) -> p kc m1", kc=2)
                               [:, kc % 2, m * 128:(m + 1) * 128],
                               xh_bf[:, kc * NT:(kc + 1) * NT]))
            tokm = mm_group(PS[bank][:, :], bank, pieces,
                            waits=t_win + [bf_toks[-1]])
            if m < PC_DI:
                t_xi = act(xi_view(m), PS[bank][:, :], AF.Identity,
                           bias=bin_sb[:, m:m + 1], waits=[tokm])
                bank_tok[bank] = t_xi
                xi_toks[m] = t_xi
            else:
                mz = m - PC_DI
                i = mz % 2
                w = [tokm]
                if zt_free[i] is not None:
                    w.append(zt_free[i])
                t_z = act(zt_sb[i][:, :], PS[bank][:, :], AF.Silu,
                          bias=bin_sb[:, m:m + 1], waits=w)
                bank_tok[bank] = t_z
                zt_free[i] = S.dma("sync", s_sz[b, mz * 128:(mz + 1) * 128,
                                               c * NT:(c + 1) * NT],
                                   zt_sb[i][:, :], waits=[t_z], sem=f"q_zt{i}")
        conv_bf_toks = []
        xcv_free = [None, None]
        sp_free = [None, None]
        for mi in range(PC_DI):
            even = (mi % 2 == 0)
            opf = vop
            cb_i = cvb[0] if even else cvb[1]
            sp_i = sp_a if even else sp_b
            t0 = opf(lambda v, mi=mi, cb_i=cb_i: v.tensor_copy(
                out=cb_i[:, 3:3 + NT], in_=xi_view(mi)), waits=[xi_toks[mi]])
            if c == 0:
                t1_ = opf(lambda v, cb_i=cb_i: v.memset(cb_i[:, 0:3], 0.0),
                          waits=[t0])
            else:
                t1_ = opf(lambda v, mi=mi, cb_i=cb_i: v.tensor_copy(
                    out=cb_i[:, 0:3], in_=halo_sb[:, mi * 3:(mi + 1) * 3]),
                    waits=[t0, halo_toks[mi]])
            w2_ = [t1_]
            if sp_free[mi % 2] is not None:
                w2_.append(sp_free[mi % 2])
            t2_ = opf(lambda v, mi=mi, cb_i=cb_i, sp_i=sp_i: v.tensor_scalar(
                out=sp_i[:, :], in0=cb_i[:, 0:NT],
                scalar1=cw_sb[:, mi * 4:mi * 4 + 1], scalar2=None, op0=ALU.mult),
                waits=w2_)
            for j in range(1, 4):
                t2_ = opf(lambda v, mi=mi, j=j, cb_i=cb_i, sp_i=sp_i:
                          v.scalar_tensor_tensor(
                              out=sp_i[:, :], in0=cb_i[:, j:j + NT],
                              scalar=cw_sb[:, mi * 4 + j:mi * 4 + j + 1],
                              in1=sp_i[:, :], op0=ALU.mult, op1=ALU.add),
                          waits=[t2_])
            halo_toks[mi] = opf(lambda v, mi=mi, cb_i=cb_i: v.tensor_copy(
                out=halo_sb[:, mi * 3:(mi + 1) * 3], in_=cb_i[:, NT:NT + 3]),
                waits=[t1_])
            i = mi % 2
            w = [t2_]
            if xcv_free[i] is not None:
                w.extend(xcv_free[i])
            t_cv = act(xcv_sb[i][:, :], sp_i[:, :], AF.Silu,
                       bias=cb_sb[:, mi:mi + 1], waits=w)
            sp_free[mi % 2] = t_cv
            tok_st = S.dma("sync", s_xc[b, mi * 128:(mi + 1) * 128,
                                        c * NT:(c + 1) * NT],
                           xcv_sb[i][:, :], waits=[t_cv], sem=f"q_xcv{i}")
            t_bf = vop(lambda v, mi=mi, i=i: v.tensor_copy(
                out=p_bf[:, mi * NT:(mi + 1) * NT], in_=xcv_sb[i][:, :]),
                waits=[t_cv])
            conv_bf_toks.append(t_bf)
            xcv_free[i] = [tok_st, t_bf]
        grp_defs = [(0, 64, 4, bx_sb[0:64, 0:1], dtlow_bf[:, :]),
                    (64, 80, 5, brow_sb[:, :], None),
                    (80, 96, 6, crow_sb[:, :], None)]
        row_toks = {}
        for (c0, c1, bidx, arg1, arg2) in grp_defs:
            w = conv_bf_toks + [t_wx]
            if bank_tok[bidx] is not None:
                w.append(bank_tok[bidx])
            tokd = None
            for kc in range(PC_DI):
                def fn(t, kc=kc, c0=c0, c1=c1, bidx=bidx):
                    return t.matmul(out=PS[bidx][0:c1 - c0, :],
                                    lhsT=wx_sb[:, :].rearrange(
                                        "p (kc m1) -> p kc m1", kc=16)[:, kc, c0:c1],
                                    rhs=p_bf[:, kc * NT:(kc + 1) * NT],
                                    start=(kc == 0), stop=(kc == PC_DI - 1))
                if kc == PC_DI - 1:
                    tokd = S.op("tensor", fn, waits=w if kc == 0 else (), inc="sP")
                else:
                    S.op("tensor", fn, waits=w if kc == 0 else ())
            if c0 == 0:
                t_e = act(dtlow_bf[:, :], PS[bidx][0:64, :], AF.Identity,
                          bias=bx_sb[0:64, 0:1], waits=[tokd])
            elif c0 == 64:
                t_e = act(brow_sb[:, :], PS[bidx][0:16, :], AF.Identity,
                          bias=bx_sb[64:80, 0:1], waits=[tokd])
            else:
                t_e = act(crow_sb[:, :], PS[bidx][0:16, :], AF.Identity,
                          bias=bxc_sb[0:16, 0:1], waits=[tokd])
            bank_tok[bidx] = t_e
            row_toks[c0] = t_e
        for (rows_sb, dst, base) in ((brow_sb, s_bi, 64), (crow_sb, s_ci, 80)):
            if dbg_braw is not None:
                S.dma("sync", dbg_braw[0 if base == 64 else 1]
                      [b, :, c * NT:(c + 1) * NT], rows_sb[:, :],
                      waits=[row_toks[base]])
            for tt in range(NT // 128):
                w5 = [row_toks[base]]
                if bank_tok[3] is not None:
                    w5.append(bank_tok[3])
                def ftr(t, rows_sb=rows_sb, tt=tt):
                    return t.transpose(out=PS[3][0:128, 0:16],
                                       in_=rows_sb[0:16, tt * 128:(tt + 1) * 128],
                                       identity=ident_sb[0:16, 0:16])
                tk = S.op("tensor", ftr, waits=w5, inc="sP")
                w_bt = [tk]
                if btile_free[0] is not None:
                    w_bt.append(btile_free[0])
                t_bt = vop(lambda v: v.tensor_copy(out=btile_sb[:, :],
                                                   in_=PS[3][0:128, 0:16]),
                           waits=w_bt)
                bank_tok[3] = t_bt
                off = (c * NT + tt * 128) * DS
                btile_free[0] = S.dma("sync", dst[b, off:off + 128 * DS].rearrange(
                    "(p s1) -> p s1", s1=DS), btile_sb[:, :], waits=[t_bt],
                    sem="q_bt")
        dtv_free = list(zt_free)
        prev_td = [None, None]
        for mi in range(PC_DI):
            bank = mi % 2
            w = [row_toks[0], t_wdt]
            if bank_tok[bank] is not None:
                w.append(bank_tok[bank])
            def fdt(t, mi=mi, bank=bank):
                return t.matmul(out=PS[bank][:, :],
                                lhsT=wdt_sb[:, mi * 128:(mi + 1) * 128],
                                rhs=dtlow_bf[:, :], start=True, stop=True)
            tokd = S.op("tensor", fdt, waits=w, inc="sP")
            wa = [tokd]
            if prev_td[0] is not None:
                wa.append(prev_td[0])
            t_a = act(sp_a[:, :], PS[bank][:, :], AF.Abs,
                      bias=bdt_sb[:, mi:mi + 1], waits=wa)
            wr = [t_a]
            if prev_td[1] is not None:
                wr.append(prev_td[1])
            t_r = act(sp_b[:, :], PS[bank][:, :], AF.Relu,
                      bias=bdt_sb[:, mi:mi + 1], waits=wr)
            bank_tok[bank] = t_r
            t_e1 = act(sp_a[:, :], sp_a[:, :], AF.Exp, scale=-1.0, waits=[t_r])
            t_e2 = act(sp_a[:, :], sp_a[:, :], AF.Ln, bias=1.0, waits=[t_e1])
            i = mi % 2
            w2_ = [t_e2]
            if dtv_free[i] is not None:
                w2_.append(dtv_free[i])
            t_d = vop(lambda v, i=i: v.tensor_add(out=dtv_sb[i][:, :],
                                                  in0=sp_a[:, :], in1=sp_b[:, :]),
                      waits=w2_)
            prev_td[0] = t_d
            prev_td[1] = t_d
            dtv_free[i] = S.dma("sync", s_dt[b, mi * 128:(mi + 1) * 128,
                                             c * NT:(c + 1) * NT],
                                dtv_sb[i][:, :], waits=[t_d], sem=f"q_zt{i}")
        return bf_toks


    def phase_a(b, c):
        S.barrier_all(exclude=("sync",))
        mat_tok, pump = ffn_prefetch()
        pump(4)
        wx_ = list(xin_free[0])
        xin_free[0] = []
        tok_x = S.dma("sync", x_in[:, :].rearrange("p (pc t1) -> p pc t1", pc=PC_D),
                      x_in_d[b].rearrange("(pc p) t1 -> p pc t1", p=128)
                      [:, :, c * NT:(c + 1) * NT], sem="q_x", waits=wx_)
        reg_free["y_acc"] = []
        reg_free["p_bf"] = []
        bf_toks, tok_lo = ln_full(x_in[:, :], [tok_x], want_lo=True)
        tok_wT = routing(bf_toks, tok_lo)
        y_toks = ffn(bf_toks, tok_wT, mat_tok, pump)
        x1_toks = []
        for m in range(PC_D):
            t_x1 = vop(lambda v, m=m: v.scalar_tensor_tensor(
                out=x_in[:, m * NT:(m + 1) * NT],
                in0=y_acc[:, m * NT:(m + 1) * NT], scalar=0.5,
                in1=x_in[:, m * NT:(m + 1) * NT],
                op0=ALU.mult, op1=ALU.add), waits=[y_toks[m]])
            x1_toks.append(t_x1)
        t_x1s = S.dma("sync", s_x1[b][:, c * NT:(c + 1) * NT].rearrange(
            "(pc p) t1 -> p pc t1", p=128),
            x_in[:, :].rearrange("p (pc t1) -> p pc t1", pc=PC_D),
            waits=x1_toks, sem="q_x1s")
        ln2_toks = mamba_pre(b, c, x1_toks)
        xin_free[0] = [t_x1s, ln2_toks[-1]]

    def phase_b(b):
        irow_free = [None, None]
        slice_free = [None, None]
        yg_free = [None, None]
        vh_free = [None, None]
        bint_free = [None]
        prev_tyg = [None]
        st_toks = [None] * PC_DI
        for tq in range(NTQ):
            bc_toks = []
            for (src, dst_int, bank) in ((s_bi, Bint, 6), (s_ci, Cint, 7)):
                for piece in range(4):
                    i = piece % 2
                    off = (tq * TQ + piece * 128) * DS
                    w = []
                    if irow_free[i] is not None:
                        w.append(irow_free[i])
                    t_r = S.dma("sync", irow_sb[i][:, :], src[b, off:off + 2048],
                                waits=w, sem=f"q_ir{i}")
                    last_ev = None
                    for qn in range(4):
                        wq = [t_r]
                        if bank_tok[bank] is not None:
                            wq.append(bank_tok[bank])
                        if bint_free[0] is not None:
                            wq.append(bint_free[0])
                        def fb(t, i=i, qn=qn, bank=bank):
                            return t.matmul(out=PS[bank][:, :], lhsT=onesr_sb[:, :],
                                            rhs=irow_sb[i][0:1, qn * 512:(qn + 1) * 512],
                                            start=True, stop=True)
                        tkb = S.op("tensor", fb, waits=wq, inc="sP")
                        t_ev = act(dst_int[:, piece * 2048 + qn * 512:
                                           piece * 2048 + (qn + 1) * 512],
                                   PS[bank][:, :], AF.Identity, waits=[tkb])
                        bank_tok[bank] = t_ev
                        last_ev = t_ev
                        bc_toks.append(t_ev)
                    irow_free[i] = last_ev
            for dc in range(PC_DI):
                i = dc % 2
                vh = vh_bufs[dc % 2]
                w = []
                if slice_free[i] is not None:
                    w.append(slice_free[i])
                t_dt = S.dma("sync", dts_sb[i][:, :],
                             s_dt[b, dc * 128:(dc + 1) * 128, tq * TQ:(tq + 1) * TQ],
                             waits=w, sem=f"q_dts{i}")
                t_xc = S.dma("sync", xcs_sb[i][:, :],
                             s_xc[b, dc * 128:(dc + 1) * 128, tq * TQ:(tq + 1) * TQ],
                             sem=f"q_xcs{i}")
                t_sz = S.dma("sync", szs_sb[i][:, :],
                             s_sz[b, dc * 128:(dc + 1) * 128, tq * TQ:(tq + 1) * TQ],
                             sem=f"q_szs{i}")
                wu = [t_dt, t_xc]
                t_u = pop_(lambda g, i=i: g.tensor_mul(out=u_sb[:, :],
                                                       in0=dts_sb[i][:, :],
                                                       in1=xcs_sb[i][:, :]),
                           waits=wu)
                wv = [t_u] + bc_toks
                if vh_free[dc % 2] is not None:
                    wv.append(vh_free[dc % 2])
                t_v = pop_(lambda g, vh=vh: g.tensor_tensor(
                    out=vh[:, :].rearrange("p (t1 s1) -> p t1 s1", s1=DS),
                    in0=u_sb[:, :].unsqueeze(2).to_broadcast([128, TQ, DS]),
                    in1=Bint[:, :].rearrange("p (t1 s1) -> p t1 s1", s1=DS),
                    op=ALU.mult), waits=wv)
                dA_free = [None, None]
                t_scan_last = None
                for s_i in range(DS):
                    ia = s_i % 2
                    wA = [t_dt, t_an]
                    if dA_free[ia] is not None:
                        wA.append(dA_free[ia])
                    t_dA = act(dA_sb[ia][:, :], dts_sb[i][:, :], AF.Exp,
                               scale=aneg_sb[:, dc * DS + s_i: dc * DS + s_i + 1],
                               waits=wA)
                    init = 0.0 if tq == 0 else state_sb[:, dc * DS + s_i:
                                                        dc * DS + s_i + 1]
                    def fsc(v, s_i=s_i, ia=ia, init=init, vh=vh):
                        vv = vh[:, :].rearrange("p (t1 s1) -> p t1 s1", s1=DS)
                        return v.tensor_tensor_scan(
                            out=vv[:, :, s_i], data0=dA_sb[ia][:, :],
                            data1=vv[:, :, s_i], initial=init,
                            op0=ALU.mult, op1=ALU.add)
                    t_sc = vop(fsc, waits=[t_dA, t_v] +
                               ([st_toks[dc]] if tq > 0 else []))
                    dA_free[ia] = t_sc
                    t_scan_last = t_sc
                t_st = vop(lambda v, dc=dc, vh=vh: v.tensor_copy(
                    out=state_sb[:, dc * DS:(dc + 1) * DS],
                    in_=vh[:, (TQ - 1) * DS:TQ * DS]), waits=[t_scan_last],
                    selfsync=True)
                st_toks[dc] = t_st
                if dc % 2 == 0:
                    t_hc = vop(lambda v, vh=vh: v.tensor_mul(out=vh[:, :],
                                                             in0=vh[:, :],
                                                             in1=Cint[:, :]),
                               waits=[t_st])
                else:
                    t_hc = pop_(lambda g, vh=vh: g.tensor_mul(out=vh[:, :],
                                                              in0=vh[:, :],
                                                              in1=Cint[:, :]),
                                waits=[t_st])
                wy = [t_hc]
                if prev_tyg[0] is not None:
                    wy.append(prev_tyg[0])
                t_y = vop(lambda v, vh=vh: v.tensor_reduce(
                    out=yt_sb[:, :],
                    in_=vh[:, :].rearrange("p (t1 s1) -> p t1 s1", s1=DS),
                    axis=AX, op=ALU.add), waits=wy, selfsync=True)
                vh_free[dc % 2] = t_y
                bint_free[0] = t_y
                t_yd = vop(lambda v, i=i, dc=dc: v.scalar_tensor_tensor(
                    out=yt_sb[:, :], in0=xcs_sb[i][:, :],
                    scalar=dpar_sb[:, dc:dc + 1], in1=yt_sb[:, :],
                    op0=ALU.mult, op1=ALU.add), waits=[t_y])
                iy = dc % 2
                wyg = [t_yd, t_sz]
                if yg_free[iy] is not None:
                    wyg.append(yg_free[iy])
                t_yg = pop_(lambda g, i=i, iy=iy: g.tensor_mul(
                    out=ygb_sb[iy][:, :], in0=yt_sb[:, :], in1=szs_sb[i][:, :]),
                    waits=wyg)
                prev_tyg[0] = t_yg
                yg_free[iy] = S.dma("sync", s_yg[b, dc * 128:(dc + 1) * 128,
                                                tq * TQ:(tq + 1) * TQ],
                                    ygb_sb[iy][:, :], waits=[t_yg],
                                    sem=f"q_yg{iy}")
                slice_free[i] = t_yg

    def phase_c(b, c):
        S.barrier_all(exclude=("sync",))
        yg_v = av(0, 16 * K, BF16)
        x2_v = av(0, 16 * K, F32)
        mat_tok_c, pump_c = ffn_prefetch()
        wc_ = list(c_free[0])
        c_free[0] = []
        tok_yg = S.dma("sync", yg_v[:, :].rearrange("p (kc t1) -> p kc t1", kc=PC_DI),
                       s_yg[b].rearrange("(kc p) t1 -> p kc t1", p=128)
                       [:, :, c * NT:(c + 1) * NT], sem="q_ygl", waits=wc_)
        tok_x1 = S.dma("sync", x1c_v[:, :].rearrange("p (pc t1) -> p pc t1", pc=PC_D),
                       s_x1[b].rearrange("(pc p) t1 -> p pc t1", p=128)
                       [:, :, c * NT:(c + 1) * NT], sem="q_x1l")
        t_wo = [load_slot(0, s_wout[0:1024].rearrange("(kc p) m -> p kc m", p=128),
                          waits=ptok["wout"]),
                load_slot(1, s_wout[1024:2048].rearrange("(kc p) m -> p kc m", p=128),
                          waits=ptok["wout"])]
        mm_toks = []
        for m in range(PC_D):
            pieces = []
            for kc in range(PC_DI):
                sl = 0 if kc < 8 else 1
                pieces.append((aw_slot(sl).rearrange("p (kc m1) -> p kc m1", kc=8)
                               [:, kc % 8, m * 128:(m + 1) * 128],
                               yg_v[:, kc * NT:(kc + 1) * NT]))
            mm_toks.append(mm_group(PS[m][:, :], m, pieces,
                                    waits=t_wo + [tok_yg]))
        x2_toks = []
        for m in range(PC_D):
            t_b = vop(lambda v, m=m: v.scalar_tensor_tensor(
                out=x2_v[:, m * NT:(m + 1) * NT], in0=PS[m][:, :],
                scalar=bout_sb[:, m:m + 1], in1=x1c_v[:, m * NT:(m + 1) * NT],
                op0=ALU.add, op1=ALU.add),
                waits=[mm_toks[m], tok_x1, mm_toks[PC_D - 1]])
            bank_tok[m] = t_b
            x2_toks.append(t_b)
        slot_free[0] = x2_toks[-1]
        slot_free[1] = x2_toks[-1]
        pump_c(4)
        reg_free["y_acc"] = []
        reg_free["p_bf"] = []
        bf_toks, tok_lo = ln_full(x2_v[:, :], x2_toks, want_lo=True)
        tok_wT = routing(bf_toks, tok_lo)
        y_toks = ffn(bf_toks, tok_wT, mat_tok_c, pump_c)
        out_toks = []
        for m in range(PC_D):
            t_o = vop(lambda v, m=m: v.scalar_tensor_tensor(
                out=x2_v[:, m * NT:(m + 1) * NT],
                in0=y_acc[:, m * NT:(m + 1) * NT], scalar=0.5,
                in1=x2_v[:, m * NT:(m + 1) * NT],
                op0=ALU.mult, op1=ALU.add), waits=[y_toks[m]])
            out_toks.append(t_o)
        t_os = S.dma("sync", out_d[b].rearrange("(pc p) t1 -> p pc t1", p=128)
              [:, :, c * NT:(c + 1) * NT],
              x2_v[:, :].rearrange("p (pc t1) -> p pc t1", pc=PC_D),
              waits=out_toks, sem="qOut")
        c_free[0] = [t_os, y_toks[-1]]

    xin_free[0] = [prep_toks[0], prep_toks[1]]
    for b in range(NB):
        for c in range(NCH):
            phase_a(b, c)
    for b in range(NB):
        S.barrier_all()
        phase_b(b)
    S.barrier_all()
    for b in range(NB):
        for c in range(NCH):
            phase_c(b, c)
    S.barrier_all()

    S.emit(block)
    st.close()
    return nc


def _prep_inputs(inputs):
    f = np.float32

    def t2(a):
        return np.ascontiguousarray(np.asarray(a).T)

    ii = {}
    ii["wg"] = t2(inputs["L0_ffn_gate_w"])
    ii["win"] = t2(inputs["L0_attn_W_in"])
    ii["binp"] = np.ascontiguousarray(
        np.asarray(inputs["L0_attn_b_in"]).reshape(32, 128).T)
    ii["cw"] = np.ascontiguousarray(
        np.asarray(inputs["L0_attn_conv_w"]).reshape(2048, 4)
        .reshape(16, 128, 4).transpose(1, 0, 2).reshape(128, 64))
    ii["cb"] = np.ascontiguousarray(
        np.asarray(inputs["L0_attn_conv_b"]).reshape(16, 128).T)
    ii["wx"] = t2(inputs["L0_attn_W_x"])
    ii["bx"] = np.ascontiguousarray(np.asarray(inputs["L0_attn_b_x"]).reshape(96, 1))
    ii["bxc"] = np.ascontiguousarray(
        np.asarray(inputs["L0_attn_b_x"])[80:96].reshape(16, 1))
    ii["wdt"] = t2(inputs["L0_attn_W_dt"])
    ii["bdt"] = np.ascontiguousarray(
        np.asarray(inputs["L0_attn_b_dt"]).reshape(16, 128).T)
    ii["alog"] = np.ascontiguousarray(
        np.asarray(inputs["L0_attn_A_log"]).reshape(16, 128, 16)
        .transpose(1, 0, 2).reshape(128, 256))
    ii["dpar"] = np.ascontiguousarray(
        np.asarray(inputs["L0_attn_D_param"]).reshape(16, 128).T)
    ii["wout"] = t2(inputs["L0_attn_W_out"])
    ii["bout"] = np.ascontiguousarray(
        np.asarray(inputs["L0_attn_b_out"]).reshape(8, 128).T)
    for nm, w_, b_ in (("w1", "L0_ffn_w1", "L0_ffn_b1"),
                       ("w2", "L0_ffn_w2", "L0_ffn_b2"),
                       ("w3", "L0_ffn_w3", "L0_ffn_b3")):
        ii[nm] = np.ascontiguousarray(np.asarray(inputs[w_]).transpose(0, 2, 1))
        ii["b" + nm[1]] = np.ascontiguousarray(
            np.asarray(inputs[b_]).T.reshape(8, 128, 8).transpose(1, 0, 2)
            .reshape(128, 64))
    ii["sw1"] = t2(inputs["L0_ffn_sw1"])
    ii["sw2"] = t2(inputs["L0_ffn_sw2"])
    ii["sw3"] = t2(inputs["L0_ffn_sw3"])
    ii["ident"] = np.eye(128, dtype=f)
    ii["onesc"] = np.ones((128, 1), dtype=f)
    ii["onesr"] = np.ones((1, 128), dtype=f)
    ii["constv"] = np.tile(np.array([[1e-5, 1.0, -1.0, -0.5]], dtype=f), (128, 1))
    sel8 = np.zeros((8, 8 * 128), dtype=f)
    for e_ in range(8):
        sel8[e_, e_ * 128:(e_ + 1) * 128] = 1.0
    ii["sel8"] = sel8
    for k in ii:
        assert ii[k].dtype == np.float32, k
    return ii


def kernel(**inputs):
    x = np.asarray(inputs["L0_attn_x"])
    shared = _prep_inputs(inputs)
    nc = build_program()
    in_maps = []
    for core in range(NCORES):
        m = dict(shared)
        m["x"] = np.ascontiguousarray(
            x[core * NB:(core + 1) * NB].transpose(0, 2, 1))
        in_maps.append(m)
    res = run_bass_kernel_spmd(nc, in_maps, core_ids=list(range(NCORES)))
    outs = [np.asarray(res.results[core]["out"]).transpose(0, 2, 1)
            for core in range(NCORES)]
    return np.ascontiguousarray(np.concatenate(outs, axis=0))


if __name__ == "__main__":
    import reference as _r
    ins = {k: np.asarray(v) for k, v in _r.setup_inputs().items()}
    out = kernel(**ins)
    ref = np.asarray(_r.reference(**ins))
    rv = float(np.mean((out - ref) ** 2) / np.mean(ref ** 2))
    print("maxabs", np.abs(out - ref).max(), "relvar", rv)


# version 07
# speedup vs baseline: 1.0344x; 1.0344x over previous
import numpy as np
import concourse.bass as bass
import concourse.mybir as mybir
from concourse.bass_utils import run_bass_kernel_spmd
from contextlib import ExitStack

F32 = mybir.dt.float32
BF16 = mybir.dt.bfloat16
U8 = mybir.dt.uint8
AF = mybir.ActivationFunctionType
ALU = mybir.AluOpType
AX = mybir.AxisListType.X

NCORES = 8
NB = 2
T = 2048
D = 1024
DI = 2048
DS = 16
DTR = 64
NE = 8
ISH = 2048
NT = 512
NCH = T // NT
PC_D = D // 128
PC_DI = DI // 128
TQ = 512
NTQ = T // TQ
EPS = 1e-5

ENGS = ("sync", "tensor", "vector", "scalar", "gpsimd")
ENG_SEM = {"vector": "sV", "scalar": "sA", "gpsimd": "sO", "tensor": "sP"}
ALL_SEMS = tuple(["qOut", "qC", "qC2", "sP", "sV", "sA", "sO",
                  "qG0", "qG1", "qP0", "qP1", "q_x", "q_x1s", "q_bt",
                  "q_ygl", "q_x1l"] +
                 [f"q_sl{i}" for i in range(4)] +
                 [f"{p}{i}" for p in ("q_zt", "q_xcv", "q_ir", "q_dts",
                                      "q_xcs", "q_szs", "q_yg") for i in range(2)])


class Sched:

    def __init__(self):
        self.q = {e: [] for e in ENGS}
        self.cnt = {s: 0 for s in ALL_SEMS}
        self.lastwait = {e: {} for e in ENGS}
        self.sems = {}

    def _wait_list(self, eng, waits):
        out = []
        for w in waits:
            if w is None:
                continue
            s, c = w
            if c <= 0:
                continue
            assert c <= self.cnt[s], f"wait on future count {s}={c} > {self.cnt[s]}"
            if self.lastwait[eng].get(s, 0) < c:
                self.lastwait[eng][s] = c
                out.append((s, c))
        return out

    def op(self, eng, fn, waits=(), inc=None, incn=1, selfsync=False):
        if selfsync and inc is None:
            inc = ENG_SEM[eng]
        w = self._wait_list(eng, list(waits))
        tok = None
        if inc is not None:
            self.cnt[inc] += incn
            tok = (inc, self.cnt[inc])
        self.q[eng].append((w, fn, tok, incn if inc else 0, selfsync))
        if selfsync:
            self.lastwait[eng][tok[0]] = tok[1]
        return tok

    def dma(self, eng, out_ap, in_ap, waits=(), sem="qOut"):
        def fn(e, out_ap=out_ap, in_ap=in_ap):
            return e.dma_start(out=out_ap, in_=in_ap)
        return self.op(eng, fn, waits=waits, inc=sem, incn=16)

    def barrier_all(self, exclude=()):
        toks = [(s, self.cnt[s]) for s in ALL_SEMS]
        for e in ENGS:
            if e in exclude:
                continue
            w = self._wait_list(e, toks)
            if w:
                self.q[e].append((w, None, None, 0, False))

    def emit(self, block):
        def make_body(eng_name):
            def body(e):
                for (w, fn, tok, incn, selfsync) in self.q[eng_name]:
                    for s, c in w:
                        e.wait_ge(self.sems[s], c)
                    if fn is None:
                        continue
                    instr = fn(e)
                    if tok is not None:
                        instr.then_inc(self.sems[tok[0]], incn)
                    if selfsync:
                        e.wait_ge(self.sems[tok[0]], tok[1])
            return body
        block.sync(make_body("sync"))
        block.tensor(make_body("tensor"))
        block.vector(make_body("vector"))
        block.scalar(make_body("scalar"))
        block.gpsimd(make_body("gpsimd"))


def build_program(debug=False):
    nc = bass.Bass("TRN2", target_bir_lowering=False)
    S = Sched()

    def din(name, shape):
        return nc.dram_tensor(name, shape, F32, kind="ExternalInput").ap()

    x_in_d = din("x", [NB, D, T])
    wg_d = din("wg", [D, NE])
    win_d = din("win", [D, 2 * DI])
    bin_d = din("binp", [128, 32])
    cw_d = din("cw", [128, 64])
    cb_d = din("cb", [128, 16])
    wx_d = din("wx", [DI, 96])
    bx_d = din("bx", [96, 1])
    bxc_d = din("bxc", [16, 1])
    wdt_d = din("wdt", [DTR, DI])
    bdt_d = din("bdt", [128, 16])
    alog_d = din("alog", [128, 256])
    dpar_d = din("dpar", [128, 16])
    wout_d = din("wout", [DI, D])
    bout_d = din("bout", [128, 8])
    w1_d = din("w1", [NE, D, D])
    b1_d = din("b1", [128, 64])
    w2_d = din("w2", [NE, D, D])
    b2_d = din("b2", [128, 64])
    w3_d = din("w3", [NE, D, D])
    b3_d = din("b3", [128, 64])
    sw1_d = din("sw1", [D, ISH])
    sw2_d = din("sw2", [D, ISH])
    sw3_d = din("sw3", [ISH, D])
    ident_d = din("ident", [128, 128])
    onesc_d = din("onesc", [128, 1])
    onesr_d = din("onesr", [1, 128])
    sel8_d = din("sel8", [NE, NE * 128])
    constv_d = din("constv", [128, 4])

    out_d = nc.dram_tensor("out", [NB, D, T], F32, kind="ExternalOutput").ap()

    def scr(name, shape, dt):
        kind = "ExternalOutput" if (debug and name.startswith(("s_x", "s_dbg"))
                                    or debug and name in ("s_sz", "s_dt", "s_bi",
                                                          "s_ci", "s_yg")) else "Internal"
        return nc.dram_tensor(name, shape, dt, kind=kind).ap()

    s_win = scr("s_win", [D, 2 * DI], BF16)
    s_w1 = scr("s_w1", [NE, D, D], BF16)
    s_w2 = scr("s_w2", [NE, D, D], BF16)
    s_w3 = scr("s_w3", [NE, D, D], BF16)
    s_sw1 = scr("s_sw1", [D, ISH], BF16)
    s_sw2 = scr("s_sw2", [D, ISH], BF16)
    s_sw3 = scr("s_sw3", [ISH, D], BF16)
    s_wout = scr("s_wout", [DI, D], BF16)
    s_wx = scr("s_wx", [DI, 96], BF16)
    s_wdt = scr("s_wdt", [DTR, DI], BF16)
    s_x1 = scr("s_x1", [NB, D, T], F32)
    s_xc = scr("s_xc", [NB, DI, T], F32)
    s_sz = scr("s_sz", [NB, DI, T], F32)
    s_dt = scr("s_dt", [NB, DI, T], F32)
    s_bi = scr("s_bi", [NB, T * DS], F32)
    s_ci = scr("s_ci", [NB, T * DS], F32)
    s_yg = scr("s_yg", [NB, DI, T], BF16)

    st = ExitStack()
    e = st.enter_context

    ident_sb = e(nc.sbuf_tensor("ident_sb", [128, 128], F32))
    onesc_sb = e(nc.sbuf_tensor("onesc_sb", [128, 1], F32))
    onesr_sb = e(nc.sbuf_tensor("onesr_sb", [1, 128], F32))
    sel8_sb = e(nc.sbuf_tensor("sel8_sb", [NE, NE * 128], F32))
    bin_sb = e(nc.sbuf_tensor("bin_sb", [128, 32], F32))
    cw_sb = e(nc.sbuf_tensor("cw_sb", [128, 64], F32))
    cb_sb = e(nc.sbuf_tensor("cb_sb", [128, 16], F32))
    bx_sb = e(nc.sbuf_tensor("bx_sb", [96, 1], F32))
    bxc_sb = e(nc.sbuf_tensor("bxc_sb", [16, 1], F32))
    bdt_sb = e(nc.sbuf_tensor("bdt_sb", [128, 16], F32))
    alog_sb = e(nc.sbuf_tensor("alog_sb", [128, 256], F32))
    aneg_sb = e(nc.sbuf_tensor("aneg_sb", [128, 256], F32))
    dpar_sb = e(nc.sbuf_tensor("dpar_sb", [128, 16], F32))
    bout_sb = e(nc.sbuf_tensor("bout_sb", [128, 8], F32))
    b1_sb = e(nc.sbuf_tensor("b1_sb", [128, 64], F32))
    b2_sb = e(nc.sbuf_tensor("b2_sb", [128, 64], F32))
    b3_sb = e(nc.sbuf_tensor("b3_sb", [128, 64], F32))
    ggf_sb = e(nc.sbuf_tensor("ggf_sb", [128, 64], F32))
    ghi_sb = e(nc.sbuf_tensor("ghi_sb", [128, 64], BF16))
    glo_sb = e(nc.sbuf_tensor("glo_sb", [128, 64], BF16))
    wx_sb = e(nc.sbuf_tensor("wx_sb", [128, 16 * 96], BF16))
    wdt_sb = e(nc.sbuf_tensor("wdt_sb", [DTR, DI], BF16))
    halo_sb = e(nc.sbuf_tensor("halo_sb", [128, 48], F32))
    state_sb = e(nc.sbuf_tensor("state_sb", [128, 256], F32))
    mean_r = e(nc.sbuf_tensor("mean_r", [1, NT], F32))
    rstd_r = e(nc.sbuf_tensor("rstd_r", [1, NT], F32))
    srow_r = e(nc.sbuf_tensor("srow_r", [1, NT], F32))
    scores_sb = e(nc.sbuf_tensor("scores_sb", [128, 8], F32))
    m8_sb = e(nc.sbuf_tensor("m8_sb", [128, 8], F32))
    m2_sb = e(nc.sbuf_tensor("m2_sb", [128, 8], F32))
    zap_sb = e(nc.sbuf_tensor("zap_sb", [128, 8], F32))
    wsel_sb = e(nc.sbuf_tensor("wsel_sb", [128, 8], F32))
    nmax_sb = e(nc.sbuf_tensor("nmax_sb", [128, 1], F32))
    ssum_sb = e(nc.sbuf_tensor("ssum_sb", [128, 1], F32))
    btile_sb = e(nc.sbuf_tensor("btile_sb", [128, 16], F32))
    dtlow_bf = e(nc.sbuf_tensor("dtlow_bf", [DTR, NT], BF16))
    constv_sb = e(nc.sbuf_tensor("constv_sb", [128, 4], F32))

    AW = e(nc.sbuf_tensor("AW", [128, 4 * 16384], U8))
    AACT = e(nc.sbuf_tensor("AACT", [128, 118784], U8))

    def aw_slot(i):
        return AW[:, i * 16384:(i + 1) * 16384].bitcast(BF16)

    def av(off_b, nbytes, dt):
        return AACT[:, off_b:off_b + nbytes].bitcast(dt)

    K = 1024
    x_in = av(0, 16 * K, F32)
    x1c_v = av(16 * K, 16 * K, F32)
    xh_bf = av(16 * K, 8 * K, BF16)
    xh_lo = av(24 * K, 8 * K, BF16)
    y_acc = av(32 * K, 16 * K, F32)
    xs_bf = av(48 * K, 8 * K, BF16)
    h_bf = av(56 * K, 8 * K, BF16)
    p_bf = av(64 * K, 16 * K, BF16)
    xf_tmp = av(80 * K, 2 * K, F32)
    g1s_bf = av(80 * K, K, BF16)
    xsq = [av(82 * K, 2 * K, F32), av(84 * K, 2 * K, F32)]
    x2m = av(86 * K, 2 * K, F32)
    tmp_f = av(88 * K, 2 * K, F32)
    wbc_bf = av(90 * K, K, BF16)
    mask_f = av(91 * K, 2 * K, F32)
    sp_a = av(93 * K, 2 * K, F32)
    sp_b = av(95 * K, 2 * K, F32)
    cvb = [av(97 * K, 2072, F32), av(97 * K + 2080, 2072, F32)]
    zt_sb = [av(102 * K, 2 * K, F32), av(104 * K, 2 * K, F32)]
    xcv_sb = [av(106 * K, 2 * K, F32), av(108 * K, 2 * K, F32)]
    dtv_sb = zt_sb
    brow_sb = av(110 * K, 2 * K, F32)[0:16, :].bitcast(F32)
    crow_sb = av(112 * K, 2 * K, F32)[0:16, :].bitcast(F32)
    wT_sb = av(114 * K, 2 * K, F32)[0:8, :].bitcast(F32)
    vh_bufs = [av(0, 32 * K, F32), av(70 * K, 32 * K, F32)]
    dts_sb = [av(32 * K, 2 * K, F32), av(34 * K, 2 * K, F32)]
    xcs_sb = [av(36 * K, 2 * K, F32), av(38 * K, 2 * K, F32)]
    szs_sb = [av(40 * K, 2 * K, F32), av(42 * K, 2 * K, F32)]
    u_sb = av(44 * K, 2 * K, F32)
    dA_sb = [av(46 * K, 2 * K, F32), av(48 * K, 2 * K, F32)]
    yt_sb = av(50 * K, 2 * K, F32)
    ygb_sb = [av(52 * K, K, BF16), av(53 * K, K, BF16)]
    irow_sb = [av(54 * K, 8 * K, F32)[0:1, :].bitcast(F32),
               av(62 * K, 8 * K, F32)[0:1, :].bitcast(F32)]
    Bint = AW[:, 0:32768].bitcast(F32)
    Cint = AW[:, 32768:65536].bitcast(F32)
    stg_sb = [av(0, 8 * K, BF16), av(8 * K, 8 * K, BF16)]

    PS = [e(nc.psum_tensor(f"PS{i}", [128, 512], F32)) for i in range(8)]
    bank_tok = [None] * 8

    for s in ALL_SEMS:
        S.sems[s] = e(nc.semaphore(s))
    block = e(nc.Block())

    def mm_group(out_view, bank, pieces, waits=(), tag=""):
        w = list(waits)
        if bank_tok[bank] is not None:
            w.append(bank_tok[bank])
        tok = None
        n = len(pieces)
        for i, (lh, rh) in enumerate(pieces):
            def fn(t, lh=lh, rh=rh, i=i, n=n, out_view=out_view):
                return t.matmul(out=out_view, lhsT=lh, rhs=rh,
                                start=(i == 0), stop=(i == n - 1))
            if i == n - 1:
                tok = S.op("tensor", fn, waits=w if i == 0 else (), inc="sP")
            else:
                S.op("tensor", fn, waits=w if i == 0 else ())
        return tok

    def act(out, in_, func, bias=None, scale=None, waits=()):
        def fn(s, out=out, in_=in_, func=func, bias=bias, scale=scale):
            kw = {}
            if bias is not None:
                kw["bias"] = bias
            if scale is not None:
                kw["scale"] = scale
            return s.activation(out=out, in_=in_, func=func, **kw)
        return S.op("scalar", fn, waits=waits, inc="sA")

    def vop(fn, waits=(), selfsync=False):
        return S.op("vector", fn, waits=waits, inc="sV", selfsync=selfsync)

    def pop_(fn, waits=()):
        return S.op("gpsimd", fn, waits=waits, inc="sO")

    slot_free = [None] * 4

    def load_slot(i, dram_ap_3d, waits=()):
        w = list(waits)
        if slot_free[i] is not None:
            w.append(slot_free[i])
        kc = dram_ap_3d.shape[1]
        m = dram_ap_3d.shape[2]
        sl = aw_slot(i)
        return S.dma("sync", sl[:, 0:kc * m].rearrange("p (kc m1) -> p kc m1", kc=kc),
                     dram_ap_3d, waits=w, sem=f"q_sl{i}")

    def expert_mat(dram, ei):
        return dram[ei].rearrange("(kc p) m -> p kc m", p=128)

    def win_view(sl):
        return s_win[sl * 256:(sl + 1) * 256].rearrange("(kc p) m -> p kc m", p=128)

    def xi_view(m):
        return AACT[:, 32768 + m * 2048: 32768 + (m + 1) * 2048].bitcast(F32)

    for dst, src in [(ident_sb, ident_d), (onesc_sb, onesc_d), (onesr_sb, onesr_d),
                     (sel8_sb, sel8_d), (bin_sb, bin_d), (cw_sb, cw_d),
                     (cb_sb, cb_d), (bx_sb, bx_d), (bxc_sb, bxc_d),
                     (bdt_sb, bdt_d), (alog_sb, alog_d), (dpar_sb, dpar_d),
                     (bout_sb, bout_d), (b1_sb, b1_d), (b2_sb, b2_d),
                     (b3_sb, b3_d), (constv_sb, constv_d)]:
        S.dma("sync", dst[:, :], src, sem="qC")
    for i_, v_ in enumerate([EPS, 1.0, -1.0, -0.5]):
        nc.const_aps.register(F32, v_, constv_sb[:, i_:i_ + 1])
    S.dma("sync", ggf_sb[:, :], wg_d.rearrange("(c p) e1 -> p c e1", p=128),
          sem="qC")
    tok_consts = ("qC", S.cnt["qC"])
    t_ghi = vop(lambda v: v.tensor_copy(out=ghi_sb[:, :], in_=ggf_sb[:, :]),
                waits=[tok_consts])
    t_glo = vop(lambda v: v.tensor_sub(out=glo_sb[:, :], in0=ggf_sb[:, :],
                                       in1=ghi_sb[:, :]), waits=[t_ghi])
    t_an = act(aneg_sb[:, :], alog_sb[:, :], AF.Exp, waits=[tok_consts])
    t_an = vop(lambda v: v.tensor_scalar(out=aneg_sb[:, :], in0=aneg_sb[:, :],
                                         scalar1=-1.0, scalar2=None, op0=ALU.mult),
               waits=[t_an])

    stg_free = [None, None]
    pctr = [0]

    def prep_one(df, ds_, r0, rows, cols):
        i = pctr[0] % 2
        pctr[0] += 1
        w = []
        if stg_free[i] is not None:
            w.append(stg_free[i])
        tok_c = S.dma("gpsimd", stg_sb[i][0:rows, 0:cols], df[r0:r0 + rows, :],
                      waits=w, sem=f"qG{i}")
        tok_s = S.dma("sync", ds_[r0:r0 + rows, :], stg_sb[i][0:rows, 0:cols],
                      waits=[tok_c], sem=f"qP{i}")
        stg_free[i] = tok_s

    ptok = {}

    def prep_done_tok():
        return [stg_free[0], stg_free[1]]

    for ei in range(NE):
        for nm, (df, ds_) in [("w1", (w1_d, s_w1)), ("w2", (w2_d, s_w2)),
                              ("w3", (w3_d, s_w3))]:
            for kc in range(8):
                prep_one(df[ei], ds_[ei], kc * 128, 128, D)
            ptok[(nm, ei)] = prep_done_tok()
    for nm, (df, ds_) in [("sw1", (sw1_d, s_sw1)), ("sw2", (sw2_d, s_sw2)),
                          ("win", (win_d, s_win)), ("sw3", (sw3_d, s_sw3)),
                          ("wout", (wout_d, s_wout)), ("wx", (wx_d, s_wx)),
                          ("wdt", (wdt_d, s_wdt))]:
        R = df.shape[0]
        r0 = 0
        while r0 < R:
            rows = min(128, R - r0)
            prep_one(df, ds_, r0, rows, df.shape[1])
            r0 += rows
        ptok[nm] = prep_done_tok()
    prep_toks = prep_done_tok()

    t_wx = S.dma("sync", wx_sb[:, :].rearrange("p (kc m1) -> p kc m1", kc=16),
                 s_wx.rearrange("(kc p) m -> p kc m", p=128), waits=ptok["wx"],
                 sem="qC2")
    S.dma("sync", wdt_sb[:, :], s_wdt, waits=ptok["wdt"], sem="qC2")
    t_wx = ("qC2", S.cnt["qC2"])
    t_wdt = t_wx

    def ln_full(src_view, src_toks, want_lo):
        ws = [t for t in src_toks if t is not None]
        w6 = list(ws)
        if bank_tok[6] is not None:
            w6.append(bank_tok[6])
        tok_sum = None
        for pc in range(PC_D):
            xb = src_view[:, pc * NT:(pc + 1) * NT]
            def fn(t, xb=xb, pc=pc):
                return t.matmul(out=PS[6][0:1, :], lhsT=onesc_sb[:, :], rhs=xb,
                                start=(pc == 0), stop=(pc == PC_D - 1))
            if pc == PC_D - 1:
                tok_sum = S.op("tensor", fn, waits=w6 if pc == 0 else (), inc="sP")
            else:
                S.op("tensor", fn, waits=w6 if pc == 0 else ())
        w7 = list(ws)
        if bank_tok[7] is not None:
            w7.append(bank_tok[7])
        sq_free = [None, None]
        tok_sq = None
        for pc in range(PC_D):
            xb = src_view[:, pc * NT:(pc + 1) * NT]
            wq = list(ws)
            if sq_free[pc % 2] is not None:
                wq.append(sq_free[pc % 2])
            t_s = act(xsq[pc % 2][:, :], xb, AF.Square, waits=wq)
            def fn(t, pc=pc):
                return t.matmul(out=PS[7][0:1, :], lhsT=onesc_sb[:, :],
                                rhs=xsq[pc % 2][:, :],
                                start=(pc == 0), stop=(pc == PC_D - 1))
            if pc == PC_D - 1:
                tok_sq = S.op("tensor", fn, waits=(w7 + [t_s]) if pc == 0 else [t_s],
                              inc="sP")
            else:
                tok_sq = S.op("tensor", fn, waits=(w7 + [t_s]) if pc == 0 else [t_s],
                              inc="sP")
            sq_free[pc % 2] = tok_sq
        t1 = vop(lambda v: v.tensor_scalar(out=mean_r[:, :], in0=PS[6][0:1, :],
                                           scalar1=1.0 / D, scalar2=None,
                                           op0=ALU.mult), waits=[tok_sum])
        t2 = vop(lambda v: v.tensor_scalar(out=srow_r[:, :], in0=PS[7][0:1, :],
                                           scalar1=1.0 / D, scalar2=None,
                                           op0=ALU.mult), waits=[tok_sq])
        t3 = vop(lambda v: v.tensor_mul(out=rstd_r[:, :], in0=mean_r[:, :],
                                        in1=mean_r[:, :]), waits=[t1])
        t4 = vop(lambda v: v.tensor_sub(out=srow_r[:, :], in0=srow_r[:, :],
                                        in1=rstd_r[:, :]), waits=[t2, t3])
        t5 = act(rstd_r[:, :], srow_r[:, :], AF.Ln, bias=float(EPS), waits=[t4])
        t6 = act(rstd_r[:, :], rstd_r[:, :], AF.Exp, scale=-0.5, waits=[t5])
        tok_mb = mm_group(PS[6][:, :], 6, [(onesr_sb[:, :], mean_r[:, :])],
                          waits=[t1])
        tok_rb = mm_group(PS[7][:, :], 7, [(onesr_sb[:, :], rstd_r[:, :])],
                          waits=[t6])
        toks_bf = []
        tok_lo = None
        for pc in range(PC_D):
            xb = src_view[:, pc * NT:(pc + 1) * NT]
            ta = vop(lambda v, xb=xb: v.tensor_sub(out=xf_tmp[:, :], in0=xb,
                                                   in1=PS[6][:, :]),
                     waits=[tok_mb])
            tb = vop(lambda v: v.tensor_mul(out=xf_tmp[:, :], in0=xf_tmp[:, :],
                                            in1=PS[7][:, :]), waits=[tok_rb, ta])
            tc = vop(lambda v, pc=pc: v.tensor_copy(
                out=xh_bf[:, pc * NT:(pc + 1) * NT], in_=xf_tmp[:, :]), waits=[tb])
            toks_bf.append(tc)
            if want_lo:
                tok_lo = vop(lambda v, pc=pc: v.tensor_sub(
                    out=xh_lo[:, pc * NT:(pc + 1) * NT], in0=xf_tmp[:, :],
                    in1=xh_bf[:, pc * NT:(pc + 1) * NT]), waits=[tc])
        bank_tok[6] = tok_lo if want_lo else toks_bf[-1]
        bank_tok[7] = bank_tok[6]
        return toks_bf, tok_lo

    def routing(bf_toks, tok_lo):
        tok_wT = None
        for tt in range(NT // 128):
            w = [bf_toks[-1], tok_lo, t_ghi, t_glo]
            if bank_tok[4] is not None:
                w.append(bank_tok[4])
            combos = []
            for pc in range(PC_D):
                lh_hi = xh_bf[:, pc * NT + tt * 128: pc * NT + (tt + 1) * 128]
                lh_lo = xh_lo[:, pc * NT + tt * 128: pc * NT + (tt + 1) * 128]
                combos.append((lh_hi, ghi_sb[:, pc * 8:(pc + 1) * 8]))
                combos.append((lh_hi, glo_sb[:, pc * 8:(pc + 1) * 8]))
                combos.append((lh_lo, ghi_sb[:, pc * 8:(pc + 1) * 8]))
            tokg = None
            for i, (lh, rh) in enumerate(combos):
                def fn(t, lh=lh, rh=rh, i=i, n=len(combos)):
                    return t.matmul(out=PS[4][:, 0:8], lhsT=lh, rhs=rh,
                                    start=(i == 0), stop=(i == n - 1))
                if i == len(combos) - 1:
                    tokg = S.op("tensor", fn, waits=w if i == 0 else (), inc="sP")
                else:
                    S.op("tensor", fn, waits=w if i == 0 else ())
            ta = vop(lambda v: v.reduce_max(out=nmax_sb[:, :], in_=PS[4][:, 0:8],
                                            axis=AX, negate=True),
                     waits=[tokg], selfsync=True)
            tb = act(scores_sb[:, :], PS[4][:, 0:8], AF.Exp, bias=nmax_sb[:, 0:1],
                     waits=[ta])
            bank_tok[4] = tb
            tc = vop(lambda v: v.reduce_sum(out=ssum_sb[:, :], in_=scores_sb[:, :],
                                            axis=AX), waits=[tb], selfsync=True)
            td = vop(lambda v: v.reciprocal(out=ssum_sb[:, :], in_=ssum_sb[:, :]),
                     waits=[tc], selfsync=True)
            te = vop(lambda v: v.tensor_scalar(out=scores_sb[:, :],
                                               in0=scores_sb[:, :],
                                               scalar1=ssum_sb[:, 0:1],
                                               scalar2=None, op0=ALU.mult),
                     waits=[td], selfsync=True)
            tf = vop(lambda v: v.max(out=m8_sb[:, :], in_=scores_sb[:, :]),
                     waits=[te], selfsync=True)
            tg = vop(lambda v: v.memset(m2_sb[:, :], -1.0), selfsync=True)
            th = vop(lambda v: v.tensor_copy(out=m2_sb[:, 0:2], in_=m8_sb[:, 0:2]),
                     waits=[tf, tg], selfsync=True)
            ti = vop(lambda v: v.match_replace(out=zap_sb[:, :],
                                               in_to_replace=m2_sb[:, :],
                                               in_values=scores_sb[:, :],
                                               imm_value=0.0),
                     waits=[th], selfsync=True)
            tj = vop(lambda v: v.tensor_sub(out=wsel_sb[:, :], in0=scores_sb[:, :],
                                            in1=zap_sb[:, :]),
                     waits=[ti], selfsync=True)
            w5 = [tj]
            if bank_tok[5] is not None:
                w5.append(bank_tok[5])
            def ftr(t):
                return t.transpose(out=PS[5][0:8, 0:128], in_=wsel_sb[:, :],
                                   identity=ident_sb[:, :])
            tk = S.op("tensor", ftr, waits=w5, inc="sP")
            tok_wT = vop(lambda v, tt=tt: v.tensor_copy(
                out=wT_sb[:, tt * 128:(tt + 1) * 128], in_=PS[5][0:8, 0:128]),
                waits=[tk])
            bank_tok[5] = tok_wT
        return tok_wT

    def ffn_prefetch():
        mats = []
        for ei in range(NE):
            mats += [("w1", ei), ("w2", ei), ("w3", ei)]
        dram_of = {"w1": s_w1, "w2": s_w2, "w3": s_w3}
        mat_tok = {}
        next_load = [0]

        def pump(n):
            while next_load[0] < len(mats) and next_load[0] < n:
                mi = next_load[0]
                kind, ei = mats[mi]
                sl = mi % 4
                mat_tok[(kind, ei)] = (sl, load_slot(
                    sl, expert_mat(dram_of[kind], ei), waits=ptok[(kind, ei)]))
                next_load[0] += 1
        return mat_tok, pump

    def ffn(bf_toks, tok_wT, mat_tok, pump):
        y_toks = [None] * PC_D
        prev_xs_last = None
        prev_x2_toks = []
        tok_mask = None
        for ei in range(NE):
            w6 = [tok_wT]
            if bank_tok[6] is not None:
                w6.append(bank_tok[6])
            def fbc(t, ei=ei):
                return t.matmul(out=PS[6][:, :],
                                lhsT=sel8_sb[:, ei * 128:(ei + 1) * 128],
                                rhs=wT_sb[:, :], start=True, stop=True)
            tbc = S.op("tensor", fbc, waits=w6, inc="sP")
            tw = act(wbc_bf[:, :], PS[6][:, :], AF.Identity,
                     waits=[tbc, prev_xs_last])
            bank_tok[6] = tw
            tok_mask = act(mask_f[:, :], wbc_bf[:, :], AF.Sign,
                           waits=[tw] + prev_x2_toks)
            xs_toks = []
            for pc in range(PC_D):
                xs_toks.append(vop(lambda v, pc=pc: v.tensor_mul(
                    out=xs_bf[:, pc * NT:(pc + 1) * NT],
                    in0=xh_bf[:, pc * NT:(pc + 1) * NT], in1=wbc_bf[:, :]),
                    waits=[tw, bf_toks[pc]]))
            prev_xs_last = xs_toks[-1]
            sl1, tok_w1 = mat_tok[("w1", ei)]
            h_toks = []
            for m in range(PC_D):
                bank = m % 2
                pieces = [(aw_slot(sl1).rearrange("p (kc m1) -> p kc m1", kc=8)
                           [:, kc, m * 128:(m + 1) * 128],
                           xs_bf[:, kc * NT:(kc + 1) * NT]) for kc in range(8)]
                tokm = mm_group(PS[bank][:, :], bank, pieces,
                                waits=[tok_w1] + xs_toks)
                t_h = act(h_bf[:, m * NT:(m + 1) * NT], PS[bank][:, :], AF.Silu,
                          bias=b1_sb[:, m * 8 + ei: m * 8 + ei + 1], waits=[tokm])
                bank_tok[bank] = t_h
                h_toks.append(t_h)
            sl2, tok_w2 = mat_tok[("w2", ei)]
            sl3, tok_w3 = mat_tok[("w3", ei)]
            prev_x2_toks = []
            for m in range(PC_D):
                b2k = 2 + (m % 2)
                pieces2 = [(aw_slot(sl2).rearrange("p (kc m1) -> p kc m1", kc=8)
                            [:, kc, m * 128:(m + 1) * 128],
                            h_bf[:, kc * NT:(kc + 1) * NT]) for kc in range(8)]
                tok2 = mm_group(PS[b2k][:, :], b2k, pieces2,
                                waits=[tok_w2] + h_toks)
                t_x2 = vop(lambda v, b2k=b2k, m=m, ei=ei: v.scalar_tensor_tensor(
                    out=x2m[:, :], in0=PS[b2k][:, :],
                    scalar=b2_sb[:, m * 8 + ei: m * 8 + ei + 1],
                    in1=mask_f[:, :], op0=ALU.add, op1=ALU.mult),
                    waits=[tok2, tok_mask])
                bank_tok[b2k] = t_x2
                prev_x2_toks.append(t_x2)
                b3k = 4 + (m % 2)
                pieces3 = [(aw_slot(sl3).rearrange("p (kc m1) -> p kc m1", kc=8)
                            [:, kc, m * 128:(m + 1) * 128],
                            xs_bf[:, kc * NT:(kc + 1) * NT]) for kc in range(8)]
                tok3 = mm_group(PS[b3k][:, :], b3k, pieces3,
                                waits=[tok_w3] + xs_toks)
                t_x3 = vop(lambda v, b3k=b3k, m=m, ei=ei: v.scalar_tensor_tensor(
                    out=tmp_f[:, :], in0=PS[b3k][:, :],
                    scalar=b3_sb[:, m * 8 + ei: m * 8 + ei + 1],
                    in1=x2m[:, :], op0=ALU.add, op1=ALU.mult),
                    waits=[tok3, t_x2] + ([y_toks[m - 1]] if m > 0 else
                                          ([y_toks[PC_D - 1]] if y_toks[PC_D - 1] else [])))
                bank_tok[b3k] = t_x3
                if ei == 0:
                    t_y = pop_(lambda g, m=m: g.tensor_copy(
                        out=y_acc[:, m * NT:(m + 1) * NT], in_=tmp_f[:, :]),
                        waits=[t_x3] + reg_free["y_acc"])
                else:
                    t_y = pop_(lambda g, m=m: g.tensor_add(
                        out=y_acc[:, m * NT:(m + 1) * NT],
                        in0=y_acc[:, m * NT:(m + 1) * NT], in1=tmp_f[:, :]),
                        waits=[t_x3, y_toks[m]])
                y_toks[m] = t_y
            done_tok = y_toks[PC_D - 1]
            for (sl, _) in (mat_tok[("w1", ei)], mat_tok[("w2", ei)],
                            mat_tok[("w3", ei)]):
                slot_free[sl] = done_tok
            pump((ei + 1) * 3 + 4)
        def sw_half(dram, half):
            return dram[half * 512:(half + 1) * 512].rearrange(
                "(kc p) m -> p kc m", p=128)
        t_sw1 = [load_slot(0, sw_half(s_sw1, 0), waits=ptok["sw1"]),
                 load_slot(1, sw_half(s_sw1, 1), waits=ptok["sw1"])]
        t_sw2 = [load_slot(2, sw_half(s_sw2, 0), waits=ptok["sw2"]),
                 load_slot(3, sw_half(s_sw2, 1), waits=ptok["sw2"])]
        p_toks = []
        prev_tp = None
        for m in range(PC_DI):
            bank = m % 2
            pieces = []
            for kc in range(8):
                sl = 0 if kc < 4 else 1
                pieces.append((aw_slot(sl).rearrange("p (kc m1) -> p kc m1", kc=4)
                               [:, kc % 4, m * 128:(m + 1) * 128],
                               xh_bf[:, kc * NT:(kc + 1) * NT]))
            tok1 = mm_group(PS[bank][:, :], bank, pieces,
                            waits=t_sw1 + [bf_toks[-1]])
            wg1 = [tok1]
            if prev_tp is not None:
                wg1.append(prev_tp)
            t_g1 = act(g1s_bf[:, :], PS[bank][:, :], AF.Silu, waits=wg1)
            bank_tok[bank] = t_g1
            bank2 = 2 + (m % 2)
            pieces = []
            for kc in range(8):
                sl = 2 if kc < 4 else 3
                pieces.append((aw_slot(sl).rearrange("p (kc m1) -> p kc m1", kc=4)
                               [:, kc % 4, m * 128:(m + 1) * 128],
                               xh_bf[:, kc * NT:(kc + 1) * NT]))
            tok2 = mm_group(PS[bank2][:, :], bank2, pieces,
                            waits=t_sw2 + [bf_toks[-1]])
            t_p = vop(lambda v, m=m, bank2=bank2: v.tensor_mul(
                out=p_bf[:, m * NT:(m + 1) * NT], in0=g1s_bf[:, :],
                in1=PS[bank2][:, :]),
                waits=[tok2, t_g1] + reg_free["p_bf"])
            bank_tok[bank2] = t_p
            prev_tp = t_p
            p_toks.append(t_p)
        slot_free[0] = p_toks[-1]
        slot_free[1] = p_toks[-1]
        t_sw3 = [load_slot(0, s_sw3[0:1024].rearrange("(kc p) m -> p kc m", p=128),
                           waits=ptok["sw3"]),
                 load_slot(1, s_sw3[1024:2048].rearrange("(kc p) m -> p kc m", p=128),
                           waits=ptok["sw3"])]
        for m in range(PC_D):
            bank = 4 + (m % 2)
            pieces = []
            for kc in range(16):
                sl = 0 if kc < 8 else 1
                pieces.append((aw_slot(sl).rearrange("p (kc m1) -> p kc m1", kc=8)
                               [:, kc % 8, m * 128:(m + 1) * 128],
                               p_bf[:, kc * NT:(kc + 1) * NT]))
            tokz = mm_group(PS[bank][:, :], bank, pieces, waits=t_sw3 + p_toks)
            t_y = vop(lambda v, m=m, bank=bank: v.tensor_add(
                out=y_acc[:, m * NT:(m + 1) * NT],
                in0=y_acc[:, m * NT:(m + 1) * NT], in1=PS[bank][:, :]),
                waits=[tokz, y_toks[m]])
            bank_tok[bank] = t_y
            y_toks[m] = t_y
        slot_free[0] = y_toks[-1]
        slot_free[1] = y_toks[-1]
        slot_free[2] = p_toks[-1]
        slot_free[3] = p_toks[-1]
        reg_free["p_bf"] = []
        reg_free["y_acc"] = []
        return y_toks

    reg_free = {"y_acc": [], "p_bf": []}
    halo_toks = [None] * PC_DI
    btile_free = [None]
    xin_free = [[]]
    c_free = [[]]
    dbg_braw = None
    if debug:
        dbg_braw = [scr("s_dbgB", [NB, 16, T], F32), scr("s_dbgC", [NB, 16, T], F32)]

    def mamba_pre(b, c, x1_toks):
        bf_toks, _ = ln_full(x_in[:, :], x1_toks, want_lo=False)
        ln2_bf = bf_toks
        t_win = [load_slot(sl, win_view(sl), waits=ptok["win"]) for sl in range(4)]
        xi_toks = [None] * PC_DI
        zt_free = [None, None]
        sl_of = [0, 0, 1, 1, 2, 2, 3, 3]
        for m in range(32):
            bank = m % 2
            pieces = []
            for kc in range(8):
                sl = sl_of[kc]
                pieces.append((aw_slot(sl).rearrange("p (kc m1) -> p kc m1", kc=2)
                               [:, kc % 2, m * 128:(m + 1) * 128],
                               xh_bf[:, kc * NT:(kc + 1) * NT]))
            tokm = mm_group(PS[bank][:, :], bank, pieces,
                            waits=t_win + [bf_toks[-1]])
            if m < PC_DI:
                t_xi = act(xi_view(m), PS[bank][:, :], AF.Identity,
                           bias=bin_sb[:, m:m + 1], waits=[tokm])
                bank_tok[bank] = t_xi
                xi_toks[m] = t_xi
            else:
                mz = m - PC_DI
                i = mz % 2
                w = [tokm]
                if zt_free[i] is not None:
                    w.append(zt_free[i])
                t_z = act(zt_sb[i][:, :], PS[bank][:, :], AF.Silu,
                          bias=bin_sb[:, m:m + 1], waits=w)
                bank_tok[bank] = t_z
                zt_free[i] = S.dma("sync", s_sz[b, mz * 128:(mz + 1) * 128,
                                               c * NT:(c + 1) * NT],
                                   zt_sb[i][:, :], waits=[t_z], sem=f"q_zt{i}")
        conv_bf_toks = []
        xcv_free = [None, None]
        sp_free = [None, None]
        for mi in range(PC_DI):
            even = (mi % 2 == 0)
            opf = vop
            cb_i = cvb[0] if even else cvb[1]
            sp_i = sp_a if even else sp_b
            t0 = opf(lambda v, mi=mi, cb_i=cb_i: v.tensor_copy(
                out=cb_i[:, 3:3 + NT], in_=xi_view(mi)), waits=[xi_toks[mi]])
            if c == 0:
                t1_ = opf(lambda v, cb_i=cb_i: v.memset(cb_i[:, 0:3], 0.0),
                          waits=[t0])
            else:
                t1_ = opf(lambda v, mi=mi, cb_i=cb_i: v.tensor_copy(
                    out=cb_i[:, 0:3], in_=halo_sb[:, mi * 3:(mi + 1) * 3]),
                    waits=[t0, halo_toks[mi]])
            w2_ = [t1_]
            if sp_free[mi % 2] is not None:
                w2_.append(sp_free[mi % 2])
            t2_ = opf(lambda v, mi=mi, cb_i=cb_i, sp_i=sp_i: v.tensor_scalar(
                out=sp_i[:, :], in0=cb_i[:, 0:NT],
                scalar1=cw_sb[:, mi * 4:mi * 4 + 1], scalar2=None, op0=ALU.mult),
                waits=w2_)
            for j in range(1, 4):
                t2_ = opf(lambda v, mi=mi, j=j, cb_i=cb_i, sp_i=sp_i:
                          v.scalar_tensor_tensor(
                              out=sp_i[:, :], in0=cb_i[:, j:j + NT],
                              scalar=cw_sb[:, mi * 4 + j:mi * 4 + j + 1],
                              in1=sp_i[:, :], op0=ALU.mult, op1=ALU.add),
                          waits=[t2_])
            halo_toks[mi] = opf(lambda v, mi=mi, cb_i=cb_i: v.tensor_copy(
                out=halo_sb[:, mi * 3:(mi + 1) * 3], in_=cb_i[:, NT:NT + 3]),
                waits=[t1_])
            i = mi % 2
            w = [t2_]
            if xcv_free[i] is not None:
                w.extend(xcv_free[i])
            t_cv = act(xcv_sb[i][:, :], sp_i[:, :], AF.Silu,
                       bias=cb_sb[:, mi:mi + 1], waits=w)
            sp_free[mi % 2] = t_cv
            tok_st = S.dma("sync", s_xc[b, mi * 128:(mi + 1) * 128,
                                        c * NT:(c + 1) * NT],
                           xcv_sb[i][:, :], waits=[t_cv], sem=f"q_xcv{i}")
            t_bf = vop(lambda v, mi=mi, i=i: v.tensor_copy(
                out=p_bf[:, mi * NT:(mi + 1) * NT], in_=xcv_sb[i][:, :]),
                waits=[t_cv])
            conv_bf_toks.append(t_bf)
            xcv_free[i] = [tok_st, t_bf]
        grp_defs = [(0, 64, 4, bx_sb[0:64, 0:1], dtlow_bf[:, :]),
                    (64, 80, 5, brow_sb[:, :], None),
                    (80, 96, 6, crow_sb[:, :], None)]
        row_toks = {}
        for (c0, c1, bidx, arg1, arg2) in grp_defs:
            w = conv_bf_toks + [t_wx]
            if bank_tok[bidx] is not None:
                w.append(bank_tok[bidx])
            tokd = None
            for kc in range(PC_DI):
                def fn(t, kc=kc, c0=c0, c1=c1, bidx=bidx):
                    return t.matmul(out=PS[bidx][0:c1 - c0, :],
                                    lhsT=wx_sb[:, :].rearrange(
                                        "p (kc m1) -> p kc m1", kc=16)[:, kc, c0:c1],
                                    rhs=p_bf[:, kc * NT:(kc + 1) * NT],
                                    start=(kc == 0), stop=(kc == PC_DI - 1))
                if kc == PC_DI - 1:
                    tokd = S.op("tensor", fn, waits=w if kc == 0 else (), inc="sP")
                else:
                    S.op("tensor", fn, waits=w if kc == 0 else ())
            if c0 == 0:
                t_e = act(dtlow_bf[:, :], PS[bidx][0:64, :], AF.Identity,
                          bias=bx_sb[0:64, 0:1], waits=[tokd])
            elif c0 == 64:
                t_e = act(brow_sb[:, :], PS[bidx][0:16, :], AF.Identity,
                          bias=bx_sb[64:80, 0:1], waits=[tokd])
            else:
                t_e = act(crow_sb[:, :], PS[bidx][0:16, :], AF.Identity,
                          bias=bxc_sb[0:16, 0:1], waits=[tokd])
            bank_tok[bidx] = t_e
            row_toks[c0] = t_e
        for (rows_sb, dst, base) in ((brow_sb, s_bi, 64), (crow_sb, s_ci, 80)):
            if dbg_braw is not None:
                S.dma("sync", dbg_braw[0 if base == 64 else 1]
                      [b, :, c * NT:(c + 1) * NT], rows_sb[:, :],
                      waits=[row_toks[base]])
            for tt in range(NT // 128):
                w5 = [row_toks[base]]
                if bank_tok[3] is not None:
                    w5.append(bank_tok[3])
                def ftr(t, rows_sb=rows_sb, tt=tt):
                    return t.transpose(out=PS[3][0:128, 0:16],
                                       in_=rows_sb[0:16, tt * 128:(tt + 1) * 128],
                                       identity=ident_sb[0:16, 0:16])
                tk = S.op("tensor", ftr, waits=w5, inc="sP")
                w_bt = [tk]
                if btile_free[0] is not None:
                    w_bt.append(btile_free[0])
                t_bt = vop(lambda v: v.tensor_copy(out=btile_sb[:, :],
                                                   in_=PS[3][0:128, 0:16]),
                           waits=w_bt)
                bank_tok[3] = t_bt
                off = (c * NT + tt * 128) * DS
                btile_free[0] = S.dma("sync", dst[b, off:off + 128 * DS].rearrange(
                    "(p s1) -> p s1", s1=DS), btile_sb[:, :], waits=[t_bt],
                    sem="q_bt")
        dtv_free = list(zt_free)
        prev_td = [None, None]
        for mi in range(PC_DI):
            bank = mi % 2
            w = [row_toks[0], t_wdt]
            if bank_tok[bank] is not None:
                w.append(bank_tok[bank])
            def fdt(t, mi=mi, bank=bank):
                return t.matmul(out=PS[bank][:, :],
                                lhsT=wdt_sb[:, mi * 128:(mi + 1) * 128],
                                rhs=dtlow_bf[:, :], start=True, stop=True)
            tokd = S.op("tensor", fdt, waits=w, inc="sP")
            wa = [tokd]
            if prev_td[0] is not None:
                wa.append(prev_td[0])
            t_a = act(sp_a[:, :], PS[bank][:, :], AF.Abs,
                      bias=bdt_sb[:, mi:mi + 1], waits=wa)
            wr = [t_a]
            if prev_td[1] is not None:
                wr.append(prev_td[1])
            t_r = act(sp_b[:, :], PS[bank][:, :], AF.Relu,
                      bias=bdt_sb[:, mi:mi + 1], waits=wr)
            bank_tok[bank] = t_r
            t_e1 = act(sp_a[:, :], sp_a[:, :], AF.Exp, scale=-1.0, waits=[t_r])
            t_e2 = act(sp_a[:, :], sp_a[:, :], AF.Ln, bias=1.0, waits=[t_e1])
            i = mi % 2
            w2_ = [t_e2]
            if dtv_free[i] is not None:
                w2_.append(dtv_free[i])
            t_d = vop(lambda v, i=i: v.tensor_add(out=dtv_sb[i][:, :],
                                                  in0=sp_a[:, :], in1=sp_b[:, :]),
                      waits=w2_)
            prev_td[0] = t_d
            prev_td[1] = t_d
            dtv_free[i] = S.dma("sync", s_dt[b, mi * 128:(mi + 1) * 128,
                                             c * NT:(c + 1) * NT],
                                dtv_sb[i][:, :], waits=[t_d], sem=f"q_zt{i}")
        return bf_toks


    def phase_a(b, c):
        S.barrier_all(exclude=("sync",))
        mat_tok, pump = ffn_prefetch()
        pump(4)
        wx_ = list(xin_free[0])
        xin_free[0] = []
        tok_x = S.dma("sync", x_in[:, :].rearrange("p (pc t1) -> p pc t1", pc=PC_D),
                      x_in_d[b].rearrange("(pc p) t1 -> p pc t1", p=128)
                      [:, :, c * NT:(c + 1) * NT], sem="q_x", waits=wx_)
        reg_free["y_acc"] = []
        reg_free["p_bf"] = []
        bf_toks, tok_lo = ln_full(x_in[:, :], [tok_x], want_lo=True)
        tok_wT = routing(bf_toks, tok_lo)
        y_toks = ffn(bf_toks, tok_wT, mat_tok, pump)
        x1_toks = []
        for m in range(PC_D):
            t_x1 = vop(lambda v, m=m: v.scalar_tensor_tensor(
                out=x_in[:, m * NT:(m + 1) * NT],
                in0=y_acc[:, m * NT:(m + 1) * NT], scalar=0.5,
                in1=x_in[:, m * NT:(m + 1) * NT],
                op0=ALU.mult, op1=ALU.add), waits=[y_toks[m]])
            x1_toks.append(t_x1)
        t_x1s = S.dma("sync", s_x1[b][:, c * NT:(c + 1) * NT].rearrange(
            "(pc p) t1 -> p pc t1", p=128),
            x_in[:, :].rearrange("p (pc t1) -> p pc t1", pc=PC_D),
            waits=x1_toks, sem="q_x1s")
        ln2_toks = mamba_pre(b, c, x1_toks)
        xin_free[0] = [t_x1s, ln2_toks[-1]]

    def phase_b(b):
        irow_free = [None, None]
        slice_free = [None, None]
        yg_free = [None, None]
        vh_free = [None, None]
        bint_free = [None]
        prev_tyg = [None]
        st_toks = [None] * PC_DI
        for tq in range(NTQ):
            bc_toks = []
            for (src, dst_int, bank) in ((s_bi, Bint, 6), (s_ci, Cint, 7)):
                for piece in range(4):
                    i = piece % 2
                    off = (tq * TQ + piece * 128) * DS
                    w = []
                    if irow_free[i] is not None:
                        w.append(irow_free[i])
                    t_r = S.dma("sync", irow_sb[i][:, :], src[b, off:off + 2048],
                                waits=w, sem=f"q_ir{i}")
                    last_ev = None
                    for qn in range(4):
                        wq = [t_r]
                        if bank_tok[bank] is not None:
                            wq.append(bank_tok[bank])
                        if bint_free[0] is not None:
                            wq.append(bint_free[0])
                        def fb(t, i=i, qn=qn, bank=bank):
                            return t.matmul(out=PS[bank][:, :], lhsT=onesr_sb[:, :],
                                            rhs=irow_sb[i][0:1, qn * 512:(qn + 1) * 512],
                                            start=True, stop=True)
                        tkb = S.op("tensor", fb, waits=wq, inc="sP")
                        t_ev = act(dst_int[:, piece * 2048 + qn * 512:
                                           piece * 2048 + (qn + 1) * 512],
                                   PS[bank][:, :], AF.Identity, waits=[tkb])
                        bank_tok[bank] = t_ev
                        last_ev = t_ev
                        bc_toks.append(t_ev)
                    irow_free[i] = last_ev
            pre = {}

            def emit_pre(dc):
                i = dc % 2
                vh = vh_bufs[dc % 2]
                w = []
                if slice_free[i] is not None:
                    w.append(slice_free[i])
                t_dt = S.dma("sync", dts_sb[i][:, :],
                             s_dt[b, dc * 128:(dc + 1) * 128, tq * TQ:(tq + 1) * TQ],
                             waits=w, sem=f"q_dts{i}")
                t_xc = S.dma("sync", xcs_sb[i][:, :],
                             s_xc[b, dc * 128:(dc + 1) * 128, tq * TQ:(tq + 1) * TQ],
                             sem=f"q_xcs{i}")
                t_sz = S.dma("sync", szs_sb[i][:, :],
                             s_sz[b, dc * 128:(dc + 1) * 128, tq * TQ:(tq + 1) * TQ],
                             sem=f"q_szs{i}")
                t_u = pop_(lambda g, i=i: g.tensor_mul(out=u_sb[:, :],
                                                       in0=dts_sb[i][:, :],
                                                       in1=xcs_sb[i][:, :]),
                           waits=[t_dt, t_xc])
                wv = [t_u] + bc_toks
                if vh_free[dc % 2] is not None:
                    wv.append(vh_free[dc % 2])
                t_v = pop_(lambda g, vh=vh: g.tensor_tensor(
                    out=vh[:, :].rearrange("p (t1 s1) -> p t1 s1", s1=DS),
                    in0=u_sb[:, :].unsqueeze(2).to_broadcast([128, TQ, DS]),
                    in1=Bint[:, :].rearrange("p (t1 s1) -> p t1 s1", s1=DS),
                    op=ALU.mult), waits=wv)
                pre[dc] = (t_dt, t_xc, t_sz, t_v)

            emit_pre(0)
            for dc in range(PC_DI):
                i = dc % 2
                vh = vh_bufs[dc % 2]
                t_dt, t_xc, t_sz, t_v = pre.pop(dc)
                dA_free = [None, None]
                t_scan_last = None
                for s_i in range(DS):
                    ia = s_i % 2
                    wA = [t_dt, t_an]
                    if dA_free[ia] is not None:
                        wA.append(dA_free[ia])
                    t_dA = act(dA_sb[ia][:, :], dts_sb[i][:, :], AF.Exp,
                               scale=aneg_sb[:, dc * DS + s_i: dc * DS + s_i + 1],
                               waits=wA)
                    init = 0.0 if tq == 0 else state_sb[:, dc * DS + s_i:
                                                        dc * DS + s_i + 1]
                    def fsc(v, s_i=s_i, ia=ia, init=init, vh=vh):
                        vv = vh[:, :].rearrange("p (t1 s1) -> p t1 s1", s1=DS)
                        return v.tensor_tensor_scan(
                            out=vv[:, :, s_i], data0=dA_sb[ia][:, :],
                            data1=vv[:, :, s_i], initial=init,
                            op0=ALU.mult, op1=ALU.add)
                    t_sc = vop(fsc, waits=[t_dA, t_v] +
                               ([st_toks[dc]] if tq > 0 else []))
                    dA_free[ia] = t_sc
                    t_scan_last = t_sc
                    if s_i == 3 and dc + 1 < PC_DI:
                        emit_pre(dc + 1)
                t_st = vop(lambda v, dc=dc, vh=vh: v.tensor_copy(
                    out=state_sb[:, dc * DS:(dc + 1) * DS],
                    in_=vh[:, (TQ - 1) * DS:TQ * DS]), waits=[t_scan_last],
                    selfsync=True)
                st_toks[dc] = t_st
                if dc % 3 == 0:
                    t_hc = vop(lambda v, vh=vh: v.tensor_mul(out=vh[:, :],
                                                             in0=vh[:, :],
                                                             in1=Cint[:, :]),
                               waits=[t_st])
                else:
                    t_hc = pop_(lambda g, vh=vh: g.tensor_mul(out=vh[:, :],
                                                              in0=vh[:, :],
                                                              in1=Cint[:, :]),
                                waits=[t_st])
                wy = [t_hc]
                if prev_tyg[0] is not None:
                    wy.append(prev_tyg[0])
                t_y = vop(lambda v, vh=vh: v.tensor_reduce(
                    out=yt_sb[:, :],
                    in_=vh[:, :].rearrange("p (t1 s1) -> p t1 s1", s1=DS),
                    axis=AX, op=ALU.add), waits=wy, selfsync=True)
                vh_free[dc % 2] = t_y
                bint_free[0] = t_y
                t_yd = vop(lambda v, i=i, dc=dc: v.scalar_tensor_tensor(
                    out=yt_sb[:, :], in0=xcs_sb[i][:, :],
                    scalar=dpar_sb[:, dc:dc + 1], in1=yt_sb[:, :],
                    op0=ALU.mult, op1=ALU.add), waits=[t_y])
                iy = dc % 2
                wyg = [t_yd, t_sz]
                if yg_free[iy] is not None:
                    wyg.append(yg_free[iy])
                t_yg = pop_(lambda g, i=i, iy=iy: g.tensor_mul(
                    out=ygb_sb[iy][:, :], in0=yt_sb[:, :], in1=szs_sb[i][:, :]),
                    waits=wyg)
                prev_tyg[0] = t_yg
                yg_free[iy] = S.dma("gpsimd", s_yg[b, dc * 128:(dc + 1) * 128,
                                                  tq * TQ:(tq + 1) * TQ],
                                    ygb_sb[iy][:, :], waits=[t_yg],
                                    sem=f"q_yg{iy}")
                slice_free[i] = t_yg

    def phase_c(b, c):
        S.barrier_all(exclude=("sync",))
        yg_v = av(0, 16 * K, BF16)
        x2_v = av(0, 16 * K, F32)
        mat_tok_c, pump_c = ffn_prefetch()
        wc_ = list(c_free[0])
        c_free[0] = []
        tok_yg = S.dma("sync", yg_v[:, :].rearrange("p (kc t1) -> p kc t1", kc=PC_DI),
                       s_yg[b].rearrange("(kc p) t1 -> p kc t1", p=128)
                       [:, :, c * NT:(c + 1) * NT], sem="q_ygl", waits=wc_)
        tok_x1 = S.dma("sync", x1c_v[:, :].rearrange("p (pc t1) -> p pc t1", pc=PC_D),
                       s_x1[b].rearrange("(pc p) t1 -> p pc t1", p=128)
                       [:, :, c * NT:(c + 1) * NT], sem="q_x1l")
        t_wo = [load_slot(0, s_wout[0:1024].rearrange("(kc p) m -> p kc m", p=128),
                          waits=ptok["wout"]),
                load_slot(1, s_wout[1024:2048].rearrange("(kc p) m -> p kc m", p=128),
                          waits=ptok["wout"])]
        mm_toks = []
        for m in range(PC_D):
            pieces = []
            for kc in range(PC_DI):
                sl = 0 if kc < 8 else 1
                pieces.append((aw_slot(sl).rearrange("p (kc m1) -> p kc m1", kc=8)
                               [:, kc % 8, m * 128:(m + 1) * 128],
                               yg_v[:, kc * NT:(kc + 1) * NT]))
            mm_toks.append(mm_group(PS[m][:, :], m, pieces,
                                    waits=t_wo + [tok_yg]))
        x2_toks = []
        for m in range(PC_D):
            t_b = vop(lambda v, m=m: v.scalar_tensor_tensor(
                out=x2_v[:, m * NT:(m + 1) * NT], in0=PS[m][:, :],
                scalar=bout_sb[:, m:m + 1], in1=x1c_v[:, m * NT:(m + 1) * NT],
                op0=ALU.add, op1=ALU.add),
                waits=[mm_toks[m], tok_x1, mm_toks[PC_D - 1]])
            bank_tok[m] = t_b
            x2_toks.append(t_b)
        slot_free[0] = x2_toks[-1]
        slot_free[1] = x2_toks[-1]
        pump_c(4)
        reg_free["y_acc"] = []
        reg_free["p_bf"] = []
        bf_toks, tok_lo = ln_full(x2_v[:, :], x2_toks, want_lo=True)
        tok_wT = routing(bf_toks, tok_lo)
        y_toks = ffn(bf_toks, tok_wT, mat_tok_c, pump_c)
        out_toks = []
        for m in range(PC_D):
            t_o = vop(lambda v, m=m: v.scalar_tensor_tensor(
                out=x2_v[:, m * NT:(m + 1) * NT],
                in0=y_acc[:, m * NT:(m + 1) * NT], scalar=0.5,
                in1=x2_v[:, m * NT:(m + 1) * NT],
                op0=ALU.mult, op1=ALU.add), waits=[y_toks[m]])
            out_toks.append(t_o)
        t_os = S.dma("sync", out_d[b].rearrange("(pc p) t1 -> p pc t1", p=128)
              [:, :, c * NT:(c + 1) * NT],
              x2_v[:, :].rearrange("p (pc t1) -> p pc t1", pc=PC_D),
              waits=out_toks, sem="qOut")
        c_free[0] = [t_os, y_toks[-1]]

    xin_free[0] = [prep_toks[0], prep_toks[1]]
    for b in range(NB):
        for c in range(NCH):
            phase_a(b, c)
    for b in range(NB):
        S.barrier_all()
        phase_b(b)
    S.barrier_all()
    for b in range(NB):
        for c in range(NCH):
            phase_c(b, c)
    S.barrier_all()

    S.emit(block)
    st.close()
    return nc


def _prep_inputs(inputs):
    f = np.float32

    def t2(a):
        return np.ascontiguousarray(np.asarray(a).T)

    ii = {}
    ii["wg"] = t2(inputs["L0_ffn_gate_w"])
    ii["win"] = t2(inputs["L0_attn_W_in"])
    ii["binp"] = np.ascontiguousarray(
        np.asarray(inputs["L0_attn_b_in"]).reshape(32, 128).T)
    ii["cw"] = np.ascontiguousarray(
        np.asarray(inputs["L0_attn_conv_w"]).reshape(2048, 4)
        .reshape(16, 128, 4).transpose(1, 0, 2).reshape(128, 64))
    ii["cb"] = np.ascontiguousarray(
        np.asarray(inputs["L0_attn_conv_b"]).reshape(16, 128).T)
    ii["wx"] = t2(inputs["L0_attn_W_x"])
    ii["bx"] = np.ascontiguousarray(np.asarray(inputs["L0_attn_b_x"]).reshape(96, 1))
    ii["bxc"] = np.ascontiguousarray(
        np.asarray(inputs["L0_attn_b_x"])[80:96].reshape(16, 1))
    ii["wdt"] = t2(inputs["L0_attn_W_dt"])
    ii["bdt"] = np.ascontiguousarray(
        np.asarray(inputs["L0_attn_b_dt"]).reshape(16, 128).T)
    ii["alog"] = np.ascontiguousarray(
        np.asarray(inputs["L0_attn_A_log"]).reshape(16, 128, 16)
        .transpose(1, 0, 2).reshape(128, 256))
    ii["dpar"] = np.ascontiguousarray(
        np.asarray(inputs["L0_attn_D_param"]).reshape(16, 128).T)
    ii["wout"] = t2(inputs["L0_attn_W_out"])
    ii["bout"] = np.ascontiguousarray(
        np.asarray(inputs["L0_attn_b_out"]).reshape(8, 128).T)
    for nm, w_, b_ in (("w1", "L0_ffn_w1", "L0_ffn_b1"),
                       ("w2", "L0_ffn_w2", "L0_ffn_b2"),
                       ("w3", "L0_ffn_w3", "L0_ffn_b3")):
        ii[nm] = np.ascontiguousarray(np.asarray(inputs[w_]).transpose(0, 2, 1))
        ii["b" + nm[1]] = np.ascontiguousarray(
            np.asarray(inputs[b_]).T.reshape(8, 128, 8).transpose(1, 0, 2)
            .reshape(128, 64))
    ii["sw1"] = t2(inputs["L0_ffn_sw1"])
    ii["sw2"] = t2(inputs["L0_ffn_sw2"])
    ii["sw3"] = t2(inputs["L0_ffn_sw3"])
    ii["ident"] = np.eye(128, dtype=f)
    ii["onesc"] = np.ones((128, 1), dtype=f)
    ii["onesr"] = np.ones((1, 128), dtype=f)
    ii["constv"] = np.tile(np.array([[1e-5, 1.0, -1.0, -0.5]], dtype=f), (128, 1))
    sel8 = np.zeros((8, 8 * 128), dtype=f)
    for e_ in range(8):
        sel8[e_, e_ * 128:(e_ + 1) * 128] = 1.0
    ii["sel8"] = sel8
    for k in ii:
        assert ii[k].dtype == np.float32, k
    return ii


def kernel(**inputs):
    x = np.asarray(inputs["L0_attn_x"])
    shared = _prep_inputs(inputs)
    nc = build_program()
    in_maps = []
    for core in range(NCORES):
        m = dict(shared)
        m["x"] = np.ascontiguousarray(
            x[core * NB:(core + 1) * NB].transpose(0, 2, 1))
        in_maps.append(m)
    res = run_bass_kernel_spmd(nc, in_maps, core_ids=list(range(NCORES)))
    outs = [np.asarray(res.results[core]["out"]).transpose(0, 2, 1)
            for core in range(NCORES)]
    return np.ascontiguousarray(np.concatenate(outs, axis=0))


if __name__ == "__main__":
    import reference as _r
    ins = {k: np.asarray(v) for k, v in _r.setup_inputs().items()}
    out = kernel(**ins)
    ref = np.asarray(_r.reference(**ins))
    rv = float(np.mean((out - ref) ** 2) / np.mean(ref ** 2))
    print("maxabs", np.abs(out - ref).max(), "relvar", rv)


# version 08
# speedup vs baseline: 1.0406x; 1.0060x over previous
import numpy as np
import concourse.bass as bass
import concourse.mybir as mybir
from concourse.bass_utils import run_bass_kernel_spmd
from contextlib import ExitStack

F32 = mybir.dt.float32
BF16 = mybir.dt.bfloat16
U8 = mybir.dt.uint8
AF = mybir.ActivationFunctionType
ALU = mybir.AluOpType
AX = mybir.AxisListType.X

NCORES = 8
NB = 2
T = 2048
D = 1024
DI = 2048
DS = 16
DTR = 64
NE = 8
ISH = 2048
NT = 512
NCH = T // NT
PC_D = D // 128
PC_DI = DI // 128
TQ = 512
NTQ = T // TQ
EPS = 1e-5

ENGS = ("sync", "tensor", "vector", "scalar", "gpsimd")
ENG_SEM = {"vector": "sV", "scalar": "sA", "gpsimd": "sO", "tensor": "sP"}
ALL_SEMS = tuple(["qOut", "qC", "qC2", "sP", "sV", "sA", "sO",
                  "qG0", "qG1", "qP0", "qP1", "q_x", "q_x1s", "q_bt",
                  "q_ygl", "q_x1l"] +
                 [f"q_sl{i}" for i in range(4)] +
                 [f"{p}{i}" for p in ("q_zt", "q_xcv", "q_ir", "q_dts",
                                      "q_xcs", "q_szs", "q_yg") for i in range(2)])


class Sched:

    def __init__(self):
        self.q = {e: [] for e in ENGS}
        self.cnt = {s: 0 for s in ALL_SEMS}
        self.lastwait = {e: {} for e in ENGS}
        self.sems = {}

    def _wait_list(self, eng, waits):
        out = []
        for w in waits:
            if w is None:
                continue
            s, c = w
            if c <= 0:
                continue
            assert c <= self.cnt[s], f"wait on future count {s}={c} > {self.cnt[s]}"
            if self.lastwait[eng].get(s, 0) < c:
                self.lastwait[eng][s] = c
                out.append((s, c))
        return out

    def op(self, eng, fn, waits=(), inc=None, incn=1, selfsync=False):
        if selfsync and inc is None:
            inc = ENG_SEM[eng]
        w = self._wait_list(eng, list(waits))
        tok = None
        if inc is not None:
            self.cnt[inc] += incn
            tok = (inc, self.cnt[inc])
        self.q[eng].append((w, fn, tok, incn if inc else 0, selfsync))
        if selfsync:
            self.lastwait[eng][tok[0]] = tok[1]
        return tok

    def dma(self, eng, out_ap, in_ap, waits=(), sem="qOut"):
        def fn(e, out_ap=out_ap, in_ap=in_ap):
            return e.dma_start(out=out_ap, in_=in_ap)
        return self.op(eng, fn, waits=waits, inc=sem, incn=16)

    def barrier_all(self, exclude=()):
        toks = [(s, self.cnt[s]) for s in ALL_SEMS]
        for e in ENGS:
            if e in exclude:
                continue
            w = self._wait_list(e, toks)
            if w:
                self.q[e].append((w, None, None, 0, False))

    def emit(self, block):
        def make_body(eng_name):
            def body(e):
                for (w, fn, tok, incn, selfsync) in self.q[eng_name]:
                    for s, c in w:
                        e.wait_ge(self.sems[s], c)
                    if fn is None:
                        continue
                    instr = fn(e)
                    if tok is not None:
                        instr.then_inc(self.sems[tok[0]], incn)
                    if selfsync:
                        e.wait_ge(self.sems[tok[0]], tok[1])
            return body
        block.sync(make_body("sync"))
        block.tensor(make_body("tensor"))
        block.vector(make_body("vector"))
        block.scalar(make_body("scalar"))
        block.gpsimd(make_body("gpsimd"))


def build_program(debug=False):
    nc = bass.Bass("TRN2", target_bir_lowering=False)
    S = Sched()

    def din(name, shape):
        return nc.dram_tensor(name, shape, F32, kind="ExternalInput").ap()

    x_in_d = din("x", [NB, D, T])
    wg_d = din("wg", [D, NE])
    win_d = din("win", [D, 2 * DI])
    bin_d = din("binp", [128, 32])
    cw_d = din("cw", [128, 64])
    cb_d = din("cb", [128, 16])
    wx_d = din("wx", [DI, 96])
    bx_d = din("bx", [96, 1])
    bxc_d = din("bxc", [16, 1])
    wdt_d = din("wdt", [DTR, DI])
    bdt_d = din("bdt", [128, 16])
    alog_d = din("alog", [128, 256])
    dpar_d = din("dpar", [128, 16])
    wout_d = din("wout", [DI, D])
    bout_d = din("bout", [128, 8])
    w1_d = din("w1", [NE, D, D])
    b1_d = din("b1", [128, 64])
    w2_d = din("w2", [NE, D, D])
    b2_d = din("b2", [128, 64])
    w3_d = din("w3", [NE, D, D])
    b3_d = din("b3", [128, 64])
    sw1_d = din("sw1", [D, ISH])
    sw2_d = din("sw2", [D, ISH])
    sw3_d = din("sw3", [ISH, D])
    ident_d = din("ident", [128, 128])
    onesc_d = din("onesc", [128, 1])
    onesr_d = din("onesr", [1, 128])
    sel8_d = din("sel8", [NE, NE * 128])
    constv_d = din("constv", [128, 4])

    out_d = nc.dram_tensor("out", [NB, D, T], F32, kind="ExternalOutput").ap()

    def scr(name, shape, dt):
        kind = "ExternalOutput" if (debug and name.startswith(("s_x", "s_dbg"))
                                    or debug and name in ("s_sz", "s_dt", "s_bi",
                                                          "s_ci", "s_yg")) else "Internal"
        return nc.dram_tensor(name, shape, dt, kind=kind).ap()

    s_win = scr("s_win", [D, 2 * DI], BF16)
    s_w1 = scr("s_w1", [NE, D, D], BF16)
    s_w2 = scr("s_w2", [NE, D, D], BF16)
    s_w3 = scr("s_w3", [NE, D, D], BF16)
    s_sw1 = scr("s_sw1", [D, ISH], BF16)
    s_sw2 = scr("s_sw2", [D, ISH], BF16)
    s_sw3 = scr("s_sw3", [ISH, D], BF16)
    s_wout = scr("s_wout", [DI, D], BF16)
    s_wx = scr("s_wx", [DI, 96], BF16)
    s_wdt = scr("s_wdt", [DTR, DI], BF16)
    s_x1 = scr("s_x1", [NB, D, T], F32)
    s_xc = scr("s_xc", [NB, DI, T], F32)
    s_sz = scr("s_sz", [NB, DI, T], F32)
    s_dt = scr("s_dt", [NB, DI, T], F32)
    s_bi = scr("s_bi", [NB, DS, T], F32)
    s_ci = scr("s_ci", [NB, DS, T], F32)
    s_yg = scr("s_yg", [NB, DI, T], BF16)

    st = ExitStack()
    e = st.enter_context

    ident_sb = e(nc.sbuf_tensor("ident_sb", [128, 128], F32))
    onesc_sb = e(nc.sbuf_tensor("onesc_sb", [128, 1], F32))
    onesr_sb = e(nc.sbuf_tensor("onesr_sb", [1, 128], F32))
    sel8_sb = e(nc.sbuf_tensor("sel8_sb", [NE, NE * 128], F32))
    bin_sb = e(nc.sbuf_tensor("bin_sb", [128, 32], F32))
    cw_sb = e(nc.sbuf_tensor("cw_sb", [128, 64], F32))
    cb_sb = e(nc.sbuf_tensor("cb_sb", [128, 16], F32))
    bx_sb = e(nc.sbuf_tensor("bx_sb", [96, 1], F32))
    bxc_sb = e(nc.sbuf_tensor("bxc_sb", [16, 1], F32))
    bdt_sb = e(nc.sbuf_tensor("bdt_sb", [128, 16], F32))
    alog_sb = e(nc.sbuf_tensor("alog_sb", [128, 256], F32))
    aneg_sb = e(nc.sbuf_tensor("aneg_sb", [128, 256], F32))
    dpar_sb = e(nc.sbuf_tensor("dpar_sb", [128, 16], F32))
    bout_sb = e(nc.sbuf_tensor("bout_sb", [128, 8], F32))
    b1_sb = e(nc.sbuf_tensor("b1_sb", [128, 64], F32))
    b2_sb = e(nc.sbuf_tensor("b2_sb", [128, 64], F32))
    b3_sb = e(nc.sbuf_tensor("b3_sb", [128, 64], F32))
    ggf_sb = e(nc.sbuf_tensor("ggf_sb", [128, 64], F32))
    ghi_sb = e(nc.sbuf_tensor("ghi_sb", [128, 64], BF16))
    glo_sb = e(nc.sbuf_tensor("glo_sb", [128, 64], BF16))
    wx_sb = e(nc.sbuf_tensor("wx_sb", [128, 16 * 96], BF16))
    wdt_sb = e(nc.sbuf_tensor("wdt_sb", [DTR, DI], BF16))
    halo_sb = e(nc.sbuf_tensor("halo_sb", [128, 48], F32))
    state_sb = e(nc.sbuf_tensor("state_sb", [128, 256], F32))
    mean_r = e(nc.sbuf_tensor("mean_r", [1, NT], F32))
    rstd_r = e(nc.sbuf_tensor("rstd_r", [1, NT], F32))
    srow_r = e(nc.sbuf_tensor("srow_r", [1, NT], F32))
    scores_sb = e(nc.sbuf_tensor("scores_sb", [128, 8], F32))
    m8_sb = e(nc.sbuf_tensor("m8_sb", [128, 8], F32))
    m2_sb = e(nc.sbuf_tensor("m2_sb", [128, 8], F32))
    zap_sb = e(nc.sbuf_tensor("zap_sb", [128, 8], F32))
    wsel_sb = e(nc.sbuf_tensor("wsel_sb", [128, 8], F32))
    nmax_sb = e(nc.sbuf_tensor("nmax_sb", [128, 1], F32))
    ssum_sb = e(nc.sbuf_tensor("ssum_sb", [128, 1], F32))
    btile_sb = e(nc.sbuf_tensor("btile_sb", [128, 16], F32))
    dtlow_bf = e(nc.sbuf_tensor("dtlow_bf", [DTR, NT], BF16))
    constv_sb = e(nc.sbuf_tensor("constv_sb", [128, 4], F32))

    AW = e(nc.sbuf_tensor("AW", [128, 4 * 16384], U8))
    AACT = e(nc.sbuf_tensor("AACT", [128, 118784], U8))

    def aw_slot(i):
        return AW[:, i * 16384:(i + 1) * 16384].bitcast(BF16)

    def av(off_b, nbytes, dt):
        return AACT[:, off_b:off_b + nbytes].bitcast(dt)

    K = 1024
    x_in = av(0, 16 * K, F32)
    x1c_v = av(16 * K, 16 * K, F32)
    xh_bf = av(16 * K, 8 * K, BF16)
    xh_lo = av(24 * K, 8 * K, BF16)
    y_acc = av(32 * K, 16 * K, F32)
    xs_bf = av(48 * K, 8 * K, BF16)
    h_bf = av(56 * K, 8 * K, BF16)
    p_bf = av(64 * K, 16 * K, BF16)
    xf_tmp = av(80 * K, 2 * K, F32)
    g1s_bf = av(80 * K, K, BF16)
    xsq = [av(82 * K, 2 * K, F32), av(84 * K, 2 * K, F32)]
    x2m = av(86 * K, 2 * K, F32)
    tmp_f = av(88 * K, 2 * K, F32)
    wbc_bf = av(90 * K, K, BF16)
    mask_f = av(91 * K, 2 * K, F32)
    sp_a = av(93 * K, 2 * K, F32)
    sp_b = av(95 * K, 2 * K, F32)
    cvb = [av(97 * K, 2072, F32), av(97 * K + 2080, 2072, F32)]
    zt_sb = [av(102 * K, 2 * K, F32), av(104 * K, 2 * K, F32)]
    xcv_sb = [av(106 * K, 2 * K, F32), av(108 * K, 2 * K, F32)]
    dtv_sb = zt_sb
    brow_sb = av(110 * K, 2 * K, F32)[0:16, :].bitcast(F32)
    crow_sb = av(112 * K, 2 * K, F32)[0:16, :].bitcast(F32)
    wT_sb = av(114 * K, 2 * K, F32)[0:8, :].bitcast(F32)
    dA_big = [av(0, 32 * K, F32), av(32 * K, 32 * K, F32)]
    dts_sb = [av(64 * K, 2 * K, F32), av(66 * K, 2 * K, F32)]
    xcs_sb = [av(68 * K, 2 * K, F32), av(70 * K, 2 * K, F32)]
    szs_sb = [av(72 * K, 2 * K, F32), av(74 * K, 2 * K, F32)]
    u_sb = av(76 * K, K, BF16)
    yt_sb = av(77 * K, 2 * K, F32)
    ygb_sb = [av(79 * K, K, BF16), av(80 * K, K, BF16)]
    sfix_sb = av(81 * K, 64, F32)
    irow_sb = [av(82 * K, 2 * K, F32)[0:1, :].bitcast(F32),
               av(84 * K, 2 * K, F32)[0:1, :].bitcast(F32)]
    Bint = aw_slot(0)
    Cint = aw_slot(1)
    vh_bufs = [aw_slot(2), aw_slot(3)]
    stg_sb = [av(0, 8 * K, BF16), av(8 * K, 8 * K, BF16)]

    PS = [e(nc.psum_tensor(f"PS{i}", [128, 512], F32)) for i in range(8)]
    bank_tok = [None] * 8

    for s in ALL_SEMS:
        S.sems[s] = e(nc.semaphore(s))
    block = e(nc.Block())

    def mm_group(out_view, bank, pieces, waits=(), tag=""):
        w = list(waits)
        if bank_tok[bank] is not None:
            w.append(bank_tok[bank])
        tok = None
        n = len(pieces)
        for i, (lh, rh) in enumerate(pieces):
            def fn(t, lh=lh, rh=rh, i=i, n=n, out_view=out_view):
                return t.matmul(out=out_view, lhsT=lh, rhs=rh,
                                start=(i == 0), stop=(i == n - 1))
            if i == n - 1:
                tok = S.op("tensor", fn, waits=w if i == 0 else (), inc="sP")
            else:
                S.op("tensor", fn, waits=w if i == 0 else ())
        return tok

    def act(out, in_, func, bias=None, scale=None, waits=()):
        def fn(s, out=out, in_=in_, func=func, bias=bias, scale=scale):
            kw = {}
            if bias is not None:
                kw["bias"] = bias
            if scale is not None:
                kw["scale"] = scale
            return s.activation(out=out, in_=in_, func=func, **kw)
        return S.op("scalar", fn, waits=waits, inc="sA")

    def vop(fn, waits=(), selfsync=False):
        return S.op("vector", fn, waits=waits, inc="sV", selfsync=selfsync)

    def pop_(fn, waits=()):
        return S.op("gpsimd", fn, waits=waits, inc="sO")

    slot_free = [None] * 4

    def load_slot(i, dram_ap_3d, waits=()):
        w = list(waits)
        if slot_free[i] is not None:
            w.append(slot_free[i])
        kc = dram_ap_3d.shape[1]
        m = dram_ap_3d.shape[2]
        sl = aw_slot(i)
        return S.dma("sync", sl[:, 0:kc * m].rearrange("p (kc m1) -> p kc m1", kc=kc),
                     dram_ap_3d, waits=w, sem=f"q_sl{i}")

    def expert_mat(dram, ei):
        return dram[ei].rearrange("(kc p) m -> p kc m", p=128)

    def win_view(sl):
        return s_win[sl * 256:(sl + 1) * 256].rearrange("(kc p) m -> p kc m", p=128)

    def xi_view(m):
        return AACT[:, 32768 + m * 2048: 32768 + (m + 1) * 2048].bitcast(F32)

    for dst, src in [(ident_sb, ident_d), (onesc_sb, onesc_d), (onesr_sb, onesr_d),
                     (sel8_sb, sel8_d), (bin_sb, bin_d), (cw_sb, cw_d),
                     (cb_sb, cb_d), (bx_sb, bx_d), (bxc_sb, bxc_d),
                     (bdt_sb, bdt_d), (alog_sb, alog_d), (dpar_sb, dpar_d),
                     (bout_sb, bout_d), (b1_sb, b1_d), (b2_sb, b2_d),
                     (b3_sb, b3_d), (constv_sb, constv_d)]:
        S.dma("sync", dst[:, :], src, sem="qC")
    for i_, v_ in enumerate([EPS, 1.0, -1.0, -0.5]):
        nc.const_aps.register(F32, v_, constv_sb[:, i_:i_ + 1])
    S.dma("sync", ggf_sb[:, :], wg_d.rearrange("(c p) e1 -> p c e1", p=128),
          sem="qC")
    tok_consts = ("qC", S.cnt["qC"])
    t_ghi = vop(lambda v: v.tensor_copy(out=ghi_sb[:, :], in_=ggf_sb[:, :]),
                waits=[tok_consts])
    t_glo = vop(lambda v: v.tensor_sub(out=glo_sb[:, :], in0=ggf_sb[:, :],
                                       in1=ghi_sb[:, :]), waits=[t_ghi])
    t_an = act(aneg_sb[:, :], alog_sb[:, :], AF.Exp, waits=[tok_consts])
    t_an = vop(lambda v: v.tensor_scalar(out=aneg_sb[:, :], in0=aneg_sb[:, :],
                                         scalar1=-1.0, scalar2=None, op0=ALU.mult),
               waits=[t_an])

    stg_free = [None, None]
    pctr = [0]

    def prep_one(df, ds_, r0, rows, cols):
        i = pctr[0] % 2
        pctr[0] += 1
        w = []
        if stg_free[i] is not None:
            w.append(stg_free[i])
        tok_c = S.dma("gpsimd", stg_sb[i][0:rows, 0:cols], df[r0:r0 + rows, :],
                      waits=w, sem=f"qG{i}")
        tok_s = S.dma("sync", ds_[r0:r0 + rows, :], stg_sb[i][0:rows, 0:cols],
                      waits=[tok_c], sem=f"qP{i}")
        stg_free[i] = tok_s

    ptok = {}

    def prep_done_tok():
        return [stg_free[0], stg_free[1]]

    for ei in range(NE):
        for nm, (df, ds_) in [("w1", (w1_d, s_w1)), ("w2", (w2_d, s_w2)),
                              ("w3", (w3_d, s_w3))]:
            for kc in range(8):
                prep_one(df[ei], ds_[ei], kc * 128, 128, D)
            ptok[(nm, ei)] = prep_done_tok()
    for nm, (df, ds_) in [("sw1", (sw1_d, s_sw1)), ("sw2", (sw2_d, s_sw2)),
                          ("win", (win_d, s_win)), ("sw3", (sw3_d, s_sw3)),
                          ("wout", (wout_d, s_wout)), ("wx", (wx_d, s_wx)),
                          ("wdt", (wdt_d, s_wdt))]:
        R = df.shape[0]
        r0 = 0
        while r0 < R:
            rows = min(128, R - r0)
            prep_one(df, ds_, r0, rows, df.shape[1])
            r0 += rows
        ptok[nm] = prep_done_tok()
    prep_toks = prep_done_tok()

    t_wx = S.dma("sync", wx_sb[:, :].rearrange("p (kc m1) -> p kc m1", kc=16),
                 s_wx.rearrange("(kc p) m -> p kc m", p=128), waits=ptok["wx"],
                 sem="qC2")
    S.dma("sync", wdt_sb[:, :], s_wdt, waits=ptok["wdt"], sem="qC2")
    t_wx = ("qC2", S.cnt["qC2"])
    t_wdt = t_wx

    def ln_full(src_view, src_toks, want_lo):
        ws = [t for t in src_toks if t is not None]
        w6 = list(ws)
        if bank_tok[6] is not None:
            w6.append(bank_tok[6])
        tok_sum = None
        for pc in range(PC_D):
            xb = src_view[:, pc * NT:(pc + 1) * NT]
            def fn(t, xb=xb, pc=pc):
                return t.matmul(out=PS[6][0:1, :], lhsT=onesc_sb[:, :], rhs=xb,
                                start=(pc == 0), stop=(pc == PC_D - 1))
            if pc == PC_D - 1:
                tok_sum = S.op("tensor", fn, waits=w6 if pc == 0 else (), inc="sP")
            else:
                S.op("tensor", fn, waits=w6 if pc == 0 else ())
        w7 = list(ws)
        if bank_tok[7] is not None:
            w7.append(bank_tok[7])
        sq_free = [None, None]
        tok_sq = None
        for pc in range(PC_D):
            xb = src_view[:, pc * NT:(pc + 1) * NT]
            wq = list(ws)
            if sq_free[pc % 2] is not None:
                wq.append(sq_free[pc % 2])
            t_s = act(xsq[pc % 2][:, :], xb, AF.Square, waits=wq)
            def fn(t, pc=pc):
                return t.matmul(out=PS[7][0:1, :], lhsT=onesc_sb[:, :],
                                rhs=xsq[pc % 2][:, :],
                                start=(pc == 0), stop=(pc == PC_D - 1))
            if pc == PC_D - 1:
                tok_sq = S.op("tensor", fn, waits=(w7 + [t_s]) if pc == 0 else [t_s],
                              inc="sP")
            else:
                tok_sq = S.op("tensor", fn, waits=(w7 + [t_s]) if pc == 0 else [t_s],
                              inc="sP")
            sq_free[pc % 2] = tok_sq
        t1 = vop(lambda v: v.tensor_scalar(out=mean_r[:, :], in0=PS[6][0:1, :],
                                           scalar1=1.0 / D, scalar2=None,
                                           op0=ALU.mult), waits=[tok_sum])
        t2 = vop(lambda v: v.tensor_scalar(out=srow_r[:, :], in0=PS[7][0:1, :],
                                           scalar1=1.0 / D, scalar2=None,
                                           op0=ALU.mult), waits=[tok_sq])
        t3 = vop(lambda v: v.tensor_mul(out=rstd_r[:, :], in0=mean_r[:, :],
                                        in1=mean_r[:, :]), waits=[t1])
        t4 = vop(lambda v: v.tensor_sub(out=srow_r[:, :], in0=srow_r[:, :],
                                        in1=rstd_r[:, :]), waits=[t2, t3])
        t5 = act(rstd_r[:, :], srow_r[:, :], AF.Ln, bias=float(EPS), waits=[t4])
        t6 = act(rstd_r[:, :], rstd_r[:, :], AF.Exp, scale=-0.5, waits=[t5])
        tok_mb = mm_group(PS[6][:, :], 6, [(onesr_sb[:, :], mean_r[:, :])],
                          waits=[t1])
        tok_rb = mm_group(PS[7][:, :], 7, [(onesr_sb[:, :], rstd_r[:, :])],
                          waits=[t6])
        toks_bf = []
        tok_lo = None
        for pc in range(PC_D):
            xb = src_view[:, pc * NT:(pc + 1) * NT]
            ta = vop(lambda v, xb=xb: v.tensor_sub(out=xf_tmp[:, :], in0=xb,
                                                   in1=PS[6][:, :]),
                     waits=[tok_mb])
            tb = vop(lambda v: v.tensor_mul(out=xf_tmp[:, :], in0=xf_tmp[:, :],
                                            in1=PS[7][:, :]), waits=[tok_rb, ta])
            tc = vop(lambda v, pc=pc: v.tensor_copy(
                out=xh_bf[:, pc * NT:(pc + 1) * NT], in_=xf_tmp[:, :]), waits=[tb])
            toks_bf.append(tc)
            if want_lo:
                tok_lo = vop(lambda v, pc=pc: v.tensor_sub(
                    out=xh_lo[:, pc * NT:(pc + 1) * NT], in0=xf_tmp[:, :],
                    in1=xh_bf[:, pc * NT:(pc + 1) * NT]), waits=[tc])
        bank_tok[6] = tok_lo if want_lo else toks_bf[-1]
        bank_tok[7] = bank_tok[6]
        return toks_bf, tok_lo

    def routing(bf_toks, tok_lo):
        tok_wT = None
        for tt in range(NT // 128):
            w = [bf_toks[-1], tok_lo, t_ghi, t_glo]
            if bank_tok[4] is not None:
                w.append(bank_tok[4])
            combos = []
            for pc in range(PC_D):
                lh_hi = xh_bf[:, pc * NT + tt * 128: pc * NT + (tt + 1) * 128]
                lh_lo = xh_lo[:, pc * NT + tt * 128: pc * NT + (tt + 1) * 128]
                combos.append((lh_hi, ghi_sb[:, pc * 8:(pc + 1) * 8]))
                combos.append((lh_hi, glo_sb[:, pc * 8:(pc + 1) * 8]))
                combos.append((lh_lo, ghi_sb[:, pc * 8:(pc + 1) * 8]))
            tokg = None
            for i, (lh, rh) in enumerate(combos):
                def fn(t, lh=lh, rh=rh, i=i, n=len(combos)):
                    return t.matmul(out=PS[4][:, 0:8], lhsT=lh, rhs=rh,
                                    start=(i == 0), stop=(i == n - 1))
                if i == len(combos) - 1:
                    tokg = S.op("tensor", fn, waits=w if i == 0 else (), inc="sP")
                else:
                    S.op("tensor", fn, waits=w if i == 0 else ())
            ta = vop(lambda v: v.reduce_max(out=nmax_sb[:, :], in_=PS[4][:, 0:8],
                                            axis=AX, negate=True),
                     waits=[tokg], selfsync=True)
            tb = act(scores_sb[:, :], PS[4][:, 0:8], AF.Exp, bias=nmax_sb[:, 0:1],
                     waits=[ta])
            bank_tok[4] = tb
            tc = vop(lambda v: v.reduce_sum(out=ssum_sb[:, :], in_=scores_sb[:, :],
                                            axis=AX), waits=[tb], selfsync=True)
            td = vop(lambda v: v.reciprocal(out=ssum_sb[:, :], in_=ssum_sb[:, :]),
                     waits=[tc], selfsync=True)
            te = vop(lambda v: v.tensor_scalar(out=scores_sb[:, :],
                                               in0=scores_sb[:, :],
                                               scalar1=ssum_sb[:, 0:1],
                                               scalar2=None, op0=ALU.mult),
                     waits=[td], selfsync=True)
            tf = vop(lambda v: v.max(out=m8_sb[:, :], in_=scores_sb[:, :]),
                     waits=[te], selfsync=True)
            tg = vop(lambda v: v.memset(m2_sb[:, :], -1.0), selfsync=True)
            th = vop(lambda v: v.tensor_copy(out=m2_sb[:, 0:2], in_=m8_sb[:, 0:2]),
                     waits=[tf, tg], selfsync=True)
            ti = vop(lambda v: v.match_replace(out=zap_sb[:, :],
                                               in_to_replace=m2_sb[:, :],
                                               in_values=scores_sb[:, :],
                                               imm_value=0.0),
                     waits=[th], selfsync=True)
            tj = vop(lambda v: v.tensor_sub(out=wsel_sb[:, :], in0=scores_sb[:, :],
                                            in1=zap_sb[:, :]),
                     waits=[ti], selfsync=True)
            w5 = [tj]
            if bank_tok[5] is not None:
                w5.append(bank_tok[5])
            def ftr(t):
                return t.transpose(out=PS[5][0:8, 0:128], in_=wsel_sb[:, :],
                                   identity=ident_sb[:, :])
            tk = S.op("tensor", ftr, waits=w5, inc="sP")
            tok_wT = vop(lambda v, tt=tt: v.tensor_copy(
                out=wT_sb[:, tt * 128:(tt + 1) * 128], in_=PS[5][0:8, 0:128]),
                waits=[tk])
            bank_tok[5] = tok_wT
        return tok_wT

    def ffn_prefetch():
        mats = []
        for ei in range(NE):
            mats += [("w1", ei), ("w2", ei), ("w3", ei)]
        dram_of = {"w1": s_w1, "w2": s_w2, "w3": s_w3}
        mat_tok = {}
        next_load = [0]

        def pump(n):
            while next_load[0] < len(mats) and next_load[0] < n:
                mi = next_load[0]
                kind, ei = mats[mi]
                sl = mi % 4
                mat_tok[(kind, ei)] = (sl, load_slot(
                    sl, expert_mat(dram_of[kind], ei), waits=ptok[(kind, ei)]))
                next_load[0] += 1
        return mat_tok, pump

    def ffn(bf_toks, tok_wT, mat_tok, pump):
        y_toks = [None] * PC_D
        prev_xs_last = None
        prev_x2_toks = []
        tok_mask = None
        for ei in range(NE):
            w6 = [tok_wT]
            if bank_tok[6] is not None:
                w6.append(bank_tok[6])
            def fbc(t, ei=ei):
                return t.matmul(out=PS[6][:, :],
                                lhsT=sel8_sb[:, ei * 128:(ei + 1) * 128],
                                rhs=wT_sb[:, :], start=True, stop=True)
            tbc = S.op("tensor", fbc, waits=w6, inc="sP")
            tw = act(wbc_bf[:, :], PS[6][:, :], AF.Identity,
                     waits=[tbc, prev_xs_last])
            bank_tok[6] = tw
            tok_mask = act(mask_f[:, :], wbc_bf[:, :], AF.Sign,
                           waits=[tw] + prev_x2_toks)
            xs_toks = []
            for pc in range(PC_D):
                xs_toks.append(vop(lambda v, pc=pc: v.tensor_mul(
                    out=xs_bf[:, pc * NT:(pc + 1) * NT],
                    in0=xh_bf[:, pc * NT:(pc + 1) * NT], in1=wbc_bf[:, :]),
                    waits=[tw, bf_toks[pc]]))
            prev_xs_last = xs_toks[-1]
            sl1, tok_w1 = mat_tok[("w1", ei)]
            h_toks = []
            for m in range(PC_D):
                bank = m % 2
                pieces = [(aw_slot(sl1).rearrange("p (kc m1) -> p kc m1", kc=8)
                           [:, kc, m * 128:(m + 1) * 128],
                           xs_bf[:, kc * NT:(kc + 1) * NT]) for kc in range(8)]
                tokm = mm_group(PS[bank][:, :], bank, pieces,
                                waits=[tok_w1] + xs_toks)
                t_h = act(h_bf[:, m * NT:(m + 1) * NT], PS[bank][:, :], AF.Silu,
                          bias=b1_sb[:, m * 8 + ei: m * 8 + ei + 1], waits=[tokm])
                bank_tok[bank] = t_h
                h_toks.append(t_h)
            sl2, tok_w2 = mat_tok[("w2", ei)]
            sl3, tok_w3 = mat_tok[("w3", ei)]
            prev_x2_toks = []
            for m in range(PC_D):
                b2k = 2 + (m % 2)
                pieces2 = [(aw_slot(sl2).rearrange("p (kc m1) -> p kc m1", kc=8)
                            [:, kc, m * 128:(m + 1) * 128],
                            h_bf[:, kc * NT:(kc + 1) * NT]) for kc in range(8)]
                tok2 = mm_group(PS[b2k][:, :], b2k, pieces2,
                                waits=[tok_w2] + h_toks)
                t_x2 = vop(lambda v, b2k=b2k, m=m, ei=ei: v.scalar_tensor_tensor(
                    out=x2m[:, :], in0=PS[b2k][:, :],
                    scalar=b2_sb[:, m * 8 + ei: m * 8 + ei + 1],
                    in1=mask_f[:, :], op0=ALU.add, op1=ALU.mult),
                    waits=[tok2, tok_mask])
                bank_tok[b2k] = t_x2
                prev_x2_toks.append(t_x2)
                b3k = 4 + (m % 2)
                pieces3 = [(aw_slot(sl3).rearrange("p (kc m1) -> p kc m1", kc=8)
                            [:, kc, m * 128:(m + 1) * 128],
                            xs_bf[:, kc * NT:(kc + 1) * NT]) for kc in range(8)]
                tok3 = mm_group(PS[b3k][:, :], b3k, pieces3,
                                waits=[tok_w3] + xs_toks)
                t_x3 = vop(lambda v, b3k=b3k, m=m, ei=ei: v.scalar_tensor_tensor(
                    out=tmp_f[:, :], in0=PS[b3k][:, :],
                    scalar=b3_sb[:, m * 8 + ei: m * 8 + ei + 1],
                    in1=x2m[:, :], op0=ALU.add, op1=ALU.mult),
                    waits=[tok3, t_x2] + ([y_toks[m - 1]] if m > 0 else
                                          ([y_toks[PC_D - 1]] if y_toks[PC_D - 1] else [])))
                bank_tok[b3k] = t_x3
                if ei == 0:
                    t_y = pop_(lambda g, m=m: g.tensor_copy(
                        out=y_acc[:, m * NT:(m + 1) * NT], in_=tmp_f[:, :]),
                        waits=[t_x3] + reg_free["y_acc"])
                else:
                    t_y = pop_(lambda g, m=m: g.tensor_add(
                        out=y_acc[:, m * NT:(m + 1) * NT],
                        in0=y_acc[:, m * NT:(m + 1) * NT], in1=tmp_f[:, :]),
                        waits=[t_x3, y_toks[m]])
                y_toks[m] = t_y
            done_tok = y_toks[PC_D - 1]
            for (sl, _) in (mat_tok[("w1", ei)], mat_tok[("w2", ei)],
                            mat_tok[("w3", ei)]):
                slot_free[sl] = done_tok
            pump((ei + 1) * 3 + 4)
        def sw_half(dram, half):
            return dram[half * 512:(half + 1) * 512].rearrange(
                "(kc p) m -> p kc m", p=128)
        t_sw1 = [load_slot(0, sw_half(s_sw1, 0), waits=ptok["sw1"]),
                 load_slot(1, sw_half(s_sw1, 1), waits=ptok["sw1"])]
        t_sw2 = [load_slot(2, sw_half(s_sw2, 0), waits=ptok["sw2"]),
                 load_slot(3, sw_half(s_sw2, 1), waits=ptok["sw2"])]
        p_toks = []
        prev_tp = None
        for m in range(PC_DI):
            bank = m % 2
            pieces = []
            for kc in range(8):
                sl = 0 if kc < 4 else 1
                pieces.append((aw_slot(sl).rearrange("p (kc m1) -> p kc m1", kc=4)
                               [:, kc % 4, m * 128:(m + 1) * 128],
                               xh_bf[:, kc * NT:(kc + 1) * NT]))
            tok1 = mm_group(PS[bank][:, :], bank, pieces,
                            waits=t_sw1 + [bf_toks[-1]])
            wg1 = [tok1]
            if prev_tp is not None:
                wg1.append(prev_tp)
            t_g1 = act(g1s_bf[:, :], PS[bank][:, :], AF.Silu, waits=wg1)
            bank_tok[bank] = t_g1
            bank2 = 2 + (m % 2)
            pieces = []
            for kc in range(8):
                sl = 2 if kc < 4 else 3
                pieces.append((aw_slot(sl).rearrange("p (kc m1) -> p kc m1", kc=4)
                               [:, kc % 4, m * 128:(m + 1) * 128],
                               xh_bf[:, kc * NT:(kc + 1) * NT]))
            tok2 = mm_group(PS[bank2][:, :], bank2, pieces,
                            waits=t_sw2 + [bf_toks[-1]])
            t_p = vop(lambda v, m=m, bank2=bank2: v.tensor_mul(
                out=p_bf[:, m * NT:(m + 1) * NT], in0=g1s_bf[:, :],
                in1=PS[bank2][:, :]),
                waits=[tok2, t_g1] + reg_free["p_bf"])
            bank_tok[bank2] = t_p
            prev_tp = t_p
            p_toks.append(t_p)
        slot_free[0] = p_toks[-1]
        slot_free[1] = p_toks[-1]
        t_sw3 = [load_slot(0, s_sw3[0:1024].rearrange("(kc p) m -> p kc m", p=128),
                           waits=ptok["sw3"]),
                 load_slot(1, s_sw3[1024:2048].rearrange("(kc p) m -> p kc m", p=128),
                           waits=ptok["sw3"])]
        for m in range(PC_D):
            bank = 4 + (m % 2)
            pieces = []
            for kc in range(16):
                sl = 0 if kc < 8 else 1
                pieces.append((aw_slot(sl).rearrange("p (kc m1) -> p kc m1", kc=8)
                               [:, kc % 8, m * 128:(m + 1) * 128],
                               p_bf[:, kc * NT:(kc + 1) * NT]))
            tokz = mm_group(PS[bank][:, :], bank, pieces, waits=t_sw3 + p_toks)
            t_y = vop(lambda v, m=m, bank=bank: v.tensor_add(
                out=y_acc[:, m * NT:(m + 1) * NT],
                in0=y_acc[:, m * NT:(m + 1) * NT], in1=PS[bank][:, :]),
                waits=[tokz, y_toks[m]])
            bank_tok[bank] = t_y
            y_toks[m] = t_y
        slot_free[0] = y_toks[-1]
        slot_free[1] = y_toks[-1]
        slot_free[2] = p_toks[-1]
        slot_free[3] = p_toks[-1]
        reg_free["p_bf"] = []
        reg_free["y_acc"] = []
        return y_toks

    reg_free = {"y_acc": [], "p_bf": []}
    halo_toks = [None] * PC_DI
    btile_free = [None]
    xin_free = [[]]
    c_free = [[]]
    dbg_braw = None
    if debug:
        dbg_braw = [scr("s_dbgB", [NB, 16, T], F32), scr("s_dbgC", [NB, 16, T], F32)]

    def mamba_pre(b, c, x1_toks):
        bf_toks, _ = ln_full(x_in[:, :], x1_toks, want_lo=False)
        ln2_bf = bf_toks
        t_win = [load_slot(sl, win_view(sl), waits=ptok["win"]) for sl in range(4)]
        xi_toks = [None] * PC_DI
        zt_free = [None, None]
        sl_of = [0, 0, 1, 1, 2, 2, 3, 3]
        for m in range(32):
            bank = m % 2
            pieces = []
            for kc in range(8):
                sl = sl_of[kc]
                pieces.append((aw_slot(sl).rearrange("p (kc m1) -> p kc m1", kc=2)
                               [:, kc % 2, m * 128:(m + 1) * 128],
                               xh_bf[:, kc * NT:(kc + 1) * NT]))
            tokm = mm_group(PS[bank][:, :], bank, pieces,
                            waits=t_win + [bf_toks[-1]])
            if m < PC_DI:
                t_xi = act(xi_view(m), PS[bank][:, :], AF.Identity,
                           bias=bin_sb[:, m:m + 1], waits=[tokm])
                bank_tok[bank] = t_xi
                xi_toks[m] = t_xi
            else:
                mz = m - PC_DI
                i = mz % 2
                w = [tokm]
                if zt_free[i] is not None:
                    w.append(zt_free[i])
                t_z = act(zt_sb[i][:, :], PS[bank][:, :], AF.Silu,
                          bias=bin_sb[:, m:m + 1], waits=w)
                bank_tok[bank] = t_z
                zt_free[i] = S.dma("sync", s_sz[b, mz * 128:(mz + 1) * 128,
                                               c * NT:(c + 1) * NT],
                                   zt_sb[i][:, :], waits=[t_z], sem=f"q_zt{i}")
        conv_bf_toks = []
        xcv_free = [None, None]
        sp_free = [None, None]
        for mi in range(PC_DI):
            even = (mi % 2 == 0)
            opf = vop
            cb_i = cvb[0] if even else cvb[1]
            sp_i = sp_a if even else sp_b
            t0 = opf(lambda v, mi=mi, cb_i=cb_i: v.tensor_copy(
                out=cb_i[:, 3:3 + NT], in_=xi_view(mi)), waits=[xi_toks[mi]])
            if c == 0:
                t1_ = opf(lambda v, cb_i=cb_i: v.memset(cb_i[:, 0:3], 0.0),
                          waits=[t0])
            else:
                t1_ = opf(lambda v, mi=mi, cb_i=cb_i: v.tensor_copy(
                    out=cb_i[:, 0:3], in_=halo_sb[:, mi * 3:(mi + 1) * 3]),
                    waits=[t0, halo_toks[mi]])
            w2_ = [t1_]
            if sp_free[mi % 2] is not None:
                w2_.append(sp_free[mi % 2])
            t2_ = opf(lambda v, mi=mi, cb_i=cb_i, sp_i=sp_i: v.tensor_scalar(
                out=sp_i[:, :], in0=cb_i[:, 0:NT],
                scalar1=cw_sb[:, mi * 4:mi * 4 + 1], scalar2=None, op0=ALU.mult),
                waits=w2_)
            for j in range(1, 4):
                t2_ = opf(lambda v, mi=mi, j=j, cb_i=cb_i, sp_i=sp_i:
                          v.scalar_tensor_tensor(
                              out=sp_i[:, :], in0=cb_i[:, j:j + NT],
                              scalar=cw_sb[:, mi * 4 + j:mi * 4 + j + 1],
                              in1=sp_i[:, :], op0=ALU.mult, op1=ALU.add),
                          waits=[t2_])
            halo_toks[mi] = opf(lambda v, mi=mi, cb_i=cb_i: v.tensor_copy(
                out=halo_sb[:, mi * 3:(mi + 1) * 3], in_=cb_i[:, NT:NT + 3]),
                waits=[t1_])
            i = mi % 2
            w = [t2_]
            if xcv_free[i] is not None:
                w.extend(xcv_free[i])
            t_cv = act(xcv_sb[i][:, :], sp_i[:, :], AF.Silu,
                       bias=cb_sb[:, mi:mi + 1], waits=w)
            sp_free[mi % 2] = t_cv
            tok_st = S.dma("sync", s_xc[b, mi * 128:(mi + 1) * 128,
                                        c * NT:(c + 1) * NT],
                           xcv_sb[i][:, :], waits=[t_cv], sem=f"q_xcv{i}")
            t_bf = vop(lambda v, mi=mi, i=i: v.tensor_copy(
                out=p_bf[:, mi * NT:(mi + 1) * NT], in_=xcv_sb[i][:, :]),
                waits=[t_cv])
            conv_bf_toks.append(t_bf)
            xcv_free[i] = [tok_st, t_bf]
        grp_defs = [(0, 64, 4, bx_sb[0:64, 0:1], dtlow_bf[:, :]),
                    (64, 80, 5, brow_sb[:, :], None),
                    (80, 96, 6, crow_sb[:, :], None)]
        row_toks = {}
        for (c0, c1, bidx, arg1, arg2) in grp_defs:
            w = conv_bf_toks + [t_wx]
            if bank_tok[bidx] is not None:
                w.append(bank_tok[bidx])
            tokd = None
            for kc in range(PC_DI):
                def fn(t, kc=kc, c0=c0, c1=c1, bidx=bidx):
                    return t.matmul(out=PS[bidx][0:c1 - c0, :],
                                    lhsT=wx_sb[:, :].rearrange(
                                        "p (kc m1) -> p kc m1", kc=16)[:, kc, c0:c1],
                                    rhs=p_bf[:, kc * NT:(kc + 1) * NT],
                                    start=(kc == 0), stop=(kc == PC_DI - 1))
                if kc == PC_DI - 1:
                    tokd = S.op("tensor", fn, waits=w if kc == 0 else (), inc="sP")
                else:
                    S.op("tensor", fn, waits=w if kc == 0 else ())
            if c0 == 0:
                t_e = act(dtlow_bf[:, :], PS[bidx][0:64, :], AF.Identity,
                          bias=bx_sb[0:64, 0:1], waits=[tokd])
            elif c0 == 64:
                t_e = act(brow_sb[:, :], PS[bidx][0:16, :], AF.Identity,
                          bias=bx_sb[64:80, 0:1], waits=[tokd])
            else:
                t_e = act(crow_sb[:, :], PS[bidx][0:16, :], AF.Identity,
                          bias=bxc_sb[0:16, 0:1], waits=[tokd])
            bank_tok[bidx] = t_e
            row_toks[c0] = t_e
        for (rows_sb, dst, base) in ((brow_sb, s_bi, 64), (crow_sb, s_ci, 80)):
            S.dma("sync", dst[b, :, c * NT:(c + 1) * NT], rows_sb[:, :],
                  waits=[row_toks[base]], sem="q_bt")
        dtv_free = list(zt_free)
        prev_td = [None, None]
        for mi in range(PC_DI):
            bank = mi % 2
            w = [row_toks[0], t_wdt]
            if bank_tok[bank] is not None:
                w.append(bank_tok[bank])
            def fdt(t, mi=mi, bank=bank):
                return t.matmul(out=PS[bank][:, :],
                                lhsT=wdt_sb[:, mi * 128:(mi + 1) * 128],
                                rhs=dtlow_bf[:, :], start=True, stop=True)
            tokd = S.op("tensor", fdt, waits=w, inc="sP")
            wa = [tokd]
            if prev_td[0] is not None:
                wa.append(prev_td[0])
            t_a = act(sp_a[:, :], PS[bank][:, :], AF.Abs,
                      bias=bdt_sb[:, mi:mi + 1], waits=wa)
            wr = [t_a]
            if prev_td[1] is not None:
                wr.append(prev_td[1])
            t_r = act(sp_b[:, :], PS[bank][:, :], AF.Relu,
                      bias=bdt_sb[:, mi:mi + 1], waits=wr)
            bank_tok[bank] = t_r
            t_e1 = act(sp_a[:, :], sp_a[:, :], AF.Exp, scale=-1.0, waits=[t_r])
            t_e2 = act(sp_a[:, :], sp_a[:, :], AF.Ln, bias=1.0, waits=[t_e1])
            i = mi % 2
            w2_ = [t_e2]
            if dtv_free[i] is not None:
                w2_.append(dtv_free[i])
            t_d = vop(lambda v, i=i: v.tensor_add(out=dtv_sb[i][:, :],
                                                  in0=sp_a[:, :], in1=sp_b[:, :]),
                      waits=w2_)
            prev_td[0] = t_d
            prev_td[1] = t_d
            dtv_free[i] = S.dma("sync", s_dt[b, mi * 128:(mi + 1) * 128,
                                             c * NT:(c + 1) * NT],
                                dtv_sb[i][:, :], waits=[t_d], sem=f"q_zt{i}")
        return bf_toks


    def phase_a(b, c):
        S.barrier_all(exclude=("sync",))
        mat_tok, pump = ffn_prefetch()
        pump(4)
        wx_ = list(xin_free[0])
        xin_free[0] = []
        tok_x = S.dma("sync", x_in[:, :].rearrange("p (pc t1) -> p pc t1", pc=PC_D),
                      x_in_d[b].rearrange("(pc p) t1 -> p pc t1", p=128)
                      [:, :, c * NT:(c + 1) * NT], sem="q_x", waits=wx_)
        reg_free["y_acc"] = []
        reg_free["p_bf"] = []
        bf_toks, tok_lo = ln_full(x_in[:, :], [tok_x], want_lo=True)
        tok_wT = routing(bf_toks, tok_lo)
        y_toks = ffn(bf_toks, tok_wT, mat_tok, pump)
        x1_toks = []
        for m in range(PC_D):
            t_x1 = vop(lambda v, m=m: v.scalar_tensor_tensor(
                out=x_in[:, m * NT:(m + 1) * NT],
                in0=y_acc[:, m * NT:(m + 1) * NT], scalar=0.5,
                in1=x_in[:, m * NT:(m + 1) * NT],
                op0=ALU.mult, op1=ALU.add), waits=[y_toks[m]])
            x1_toks.append(t_x1)
        t_x1s = S.dma("sync", s_x1[b][:, c * NT:(c + 1) * NT].rearrange(
            "(pc p) t1 -> p pc t1", p=128),
            x_in[:, :].rearrange("p (pc t1) -> p pc t1", pc=PC_D),
            waits=x1_toks, sem="q_x1s")
        ln2_toks = mamba_pre(b, c, x1_toks)
        xin_free[0] = [t_x1s, ln2_toks[-1]]

    def phase_b(b):
        irow_free = [None, None]
        slice_free = [None, None]
        yg_free = [None, None]
        vh_free = [None, None]
        dA_free_big = [None, None]
        bint_free = [None]
        prev_tyg = [None]
        st_toks = [None] * PC_DI
        for tq in range(NTQ):
            bc_toks = []
            for (src, dst_int, bank) in ((s_bi, Bint, 6), (s_ci, Cint, 7)):
                for s_i in range(DS):
                    i = s_i % 2
                    w = []
                    if irow_free[i] is not None:
                        w.append(irow_free[i])
                    t_r = S.dma("sync", irow_sb[i][:, :],
                                src[b, s_i, tq * TQ:(tq + 1) * TQ],
                                waits=w, sem=f"q_ir{i}")
                    wq = [t_r]
                    if bank_tok[bank] is not None:
                        wq.append(bank_tok[bank])
                    if bint_free[0] is not None:
                        wq.append(bint_free[0])
                    def fb(t, i=i, bank=bank):
                        return t.matmul(out=PS[bank][:, :], lhsT=onesr_sb[:, :],
                                        rhs=irow_sb[i][0:1, :],
                                        start=True, stop=True)
                    tkb = S.op("tensor", fb, waits=wq, inc="sP")
                    t_ev = act(dst_int[:, s_i * TQ:(s_i + 1) * TQ],
                               PS[bank][:, :], AF.Identity, waits=[tkb])
                    bank_tok[bank] = t_ev
                    irow_free[i] = t_ev
                    bc_toks.append(t_ev)
            pre = {}

            def emit_pre(dc):
                i = dc % 2
                vh = vh_bufs[dc % 2]
                w = []
                if slice_free[i] is not None:
                    w.append(slice_free[i])
                t_dt = S.dma("sync", dts_sb[i][:, :],
                             s_dt[b, dc * 128:(dc + 1) * 128, tq * TQ:(tq + 1) * TQ],
                             waits=w, sem=f"q_dts{i}")
                t_xc = S.dma("sync", xcs_sb[i][:, :],
                             s_xc[b, dc * 128:(dc + 1) * 128, tq * TQ:(tq + 1) * TQ],
                             sem=f"q_xcs{i}")
                t_sz = S.dma("sync", szs_sb[i][:, :],
                             s_sz[b, dc * 128:(dc + 1) * 128, tq * TQ:(tq + 1) * TQ],
                             sem=f"q_szs{i}")
                t_u = pop_(lambda g, i=i: g.tensor_mul(out=u_sb[:, :],
                                                       in0=dts_sb[i][:, :],
                                                       in1=xcs_sb[i][:, :]),
                           waits=[t_dt, t_xc])
                wv = [t_u] + bc_toks
                if vh_free[dc % 2] is not None:
                    wv.append(vh_free[dc % 2])
                t_v = pop_(lambda g, vh=vh: g.tensor_tensor(
                    out=vh[:, :].rearrange("p (s1 t1) -> p s1 t1", s1=DS),
                    in0=u_sb[:, :].unsqueeze(1).to_broadcast([128, DS, TQ]),
                    in1=Bint[:, :].rearrange("p (s1 t1) -> p s1 t1", s1=DS),
                    op=ALU.mult), waits=wv)
                dw = [t_dt, t_an]
                if dA_free_big[dc % 2] is not None:
                    dw.append(dA_free_big[dc % 2])
                t_dA = None
                for s_i in range(DS):
                    t_dA = act(dA_big[dc % 2][:, s_i * TQ:(s_i + 1) * TQ],
                               dts_sb[i][:, :], AF.Exp,
                               scale=aneg_sb[:, dc * DS + s_i: dc * DS + s_i + 1],
                               waits=dw if s_i == 0 else ())
                pre[dc] = (t_dt, t_xc, t_sz, t_v, t_dA)

            emit_pre(0)
            for dc in range(PC_DI):
                i = dc % 2
                vh = vh_bufs[dc % 2]
                dAb = dA_big[dc % 2]
                t_dt, t_xc, t_sz, t_v, t_dA = pre.pop(dc)
                seg0 = dAb[:, :].rearrange("p (s1 t1) -> p s1 t1", t1=TQ)[:, :, 0]
                if tq > 0:
                    t_f1 = vop(lambda v, seg0=seg0, dc=dc: v.tensor_mul(
                        out=sfix_sb[:, 0:16], in0=seg0,
                        in1=state_sb[:, dc * DS:(dc + 1) * DS]),
                        waits=[t_dA, st_toks[dc]])
                    t_f2 = vop(lambda v, vh=vh: v.tensor_add(
                        out=vh[:, :].rearrange("p (s1 t1) -> p s1 t1", t1=TQ)[:, :, 0],
                        in0=vh[:, :].rearrange("p (s1 t1) -> p s1 t1", t1=TQ)[:, :, 0],
                        in1=sfix_sb[:, 0:16]), waits=[t_f1, t_v])
                else:
                    t_f2 = t_v
                t_z0 = vop(lambda v, seg0=seg0: v.tensor_scalar(
                    out=seg0, in0=seg0, scalar1=0.0, scalar2=None, op0=ALU.mult),
                    waits=[t_dA] + ([t_f1] if tq > 0 else []))
                t_sc = vop(lambda v, vh=vh, dAb=dAb: v.tensor_tensor_scan(
                    out=vh[:, :], data0=dAb[:, :], data1=vh[:, :],
                    initial=0.0, op0=ALU.mult, op1=ALU.add),
                    waits=[t_z0, t_f2], selfsync=True)
                dA_free_big[dc % 2] = t_sc
                t_st = vop(lambda v, dc=dc, vh=vh: v.tensor_copy(
                    out=state_sb[:, dc * DS:(dc + 1) * DS],
                    in_=vh[:, :].rearrange("p (s1 t1) -> p s1 t1", t1=TQ)[:, :, TQ - 1]),
                    waits=[t_sc], selfsync=True)
                st_toks[dc] = t_st
                if dc + 1 < PC_DI:
                    emit_pre(dc + 1)
                if dc % 3 == 0:
                    t_hc = vop(lambda v, vh=vh: v.tensor_mul(out=vh[:, :],
                                                             in0=vh[:, :],
                                                             in1=Cint[:, :]),
                               waits=[t_st])
                else:
                    t_hc = pop_(lambda g, vh=vh: g.tensor_mul(out=vh[:, :],
                                                              in0=vh[:, :],
                                                              in1=Cint[:, :]),
                                waits=[t_st])
                wy = [t_hc]
                if prev_tyg[0] is not None:
                    wy.append(prev_tyg[0])
                t_y = vop(lambda v, vh=vh: v.tensor_reduce(
                    out=yt_sb[:, :],
                    in_=vh[:, :].rearrange("p (s1 t1) -> p t1 s1", s1=DS),
                    axis=AX, op=ALU.add), waits=wy, selfsync=True)
                vh_free[dc % 2] = t_y
                bint_free[0] = t_y
                t_yd = vop(lambda v, i=i, dc=dc: v.scalar_tensor_tensor(
                    out=yt_sb[:, :], in0=xcs_sb[i][:, :],
                    scalar=dpar_sb[:, dc:dc + 1], in1=yt_sb[:, :],
                    op0=ALU.mult, op1=ALU.add), waits=[t_y])
                iy = dc % 2
                wyg = [t_yd, t_sz]
                if yg_free[iy] is not None:
                    wyg.append(yg_free[iy])
                t_yg = pop_(lambda g, i=i, iy=iy: g.tensor_mul(
                    out=ygb_sb[iy][:, :], in0=yt_sb[:, :], in1=szs_sb[i][:, :]),
                    waits=wyg)
                prev_tyg[0] = t_yg
                yg_free[iy] = S.dma("gpsimd", s_yg[b, dc * 128:(dc + 1) * 128,
                                                  tq * TQ:(tq + 1) * TQ],
                                    ygb_sb[iy][:, :], waits=[t_yg],
                                    sem=f"q_yg{iy}")
                slice_free[i] = t_yg

    def phase_c(b, c):
        S.barrier_all(exclude=("sync",))
        yg_v = av(0, 16 * K, BF16)
        x2_v = av(0, 16 * K, F32)
        mat_tok_c, pump_c = ffn_prefetch()
        wc_ = list(c_free[0])
        c_free[0] = []
        tok_yg = S.dma("sync", yg_v[:, :].rearrange("p (kc t1) -> p kc t1", kc=PC_DI),
                       s_yg[b].rearrange("(kc p) t1 -> p kc t1", p=128)
                       [:, :, c * NT:(c + 1) * NT], sem="q_ygl", waits=wc_)
        tok_x1 = S.dma("sync", x1c_v[:, :].rearrange("p (pc t1) -> p pc t1", pc=PC_D),
                       s_x1[b].rearrange("(pc p) t1 -> p pc t1", p=128)
                       [:, :, c * NT:(c + 1) * NT], sem="q_x1l")
        t_wo = [load_slot(0, s_wout[0:1024].rearrange("(kc p) m -> p kc m", p=128),
                          waits=ptok["wout"]),
                load_slot(1, s_wout[1024:2048].rearrange("(kc p) m -> p kc m", p=128),
                          waits=ptok["wout"])]
        mm_toks = []
        for m in range(PC_D):
            pieces = []
            for kc in range(PC_DI):
                sl = 0 if kc < 8 else 1
                pieces.append((aw_slot(sl).rearrange("p (kc m1) -> p kc m1", kc=8)
                               [:, kc % 8, m * 128:(m + 1) * 128],
                               yg_v[:, kc * NT:(kc + 1) * NT]))
            mm_toks.append(mm_group(PS[m][:, :], m, pieces,
                                    waits=t_wo + [tok_yg]))
        x2_toks = []
        for m in range(PC_D):
            t_b = vop(lambda v, m=m: v.scalar_tensor_tensor(
                out=x2_v[:, m * NT:(m + 1) * NT], in0=PS[m][:, :],
                scalar=bout_sb[:, m:m + 1], in1=x1c_v[:, m * NT:(m + 1) * NT],
                op0=ALU.add, op1=ALU.add),
                waits=[mm_toks[m], tok_x1, mm_toks[PC_D - 1]])
            bank_tok[m] = t_b
            x2_toks.append(t_b)
        slot_free[0] = x2_toks[-1]
        slot_free[1] = x2_toks[-1]
        pump_c(4)
        reg_free["y_acc"] = []
        reg_free["p_bf"] = []
        bf_toks, tok_lo = ln_full(x2_v[:, :], x2_toks, want_lo=True)
        tok_wT = routing(bf_toks, tok_lo)
        y_toks = ffn(bf_toks, tok_wT, mat_tok_c, pump_c)
        out_toks = []
        for m in range(PC_D):
            t_o = vop(lambda v, m=m: v.scalar_tensor_tensor(
                out=x2_v[:, m * NT:(m + 1) * NT],
                in0=y_acc[:, m * NT:(m + 1) * NT], scalar=0.5,
                in1=x2_v[:, m * NT:(m + 1) * NT],
                op0=ALU.mult, op1=ALU.add), waits=[y_toks[m]])
            out_toks.append(t_o)
        t_os = S.dma("sync", out_d[b].rearrange("(pc p) t1 -> p pc t1", p=128)
              [:, :, c * NT:(c + 1) * NT],
              x2_v[:, :].rearrange("p (pc t1) -> p pc t1", pc=PC_D),
              waits=out_toks, sem="qOut")
        c_free[0] = [t_os, y_toks[-1]]

    xin_free[0] = [prep_toks[0], prep_toks[1]]
    for b in range(NB):
        for c in range(NCH):
            phase_a(b, c)
    for b in range(NB):
        S.barrier_all()
        phase_b(b)
    S.barrier_all()
    for b in range(NB):
        for c in range(NCH):
            phase_c(b, c)
    S.barrier_all()

    S.emit(block)
    st.close()
    return nc


def _prep_inputs(inputs):
    f = np.float32

    def t2(a):
        return np.ascontiguousarray(np.asarray(a).T)

    ii = {}
    ii["wg"] = t2(inputs["L0_ffn_gate_w"])
    ii["win"] = t2(inputs["L0_attn_W_in"])
    ii["binp"] = np.ascontiguousarray(
        np.asarray(inputs["L0_attn_b_in"]).reshape(32, 128).T)
    ii["cw"] = np.ascontiguousarray(
        np.asarray(inputs["L0_attn_conv_w"]).reshape(2048, 4)
        .reshape(16, 128, 4).transpose(1, 0, 2).reshape(128, 64))
    ii["cb"] = np.ascontiguousarray(
        np.asarray(inputs["L0_attn_conv_b"]).reshape(16, 128).T)
    ii["wx"] = t2(inputs["L0_attn_W_x"])
    ii["bx"] = np.ascontiguousarray(np.asarray(inputs["L0_attn_b_x"]).reshape(96, 1))
    ii["bxc"] = np.ascontiguousarray(
        np.asarray(inputs["L0_attn_b_x"])[80:96].reshape(16, 1))
    ii["wdt"] = t2(inputs["L0_attn_W_dt"])
    ii["bdt"] = np.ascontiguousarray(
        np.asarray(inputs["L0_attn_b_dt"]).reshape(16, 128).T)
    ii["alog"] = np.ascontiguousarray(
        np.asarray(inputs["L0_attn_A_log"]).reshape(16, 128, 16)
        .transpose(1, 0, 2).reshape(128, 256))
    ii["dpar"] = np.ascontiguousarray(
        np.asarray(inputs["L0_attn_D_param"]).reshape(16, 128).T)
    ii["wout"] = t2(inputs["L0_attn_W_out"])
    ii["bout"] = np.ascontiguousarray(
        np.asarray(inputs["L0_attn_b_out"]).reshape(8, 128).T)
    for nm, w_, b_ in (("w1", "L0_ffn_w1", "L0_ffn_b1"),
                       ("w2", "L0_ffn_w2", "L0_ffn_b2"),
                       ("w3", "L0_ffn_w3", "L0_ffn_b3")):
        ii[nm] = np.ascontiguousarray(np.asarray(inputs[w_]).transpose(0, 2, 1))
        ii["b" + nm[1]] = np.ascontiguousarray(
            np.asarray(inputs[b_]).T.reshape(8, 128, 8).transpose(1, 0, 2)
            .reshape(128, 64))
    ii["sw1"] = t2(inputs["L0_ffn_sw1"])
    ii["sw2"] = t2(inputs["L0_ffn_sw2"])
    ii["sw3"] = t2(inputs["L0_ffn_sw3"])
    ii["ident"] = np.eye(128, dtype=f)
    ii["onesc"] = np.ones((128, 1), dtype=f)
    ii["onesr"] = np.ones((1, 128), dtype=f)
    ii["constv"] = np.tile(np.array([[1e-5, 1.0, -1.0, -0.5]], dtype=f), (128, 1))
    sel8 = np.zeros((8, 8 * 128), dtype=f)
    for e_ in range(8):
        sel8[e_, e_ * 128:(e_ + 1) * 128] = 1.0
    ii["sel8"] = sel8
    for k in ii:
        assert ii[k].dtype == np.float32, k
    return ii


def kernel(**inputs):
    x = np.asarray(inputs["L0_attn_x"])
    shared = _prep_inputs(inputs)
    nc = build_program()
    in_maps = []
    for core in range(NCORES):
        m = dict(shared)
        m["x"] = np.ascontiguousarray(
            x[core * NB:(core + 1) * NB].transpose(0, 2, 1))
        in_maps.append(m)
    res = run_bass_kernel_spmd(nc, in_maps, core_ids=list(range(NCORES)))
    outs = [np.asarray(res.results[core]["out"]).transpose(0, 2, 1)
            for core in range(NCORES)]
    return np.ascontiguousarray(np.concatenate(outs, axis=0))


if __name__ == "__main__":
    import reference as _r
    ins = {k: np.asarray(v) for k, v in _r.setup_inputs().items()}
    out = kernel(**ins)
    ref = np.asarray(_r.reference(**ins))
    rv = float(np.mean((out - ref) ** 2) / np.mean(ref ** 2))
    print("maxabs", np.abs(out - ref).max(), "relvar", rv)


# version 09
# speedup vs baseline: 1.1553x; 1.1102x over previous
import numpy as np
import concourse.bass as bass
import concourse.mybir as mybir
from concourse.bass_utils import run_bass_kernel_spmd
from contextlib import ExitStack

F32 = mybir.dt.float32
BF16 = mybir.dt.bfloat16
U8 = mybir.dt.uint8
AF = mybir.ActivationFunctionType
ALU = mybir.AluOpType
AX = mybir.AxisListType.X

NCORES = 8
NB = 2
T = 2048
D = 1024
DI = 2048
DS = 16
DTR = 64
NE = 8
ISH = 2048
NT = 512
NCH = T // NT
PC_D = D // 128
PC_DI = DI // 128
TQ = 512
NTQ = T // TQ
EPS = 1e-5

ENGS = ("sync", "tensor", "vector", "scalar", "gpsimd")
ENG_SEM = {"vector": "sV", "scalar": "sA", "gpsimd": "sO", "tensor": "sP"}
ALL_SEMS = tuple(["qOut", "qC", "qC2", "sP", "sV", "sA", "sO",
                  "qG0", "qG1", "qP0", "qP1", "q_x", "q_x1s", "q_bt",
                  "q_ygl", "q_x1l"] +
                 [f"q_sl{i}" for i in range(4)] +
                 [f"{p}{i}" for p in ("q_zt", "q_xcv", "q_ir", "q_yg")
                  for i in range(2)] +
                 [f"{p}{i}" for p in ("q_dts", "q_xcs", "q_szs")
                  for i in range(3)])


class Sched:

    def __init__(self):
        self.q = {e: [] for e in ENGS}
        self.cnt = {s: 0 for s in ALL_SEMS}
        self.lastwait = {e: {} for e in ENGS}
        self.sems = {}

    def _wait_list(self, eng, waits):
        out = []
        for w in waits:
            if w is None:
                continue
            s, c = w
            if c <= 0:
                continue
            assert c <= self.cnt[s], f"wait on future count {s}={c} > {self.cnt[s]}"
            if self.lastwait[eng].get(s, 0) < c:
                self.lastwait[eng][s] = c
                out.append((s, c))
        return out

    def op(self, eng, fn, waits=(), inc=None, incn=1, selfsync=False):
        if selfsync and inc is None:
            inc = ENG_SEM[eng]
        w = self._wait_list(eng, list(waits))
        tok = None
        if inc is not None:
            self.cnt[inc] += incn
            tok = (inc, self.cnt[inc])
        self.q[eng].append((w, fn, tok, incn if inc else 0, selfsync))
        if selfsync:
            self.lastwait[eng][tok[0]] = tok[1]
        return tok

    def dma(self, eng, out_ap, in_ap, waits=(), sem="qOut"):
        def fn(e, out_ap=out_ap, in_ap=in_ap):
            return e.dma_start(out=out_ap, in_=in_ap)
        return self.op(eng, fn, waits=waits, inc=sem, incn=16)

    def barrier_all(self, exclude=()):
        toks = [(s, self.cnt[s]) for s in ALL_SEMS]
        for e in ENGS:
            if e in exclude:
                continue
            w = self._wait_list(e, toks)
            if w:
                self.q[e].append((w, None, None, 0, False))

    def emit(self, block):
        def make_body(eng_name):
            def body(e):
                for (w, fn, tok, incn, selfsync) in self.q[eng_name]:
                    for s, c in w:
                        e.wait_ge(self.sems[s], c)
                    if fn is None:
                        continue
                    instr = fn(e)
                    if tok is not None:
                        instr.then_inc(self.sems[tok[0]], incn)
                    if selfsync:
                        e.wait_ge(self.sems[tok[0]], tok[1])
            return body
        block.sync(make_body("sync"))
        block.tensor(make_body("tensor"))
        block.vector(make_body("vector"))
        block.scalar(make_body("scalar"))
        block.gpsimd(make_body("gpsimd"))


def build_program(debug=False):
    nc = bass.Bass("TRN2", target_bir_lowering=False)
    S = Sched()

    def din(name, shape):
        return nc.dram_tensor(name, shape, F32, kind="ExternalInput").ap()

    x_in_d = din("x", [NB, D, T])
    wg_d = din("wg", [D, NE])
    win_d = din("win", [D, 2 * DI])
    bin_d = din("binp", [128, 32])
    cw_d = din("cw", [128, 64])
    cb_d = din("cb", [128, 16])
    wx_d = din("wx", [DI, 96])
    bx_d = din("bx", [96, 1])
    bxc_d = din("bxc", [16, 1])
    wdt_d = din("wdt", [DTR, DI])
    bdt_d = din("bdt", [128, 16])
    alog_d = din("alog", [128, 256])
    dpar_d = din("dpar", [128, 16])
    wout_d = din("wout", [DI, D])
    bout_d = din("bout", [128, 8])
    w1_d = din("w1", [NE, D, D])
    b1_d = din("b1", [128, 64])
    w2_d = din("w2", [NE, D, D])
    b2_d = din("b2", [128, 64])
    w3_d = din("w3", [NE, D, D])
    b3_d = din("b3", [128, 64])
    sw1_d = din("sw1", [D, ISH])
    sw2_d = din("sw2", [D, ISH])
    sw3_d = din("sw3", [ISH, D])
    ident_d = din("ident", [128, 128])
    onesc_d = din("onesc", [128, 1])
    onesr_d = din("onesr", [1, 128])
    sel8_d = din("sel8", [NE, NE * 128])
    constv_d = din("constv", [128, 4])

    out_d = nc.dram_tensor("out", [NB, D, T], F32, kind="ExternalOutput").ap()

    def scr(name, shape, dt):
        kind = "ExternalOutput" if (debug and name.startswith(("s_x", "s_dbg"))
                                    or debug and name in ("s_sz", "s_dt", "s_bi",
                                                          "s_ci", "s_yg")) else "Internal"
        return nc.dram_tensor(name, shape, dt, kind=kind).ap()

    s_win = scr("s_win", [D, 2 * DI], BF16)
    s_w1 = scr("s_w1", [NE, D, D], BF16)
    s_w2 = scr("s_w2", [NE, D, D], BF16)
    s_w3 = scr("s_w3", [NE, D, D], BF16)
    s_sw1 = scr("s_sw1", [D, ISH], BF16)
    s_sw2 = scr("s_sw2", [D, ISH], BF16)
    s_sw3 = scr("s_sw3", [ISH, D], BF16)
    s_wout = scr("s_wout", [DI, D], BF16)
    s_wx = scr("s_wx", [DI, 96], BF16)
    s_wdt = scr("s_wdt", [DTR, DI], BF16)
    s_x1 = scr("s_x1", [NB, D, T], F32)
    s_xc = scr("s_xc", [NB, DI, T], F32)
    s_sz = scr("s_sz", [NB, DI, T], F32)
    s_dt = scr("s_dt", [NB, DI, T], F32)
    s_bi = scr("s_bi", [NB, DS, T], F32)
    s_ci = scr("s_ci", [NB, DS, T], F32)
    s_yg = scr("s_yg", [NB, DI, T], BF16)

    st = ExitStack()
    e = st.enter_context

    ident_sb = e(nc.sbuf_tensor("ident_sb", [128, 128], F32))
    onesc_sb = e(nc.sbuf_tensor("onesc_sb", [128, 1], F32))
    onesr_sb = e(nc.sbuf_tensor("onesr_sb", [1, 128], F32))
    sel8_sb = e(nc.sbuf_tensor("sel8_sb", [NE, NE * 128], F32))
    bin_sb = e(nc.sbuf_tensor("bin_sb", [128, 32], F32))
    cw_sb = e(nc.sbuf_tensor("cw_sb", [128, 64], F32))
    cb_sb = e(nc.sbuf_tensor("cb_sb", [128, 16], F32))
    bx_sb = e(nc.sbuf_tensor("bx_sb", [96, 1], F32))
    bxc_sb = e(nc.sbuf_tensor("bxc_sb", [16, 1], F32))
    bdt_sb = e(nc.sbuf_tensor("bdt_sb", [128, 16], F32))
    alog_sb = e(nc.sbuf_tensor("alog_sb", [128, 256], F32))
    aneg_sb = e(nc.sbuf_tensor("aneg_sb", [128, 256], F32))
    dpar_sb = e(nc.sbuf_tensor("dpar_sb", [128, 16], F32))
    bout_sb = e(nc.sbuf_tensor("bout_sb", [128, 8], F32))
    b1_sb = e(nc.sbuf_tensor("b1_sb", [128, 64], F32))
    b2_sb = e(nc.sbuf_tensor("b2_sb", [128, 64], F32))
    b3_sb = e(nc.sbuf_tensor("b3_sb", [128, 64], F32))
    ggf_sb = e(nc.sbuf_tensor("ggf_sb", [128, 64], F32))
    ghi_sb = e(nc.sbuf_tensor("ghi_sb", [128, 64], BF16))
    glo_sb = e(nc.sbuf_tensor("glo_sb", [128, 64], BF16))
    wx_sb = e(nc.sbuf_tensor("wx_sb", [128, 16 * 96], BF16))
    wdt_sb = e(nc.sbuf_tensor("wdt_sb", [DTR, DI], BF16))
    halo_sb = e(nc.sbuf_tensor("halo_sb", [128, 48], F32))
    state_sb = e(nc.sbuf_tensor("state_sb", [128, 256], F32))
    mean_r = e(nc.sbuf_tensor("mean_r", [1, NT], F32))
    rstd_r = e(nc.sbuf_tensor("rstd_r", [1, NT], F32))
    srow_r = e(nc.sbuf_tensor("srow_r", [1, NT], F32))
    scores_sb = e(nc.sbuf_tensor("scores_sb", [128, 8], F32))
    m8_sb = e(nc.sbuf_tensor("m8_sb", [128, 8], F32))
    m2_sb = e(nc.sbuf_tensor("m2_sb", [128, 8], F32))
    zap_sb = e(nc.sbuf_tensor("zap_sb", [128, 8], F32))
    wsel_sb = e(nc.sbuf_tensor("wsel_sb", [128, 8], F32))
    nmax_sb = e(nc.sbuf_tensor("nmax_sb", [128, 1], F32))
    ssum_sb = e(nc.sbuf_tensor("ssum_sb", [128, 1], F32))
    btile_sb = e(nc.sbuf_tensor("btile_sb", [128, 16], F32))
    dtlow_bf = e(nc.sbuf_tensor("dtlow_bf", [DTR, NT], BF16))
    constv_sb = e(nc.sbuf_tensor("constv_sb", [128, 4], F32))

    AW = e(nc.sbuf_tensor("AW", [128, 4 * 16384], U8))
    AACT = e(nc.sbuf_tensor("AACT", [128, 118784], U8))

    def aw_slot(i):
        return AW[:, i * 16384:(i + 1) * 16384].bitcast(BF16)

    def av(off_b, nbytes, dt):
        return AACT[:, off_b:off_b + nbytes].bitcast(dt)

    K = 1024
    x_in = av(0, 16 * K, F32)
    x1c_v = av(16 * K, 16 * K, F32)
    xh_bf = av(16 * K, 8 * K, BF16)
    xh_lo = av(24 * K, 8 * K, BF16)
    y_acc = av(32 * K, 16 * K, F32)
    xs_bf = av(48 * K, 8 * K, BF16)
    h_bf = av(56 * K, 8 * K, BF16)
    p_bf = av(64 * K, 16 * K, BF16)
    xf_tmp = av(80 * K, 2 * K, F32)
    g1s_bf = av(80 * K, K, BF16)
    xsq = [av(82 * K, 2 * K, F32), av(84 * K, 2 * K, F32)]
    x2m = av(86 * K, 2 * K, F32)
    tmp_f = av(88 * K, 2 * K, F32)
    wbc_bf = av(90 * K, K, BF16)
    mask_f = av(91 * K, 2 * K, F32)
    sp_a = av(93 * K, 2 * K, F32)
    sp_b = av(95 * K, 2 * K, F32)
    cvb = [av(97 * K, 2072, F32), av(97 * K + 2080, 2072, F32)]
    zt_sb = [av(102 * K, 2 * K, F32), av(104 * K, 2 * K, F32)]
    xcv_sb = [av(106 * K, 2 * K, F32), av(108 * K, 2 * K, F32)]
    dtv_sb = zt_sb
    brow_sb = av(110 * K, 2 * K, F32)[0:16, :].bitcast(F32)
    crow_sb = av(112 * K, 2 * K, F32)[0:16, :].bitcast(F32)
    wT_sb = av(114 * K, 2 * K, F32)[0:8, :].bitcast(F32)
    dA_big = [av(0, 32 * K, F32), av(32 * K, 32 * K, F32)]
    dts_sb = [av(64 * K + 2 * K * j, 2 * K, F32) for j in range(3)]
    xcs_sb = [av(70 * K + 2 * K * j, 2 * K, F32) for j in range(3)]
    szs_sb = [av(76 * K + 2 * K * j, 2 * K, F32) for j in range(3)]
    u_sb2 = [av(82 * K + K * j, K, BF16) for j in range(2)]
    yt_sb = av(84 * K, 2 * K, F32)
    ygb_sb = [av(86 * K, K, BF16), av(87 * K, K, BF16)]
    sfix_sb = av(88 * K, 64, F32)
    irow_sb = [av(89 * K, 2 * K, F32)[0:1, :].bitcast(F32),
               av(91 * K, 2 * K, F32)[0:1, :].bitcast(F32)]
    Bint = aw_slot(0)
    Cint = aw_slot(1)
    vh_bufs = [aw_slot(2), aw_slot(3)]
    stg_sb = [av(0, 8 * K, BF16), av(8 * K, 8 * K, BF16)]

    PS = [e(nc.psum_tensor(f"PS{i}", [128, 512], F32)) for i in range(8)]
    bank_tok = [None] * 8

    for s in ALL_SEMS:
        S.sems[s] = e(nc.semaphore(s))
    block = e(nc.Block())

    def mm_group(out_view, bank, pieces, waits=(), tag=""):
        w = list(waits)
        if bank_tok[bank] is not None:
            w.append(bank_tok[bank])
        tok = None
        n = len(pieces)
        for i, (lh, rh) in enumerate(pieces):
            def fn(t, lh=lh, rh=rh, i=i, n=n, out_view=out_view):
                return t.matmul(out=out_view, lhsT=lh, rhs=rh,
                                start=(i == 0), stop=(i == n - 1))
            if i == n - 1:
                tok = S.op("tensor", fn, waits=w if i == 0 else (), inc="sP")
            else:
                S.op("tensor", fn, waits=w if i == 0 else ())
        return tok

    def act(out, in_, func, bias=None, scale=None, waits=()):
        def fn(s, out=out, in_=in_, func=func, bias=bias, scale=scale):
            kw = {}
            if bias is not None:
                kw["bias"] = bias
            if scale is not None:
                kw["scale"] = scale
            return s.activation(out=out, in_=in_, func=func, **kw)
        return S.op("scalar", fn, waits=waits, inc="sA")

    def vop(fn, waits=(), selfsync=False):
        return S.op("vector", fn, waits=waits, inc="sV", selfsync=selfsync)

    def pop_(fn, waits=()):
        return S.op("gpsimd", fn, waits=waits, inc="sO")

    slot_free = [None] * 4

    def load_slot(i, dram_ap_3d, waits=()):
        w = list(waits)
        if slot_free[i] is not None:
            w.append(slot_free[i])
        kc = dram_ap_3d.shape[1]
        m = dram_ap_3d.shape[2]
        sl = aw_slot(i)
        return S.dma("sync", sl[:, 0:kc * m].rearrange("p (kc m1) -> p kc m1", kc=kc),
                     dram_ap_3d, waits=w, sem=f"q_sl{i}")

    def expert_mat(dram, ei):
        return dram[ei].rearrange("(kc p) m -> p kc m", p=128)

    def win_view(sl):
        return s_win[sl * 256:(sl + 1) * 256].rearrange("(kc p) m -> p kc m", p=128)

    def xi_view(m):
        return AACT[:, 32768 + m * 2048: 32768 + (m + 1) * 2048].bitcast(F32)

    for dst, src in [(ident_sb, ident_d), (onesc_sb, onesc_d), (onesr_sb, onesr_d),
                     (sel8_sb, sel8_d), (bin_sb, bin_d), (cw_sb, cw_d),
                     (cb_sb, cb_d), (bx_sb, bx_d), (bxc_sb, bxc_d),
                     (bdt_sb, bdt_d), (alog_sb, alog_d), (dpar_sb, dpar_d),
                     (bout_sb, bout_d), (b1_sb, b1_d), (b2_sb, b2_d),
                     (b3_sb, b3_d), (constv_sb, constv_d)]:
        S.dma("sync", dst[:, :], src, sem="qC")
    for i_, v_ in enumerate([EPS, 1.0, -1.0, -0.5]):
        nc.const_aps.register(F32, v_, constv_sb[:, i_:i_ + 1])
    S.dma("sync", ggf_sb[:, :], wg_d.rearrange("(c p) e1 -> p c e1", p=128),
          sem="qC")
    tok_consts = ("qC", S.cnt["qC"])
    t_ghi = vop(lambda v: v.tensor_copy(out=ghi_sb[:, :], in_=ggf_sb[:, :]),
                waits=[tok_consts])
    t_glo = vop(lambda v: v.tensor_sub(out=glo_sb[:, :], in0=ggf_sb[:, :],
                                       in1=ghi_sb[:, :]), waits=[t_ghi])
    t_an = act(aneg_sb[:, :], alog_sb[:, :], AF.Exp, waits=[tok_consts])
    t_an = vop(lambda v: v.tensor_scalar(out=aneg_sb[:, :], in0=aneg_sb[:, :],
                                         scalar1=-1.0, scalar2=None, op0=ALU.mult),
               waits=[t_an])

    stg_free = [None, None]
    pctr = [0]

    def prep_one(df, ds_, r0, rows, cols):
        i = pctr[0] % 2
        pctr[0] += 1
        w = []
        if stg_free[i] is not None:
            w.append(stg_free[i])
        tok_c = S.dma("gpsimd", stg_sb[i][0:rows, 0:cols], df[r0:r0 + rows, :],
                      waits=w, sem=f"qG{i}")
        tok_s = S.dma("sync", ds_[r0:r0 + rows, :], stg_sb[i][0:rows, 0:cols],
                      waits=[tok_c], sem=f"qP{i}")
        stg_free[i] = tok_s

    ptok = {}

    def prep_done_tok():
        return [stg_free[0], stg_free[1]]

    for ei in range(NE):
        for nm, (df, ds_) in [("w1", (w1_d, s_w1)), ("w2", (w2_d, s_w2)),
                              ("w3", (w3_d, s_w3))]:
            for kc in range(8):
                prep_one(df[ei], ds_[ei], kc * 128, 128, D)
            ptok[(nm, ei)] = prep_done_tok()
    for nm, (df, ds_) in [("sw1", (sw1_d, s_sw1)), ("sw2", (sw2_d, s_sw2)),
                          ("win", (win_d, s_win)), ("sw3", (sw3_d, s_sw3)),
                          ("wout", (wout_d, s_wout)), ("wx", (wx_d, s_wx)),
                          ("wdt", (wdt_d, s_wdt))]:
        R = df.shape[0]
        r0 = 0
        while r0 < R:
            rows = min(128, R - r0)
            prep_one(df, ds_, r0, rows, df.shape[1])
            r0 += rows
        ptok[nm] = prep_done_tok()
    prep_toks = prep_done_tok()

    t_wx = S.dma("sync", wx_sb[:, :].rearrange("p (kc m1) -> p kc m1", kc=16),
                 s_wx.rearrange("(kc p) m -> p kc m", p=128), waits=ptok["wx"],
                 sem="qC2")
    S.dma("sync", wdt_sb[:, :], s_wdt, waits=ptok["wdt"], sem="qC2")
    t_wx = ("qC2", S.cnt["qC2"])
    t_wdt = t_wx

    def ln_full(src_view, src_toks, want_lo):
        ws = [t for t in src_toks if t is not None]
        w6 = list(ws)
        if bank_tok[6] is not None:
            w6.append(bank_tok[6])
        tok_sum = None
        for pc in range(PC_D):
            xb = src_view[:, pc * NT:(pc + 1) * NT]
            def fn(t, xb=xb, pc=pc):
                return t.matmul(out=PS[6][0:1, :], lhsT=onesc_sb[:, :], rhs=xb,
                                start=(pc == 0), stop=(pc == PC_D - 1))
            if pc == PC_D - 1:
                tok_sum = S.op("tensor", fn, waits=w6 if pc == 0 else (), inc="sP")
            else:
                S.op("tensor", fn, waits=w6 if pc == 0 else ())
        w7 = list(ws)
        if bank_tok[7] is not None:
            w7.append(bank_tok[7])
        sq_free = [None, None]
        tok_sq = None
        for pc in range(PC_D):
            xb = src_view[:, pc * NT:(pc + 1) * NT]
            wq = list(ws)
            if sq_free[pc % 2] is not None:
                wq.append(sq_free[pc % 2])
            t_s = act(xsq[pc % 2][:, :], xb, AF.Square, waits=wq)
            def fn(t, pc=pc):
                return t.matmul(out=PS[7][0:1, :], lhsT=onesc_sb[:, :],
                                rhs=xsq[pc % 2][:, :],
                                start=(pc == 0), stop=(pc == PC_D - 1))
            if pc == PC_D - 1:
                tok_sq = S.op("tensor", fn, waits=(w7 + [t_s]) if pc == 0 else [t_s],
                              inc="sP")
            else:
                tok_sq = S.op("tensor", fn, waits=(w7 + [t_s]) if pc == 0 else [t_s],
                              inc="sP")
            sq_free[pc % 2] = tok_sq
        t1 = vop(lambda v: v.tensor_scalar(out=mean_r[:, :], in0=PS[6][0:1, :],
                                           scalar1=1.0 / D, scalar2=None,
                                           op0=ALU.mult), waits=[tok_sum])
        t2 = vop(lambda v: v.tensor_scalar(out=srow_r[:, :], in0=PS[7][0:1, :],
                                           scalar1=1.0 / D, scalar2=None,
                                           op0=ALU.mult), waits=[tok_sq])
        t3 = vop(lambda v: v.tensor_mul(out=rstd_r[:, :], in0=mean_r[:, :],
                                        in1=mean_r[:, :]), waits=[t1])
        t4 = vop(lambda v: v.tensor_sub(out=srow_r[:, :], in0=srow_r[:, :],
                                        in1=rstd_r[:, :]), waits=[t2, t3])
        t5 = act(rstd_r[:, :], srow_r[:, :], AF.Ln, bias=float(EPS), waits=[t4])
        t6 = act(rstd_r[:, :], rstd_r[:, :], AF.Exp, scale=-0.5, waits=[t5])
        tok_mb = mm_group(PS[6][:, :], 6, [(onesr_sb[:, :], mean_r[:, :])],
                          waits=[t1])
        tok_rb = mm_group(PS[7][:, :], 7, [(onesr_sb[:, :], rstd_r[:, :])],
                          waits=[t6])
        toks_bf = []
        tok_lo = None
        for pc in range(PC_D):
            xb = src_view[:, pc * NT:(pc + 1) * NT]
            ta = vop(lambda v, xb=xb: v.tensor_sub(out=xf_tmp[:, :], in0=xb,
                                                   in1=PS[6][:, :]),
                     waits=[tok_mb])
            tb = vop(lambda v: v.tensor_mul(out=xf_tmp[:, :], in0=xf_tmp[:, :],
                                            in1=PS[7][:, :]), waits=[tok_rb, ta])
            tc = vop(lambda v, pc=pc: v.tensor_copy(
                out=xh_bf[:, pc * NT:(pc + 1) * NT], in_=xf_tmp[:, :]), waits=[tb])
            toks_bf.append(tc)
            if want_lo:
                tok_lo = vop(lambda v, pc=pc: v.tensor_sub(
                    out=xh_lo[:, pc * NT:(pc + 1) * NT], in0=xf_tmp[:, :],
                    in1=xh_bf[:, pc * NT:(pc + 1) * NT]), waits=[tc])
        bank_tok[6] = tok_lo if want_lo else toks_bf[-1]
        bank_tok[7] = bank_tok[6]
        return toks_bf, tok_lo

    def routing(bf_toks, tok_lo):
        tok_wT = None
        for tt in range(NT // 128):
            w = [bf_toks[-1], tok_lo, t_ghi, t_glo]
            if bank_tok[4] is not None:
                w.append(bank_tok[4])
            combos = []
            for pc in range(PC_D):
                lh_hi = xh_bf[:, pc * NT + tt * 128: pc * NT + (tt + 1) * 128]
                lh_lo = xh_lo[:, pc * NT + tt * 128: pc * NT + (tt + 1) * 128]
                combos.append((lh_hi, ghi_sb[:, pc * 8:(pc + 1) * 8]))
                combos.append((lh_hi, glo_sb[:, pc * 8:(pc + 1) * 8]))
                combos.append((lh_lo, ghi_sb[:, pc * 8:(pc + 1) * 8]))
            tokg = None
            for i, (lh, rh) in enumerate(combos):
                def fn(t, lh=lh, rh=rh, i=i, n=len(combos)):
                    return t.matmul(out=PS[4][:, 0:8], lhsT=lh, rhs=rh,
                                    start=(i == 0), stop=(i == n - 1))
                if i == len(combos) - 1:
                    tokg = S.op("tensor", fn, waits=w if i == 0 else (), inc="sP")
                else:
                    S.op("tensor", fn, waits=w if i == 0 else ())
            ta = vop(lambda v: v.reduce_max(out=nmax_sb[:, :], in_=PS[4][:, 0:8],
                                            axis=AX, negate=True),
                     waits=[tokg], selfsync=True)
            tb = act(scores_sb[:, :], PS[4][:, 0:8], AF.Exp, bias=nmax_sb[:, 0:1],
                     waits=[ta])
            bank_tok[4] = tb
            tc = vop(lambda v: v.reduce_sum(out=ssum_sb[:, :], in_=scores_sb[:, :],
                                            axis=AX), waits=[tb], selfsync=True)
            td = vop(lambda v: v.reciprocal(out=ssum_sb[:, :], in_=ssum_sb[:, :]),
                     waits=[tc], selfsync=True)
            te = vop(lambda v: v.tensor_scalar(out=scores_sb[:, :],
                                               in0=scores_sb[:, :],
                                               scalar1=ssum_sb[:, 0:1],
                                               scalar2=None, op0=ALU.mult),
                     waits=[td], selfsync=True)
            tf = vop(lambda v: v.max(out=m8_sb[:, :], in_=scores_sb[:, :]),
                     waits=[te], selfsync=True)
            tg = vop(lambda v: v.memset(m2_sb[:, :], -1.0), selfsync=True)
            th = vop(lambda v: v.tensor_copy(out=m2_sb[:, 0:2], in_=m8_sb[:, 0:2]),
                     waits=[tf, tg], selfsync=True)
            ti = vop(lambda v: v.match_replace(out=zap_sb[:, :],
                                               in_to_replace=m2_sb[:, :],
                                               in_values=scores_sb[:, :],
                                               imm_value=0.0),
                     waits=[th], selfsync=True)
            tj = vop(lambda v: v.tensor_sub(out=wsel_sb[:, :], in0=scores_sb[:, :],
                                            in1=zap_sb[:, :]),
                     waits=[ti], selfsync=True)
            w5 = [tj]
            if bank_tok[5] is not None:
                w5.append(bank_tok[5])
            def ftr(t):
                return t.transpose(out=PS[5][0:8, 0:128], in_=wsel_sb[:, :],
                                   identity=ident_sb[:, :])
            tk = S.op("tensor", ftr, waits=w5, inc="sP")
            tok_wT = vop(lambda v, tt=tt: v.tensor_copy(
                out=wT_sb[:, tt * 128:(tt + 1) * 128], in_=PS[5][0:8, 0:128]),
                waits=[tk])
            bank_tok[5] = tok_wT
        return tok_wT

    def ffn_prefetch():
        mats = []
        for ei in range(NE):
            mats += [("w1", ei), ("w2", ei), ("w3", ei)]
        dram_of = {"w1": s_w1, "w2": s_w2, "w3": s_w3}
        mat_tok = {}
        next_load = [0]

        def pump(n):
            while next_load[0] < len(mats) and next_load[0] < n:
                mi = next_load[0]
                kind, ei = mats[mi]
                sl = mi % 4
                mat_tok[(kind, ei)] = (sl, load_slot(
                    sl, expert_mat(dram_of[kind], ei), waits=ptok[(kind, ei)]))
                next_load[0] += 1
        return mat_tok, pump

    def ffn(bf_toks, tok_wT, mat_tok, pump):
        y_toks = [None] * PC_D
        prev_xs_last = None
        prev_x2_toks = []
        tok_mask = None
        for ei in range(NE):
            w6 = [tok_wT]
            if bank_tok[6] is not None:
                w6.append(bank_tok[6])
            def fbc(t, ei=ei):
                return t.matmul(out=PS[6][:, :],
                                lhsT=sel8_sb[:, ei * 128:(ei + 1) * 128],
                                rhs=wT_sb[:, :], start=True, stop=True)
            tbc = S.op("tensor", fbc, waits=w6, inc="sP")
            tw = act(wbc_bf[:, :], PS[6][:, :], AF.Identity,
                     waits=[tbc, prev_xs_last])
            bank_tok[6] = tw
            tok_mask = act(mask_f[:, :], wbc_bf[:, :], AF.Sign,
                           waits=[tw] + prev_x2_toks)
            xs_toks = []
            for pc in range(PC_D):
                xs_toks.append(vop(lambda v, pc=pc: v.tensor_mul(
                    out=xs_bf[:, pc * NT:(pc + 1) * NT],
                    in0=xh_bf[:, pc * NT:(pc + 1) * NT], in1=wbc_bf[:, :]),
                    waits=[tw, bf_toks[pc]]))
            prev_xs_last = xs_toks[-1]
            sl1, tok_w1 = mat_tok[("w1", ei)]
            h_toks = []
            for m in range(PC_D):
                bank = m % 2
                pieces = [(aw_slot(sl1).rearrange("p (kc m1) -> p kc m1", kc=8)
                           [:, kc, m * 128:(m + 1) * 128],
                           xs_bf[:, kc * NT:(kc + 1) * NT]) for kc in range(8)]
                tokm = mm_group(PS[bank][:, :], bank, pieces,
                                waits=[tok_w1] + xs_toks)
                t_h = act(h_bf[:, m * NT:(m + 1) * NT], PS[bank][:, :], AF.Silu,
                          bias=b1_sb[:, m * 8 + ei: m * 8 + ei + 1], waits=[tokm])
                bank_tok[bank] = t_h
                h_toks.append(t_h)
            sl2, tok_w2 = mat_tok[("w2", ei)]
            sl3, tok_w3 = mat_tok[("w3", ei)]
            prev_x2_toks = []
            for m in range(PC_D):
                b2k = 2 + (m % 2)
                pieces2 = [(aw_slot(sl2).rearrange("p (kc m1) -> p kc m1", kc=8)
                            [:, kc, m * 128:(m + 1) * 128],
                            h_bf[:, kc * NT:(kc + 1) * NT]) for kc in range(8)]
                tok2 = mm_group(PS[b2k][:, :], b2k, pieces2,
                                waits=[tok_w2] + h_toks)
                t_x2 = vop(lambda v, b2k=b2k, m=m, ei=ei: v.scalar_tensor_tensor(
                    out=x2m[:, :], in0=PS[b2k][:, :],
                    scalar=b2_sb[:, m * 8 + ei: m * 8 + ei + 1],
                    in1=mask_f[:, :], op0=ALU.add, op1=ALU.mult),
                    waits=[tok2, tok_mask])
                bank_tok[b2k] = t_x2
                prev_x2_toks.append(t_x2)
                b3k = 4 + (m % 2)
                pieces3 = [(aw_slot(sl3).rearrange("p (kc m1) -> p kc m1", kc=8)
                            [:, kc, m * 128:(m + 1) * 128],
                            xs_bf[:, kc * NT:(kc + 1) * NT]) for kc in range(8)]
                tok3 = mm_group(PS[b3k][:, :], b3k, pieces3,
                                waits=[tok_w3] + xs_toks)
                t_x3 = vop(lambda v, b3k=b3k, m=m, ei=ei: v.scalar_tensor_tensor(
                    out=tmp_f[:, :], in0=PS[b3k][:, :],
                    scalar=b3_sb[:, m * 8 + ei: m * 8 + ei + 1],
                    in1=x2m[:, :], op0=ALU.add, op1=ALU.mult),
                    waits=[tok3, t_x2] + ([y_toks[m - 1]] if m > 0 else
                                          ([y_toks[PC_D - 1]] if y_toks[PC_D - 1] else [])))
                bank_tok[b3k] = t_x3
                if ei == 0:
                    t_y = pop_(lambda g, m=m: g.tensor_copy(
                        out=y_acc[:, m * NT:(m + 1) * NT], in_=tmp_f[:, :]),
                        waits=[t_x3] + reg_free["y_acc"])
                else:
                    t_y = pop_(lambda g, m=m: g.tensor_add(
                        out=y_acc[:, m * NT:(m + 1) * NT],
                        in0=y_acc[:, m * NT:(m + 1) * NT], in1=tmp_f[:, :]),
                        waits=[t_x3, y_toks[m]])
                y_toks[m] = t_y
            done_tok = y_toks[PC_D - 1]
            for (sl, _) in (mat_tok[("w1", ei)], mat_tok[("w2", ei)],
                            mat_tok[("w3", ei)]):
                slot_free[sl] = done_tok
            pump((ei + 1) * 3 + 4)
        def sw_half(dram, half):
            return dram[half * 512:(half + 1) * 512].rearrange(
                "(kc p) m -> p kc m", p=128)
        t_sw1 = [load_slot(0, sw_half(s_sw1, 0), waits=ptok["sw1"]),
                 load_slot(1, sw_half(s_sw1, 1), waits=ptok["sw1"])]
        t_sw2 = [load_slot(2, sw_half(s_sw2, 0), waits=ptok["sw2"]),
                 load_slot(3, sw_half(s_sw2, 1), waits=ptok["sw2"])]
        p_toks = []
        prev_tp = None
        for m in range(PC_DI):
            bank = m % 2
            pieces = []
            for kc in range(8):
                sl = 0 if kc < 4 else 1
                pieces.append((aw_slot(sl).rearrange("p (kc m1) -> p kc m1", kc=4)
                               [:, kc % 4, m * 128:(m + 1) * 128],
                               xh_bf[:, kc * NT:(kc + 1) * NT]))
            tok1 = mm_group(PS[bank][:, :], bank, pieces,
                            waits=t_sw1 + [bf_toks[-1]])
            wg1 = [tok1]
            if prev_tp is not None:
                wg1.append(prev_tp)
            t_g1 = act(g1s_bf[:, :], PS[bank][:, :], AF.Silu, waits=wg1)
            bank_tok[bank] = t_g1
            bank2 = 2 + (m % 2)
            pieces = []
            for kc in range(8):
                sl = 2 if kc < 4 else 3
                pieces.append((aw_slot(sl).rearrange("p (kc m1) -> p kc m1", kc=4)
                               [:, kc % 4, m * 128:(m + 1) * 128],
                               xh_bf[:, kc * NT:(kc + 1) * NT]))
            tok2 = mm_group(PS[bank2][:, :], bank2, pieces,
                            waits=t_sw2 + [bf_toks[-1]])
            t_p = vop(lambda v, m=m, bank2=bank2: v.tensor_mul(
                out=p_bf[:, m * NT:(m + 1) * NT], in0=g1s_bf[:, :],
                in1=PS[bank2][:, :]),
                waits=[tok2, t_g1] + reg_free["p_bf"])
            bank_tok[bank2] = t_p
            prev_tp = t_p
            p_toks.append(t_p)
        slot_free[0] = p_toks[-1]
        slot_free[1] = p_toks[-1]
        t_sw3 = [load_slot(0, s_sw3[0:1024].rearrange("(kc p) m -> p kc m", p=128),
                           waits=ptok["sw3"]),
                 load_slot(1, s_sw3[1024:2048].rearrange("(kc p) m -> p kc m", p=128),
                           waits=ptok["sw3"])]
        for m in range(PC_D):
            bank = 4 + (m % 2)
            pieces = []
            for kc in range(16):
                sl = 0 if kc < 8 else 1
                pieces.append((aw_slot(sl).rearrange("p (kc m1) -> p kc m1", kc=8)
                               [:, kc % 8, m * 128:(m + 1) * 128],
                               p_bf[:, kc * NT:(kc + 1) * NT]))
            tokz = mm_group(PS[bank][:, :], bank, pieces, waits=t_sw3 + p_toks)
            t_y = vop(lambda v, m=m, bank=bank: v.tensor_add(
                out=y_acc[:, m * NT:(m + 1) * NT],
                in0=y_acc[:, m * NT:(m + 1) * NT], in1=PS[bank][:, :]),
                waits=[tokz, y_toks[m]])
            bank_tok[bank] = t_y
            y_toks[m] = t_y
        slot_free[0] = y_toks[-1]
        slot_free[1] = y_toks[-1]
        slot_free[2] = p_toks[-1]
        slot_free[3] = p_toks[-1]
        reg_free["p_bf"] = []
        reg_free["y_acc"] = []
        return y_toks

    reg_free = {"y_acc": [], "p_bf": []}
    halo_toks = [None] * PC_DI
    btile_free = [None]
    xin_free = [[]]
    c_free = [[]]
    dbg_braw = None
    if debug:
        dbg_braw = [scr("s_dbgB", [NB, 16, T], F32), scr("s_dbgC", [NB, 16, T], F32)]

    def mamba_pre(b, c, x1_toks):
        bf_toks, _ = ln_full(x_in[:, :], x1_toks, want_lo=False)
        ln2_bf = bf_toks
        t_win = [load_slot(sl, win_view(sl), waits=ptok["win"]) for sl in range(4)]
        xi_toks = [None] * PC_DI
        zt_free = [None, None]
        sl_of = [0, 0, 1, 1, 2, 2, 3, 3]
        for m in range(32):
            bank = m % 2
            pieces = []
            for kc in range(8):
                sl = sl_of[kc]
                pieces.append((aw_slot(sl).rearrange("p (kc m1) -> p kc m1", kc=2)
                               [:, kc % 2, m * 128:(m + 1) * 128],
                               xh_bf[:, kc * NT:(kc + 1) * NT]))
            tokm = mm_group(PS[bank][:, :], bank, pieces,
                            waits=t_win + [bf_toks[-1]])
            if m < PC_DI:
                t_xi = act(xi_view(m), PS[bank][:, :], AF.Identity,
                           bias=bin_sb[:, m:m + 1], waits=[tokm])
                bank_tok[bank] = t_xi
                xi_toks[m] = t_xi
            else:
                mz = m - PC_DI
                i = mz % 2
                w = [tokm]
                if zt_free[i] is not None:
                    w.append(zt_free[i])
                t_z = act(zt_sb[i][:, :], PS[bank][:, :], AF.Silu,
                          bias=bin_sb[:, m:m + 1], waits=w)
                bank_tok[bank] = t_z
                zt_free[i] = S.dma("sync", s_sz[b, mz * 128:(mz + 1) * 128,
                                               c * NT:(c + 1) * NT],
                                   zt_sb[i][:, :], waits=[t_z], sem=f"q_zt{i}")
        conv_bf_toks = []
        xcv_free = [None, None]
        sp_free = [None, None]
        for mi in range(PC_DI):
            even = (mi % 2 == 0)
            opf = vop
            cb_i = cvb[0] if even else cvb[1]
            sp_i = sp_a if even else sp_b
            t0 = opf(lambda v, mi=mi, cb_i=cb_i: v.tensor_copy(
                out=cb_i[:, 3:3 + NT], in_=xi_view(mi)), waits=[xi_toks[mi]])
            if c == 0:
                t1_ = opf(lambda v, cb_i=cb_i: v.memset(cb_i[:, 0:3], 0.0),
                          waits=[t0])
            else:
                t1_ = opf(lambda v, mi=mi, cb_i=cb_i: v.tensor_copy(
                    out=cb_i[:, 0:3], in_=halo_sb[:, mi * 3:(mi + 1) * 3]),
                    waits=[t0, halo_toks[mi]])
            w2_ = [t1_]
            if sp_free[mi % 2] is not None:
                w2_.append(sp_free[mi % 2])
            t2_ = opf(lambda v, mi=mi, cb_i=cb_i, sp_i=sp_i: v.tensor_scalar(
                out=sp_i[:, :], in0=cb_i[:, 0:NT],
                scalar1=cw_sb[:, mi * 4:mi * 4 + 1], scalar2=None, op0=ALU.mult),
                waits=w2_)
            for j in range(1, 4):
                t2_ = opf(lambda v, mi=mi, j=j, cb_i=cb_i, sp_i=sp_i:
                          v.scalar_tensor_tensor(
                              out=sp_i[:, :], in0=cb_i[:, j:j + NT],
                              scalar=cw_sb[:, mi * 4 + j:mi * 4 + j + 1],
                              in1=sp_i[:, :], op0=ALU.mult, op1=ALU.add),
                          waits=[t2_])
            halo_toks[mi] = opf(lambda v, mi=mi, cb_i=cb_i: v.tensor_copy(
                out=halo_sb[:, mi * 3:(mi + 1) * 3], in_=cb_i[:, NT:NT + 3]),
                waits=[t1_])
            i = mi % 2
            w = [t2_]
            if xcv_free[i] is not None:
                w.extend(xcv_free[i])
            t_cv = act(xcv_sb[i][:, :], sp_i[:, :], AF.Silu,
                       bias=cb_sb[:, mi:mi + 1], waits=w)
            sp_free[mi % 2] = t_cv
            tok_st = S.dma("sync", s_xc[b, mi * 128:(mi + 1) * 128,
                                        c * NT:(c + 1) * NT],
                           xcv_sb[i][:, :], waits=[t_cv], sem=f"q_xcv{i}")
            t_bf = vop(lambda v, mi=mi, i=i: v.tensor_copy(
                out=p_bf[:, mi * NT:(mi + 1) * NT], in_=xcv_sb[i][:, :]),
                waits=[t_cv])
            conv_bf_toks.append(t_bf)
            xcv_free[i] = [tok_st, t_bf]
        grp_defs = [(0, 64, 4, bx_sb[0:64, 0:1], dtlow_bf[:, :]),
                    (64, 80, 5, brow_sb[:, :], None),
                    (80, 96, 6, crow_sb[:, :], None)]
        row_toks = {}
        for (c0, c1, bidx, arg1, arg2) in grp_defs:
            w = conv_bf_toks + [t_wx]
            if bank_tok[bidx] is not None:
                w.append(bank_tok[bidx])
            tokd = None
            for kc in range(PC_DI):
                def fn(t, kc=kc, c0=c0, c1=c1, bidx=bidx):
                    return t.matmul(out=PS[bidx][0:c1 - c0, :],
                                    lhsT=wx_sb[:, :].rearrange(
                                        "p (kc m1) -> p kc m1", kc=16)[:, kc, c0:c1],
                                    rhs=p_bf[:, kc * NT:(kc + 1) * NT],
                                    start=(kc == 0), stop=(kc == PC_DI - 1))
                if kc == PC_DI - 1:
                    tokd = S.op("tensor", fn, waits=w if kc == 0 else (), inc="sP")
                else:
                    S.op("tensor", fn, waits=w if kc == 0 else ())
            if c0 == 0:
                t_e = act(dtlow_bf[:, :], PS[bidx][0:64, :], AF.Identity,
                          bias=bx_sb[0:64, 0:1], waits=[tokd])
            elif c0 == 64:
                t_e = act(brow_sb[:, :], PS[bidx][0:16, :], AF.Identity,
                          bias=bx_sb[64:80, 0:1], waits=[tokd])
            else:
                t_e = act(crow_sb[:, :], PS[bidx][0:16, :], AF.Identity,
                          bias=bxc_sb[0:16, 0:1], waits=[tokd])
            bank_tok[bidx] = t_e
            row_toks[c0] = t_e
        for (rows_sb, dst, base) in ((brow_sb, s_bi, 64), (crow_sb, s_ci, 80)):
            S.dma("sync", dst[b, :, c * NT:(c + 1) * NT], rows_sb[:, :],
                  waits=[row_toks[base]], sem="q_bt")
        dtv_free = list(zt_free)
        prev_td = [None, None]
        for mi in range(PC_DI):
            bank = mi % 2
            w = [row_toks[0], t_wdt]
            if bank_tok[bank] is not None:
                w.append(bank_tok[bank])
            def fdt(t, mi=mi, bank=bank):
                return t.matmul(out=PS[bank][:, :],
                                lhsT=wdt_sb[:, mi * 128:(mi + 1) * 128],
                                rhs=dtlow_bf[:, :], start=True, stop=True)
            tokd = S.op("tensor", fdt, waits=w, inc="sP")
            wa = [tokd]
            if prev_td[0] is not None:
                wa.append(prev_td[0])
            t_a = act(sp_a[:, :], PS[bank][:, :], AF.Abs,
                      bias=bdt_sb[:, mi:mi + 1], waits=wa)
            wr = [t_a]
            if prev_td[1] is not None:
                wr.append(prev_td[1])
            t_r = act(sp_b[:, :], PS[bank][:, :], AF.Relu,
                      bias=bdt_sb[:, mi:mi + 1], waits=wr)
            bank_tok[bank] = t_r
            t_e1 = act(sp_a[:, :], sp_a[:, :], AF.Exp, scale=-1.0, waits=[t_r])
            t_e2 = act(sp_a[:, :], sp_a[:, :], AF.Ln, bias=1.0, waits=[t_e1])
            i = mi % 2
            w2_ = [t_e2]
            if dtv_free[i] is not None:
                w2_.append(dtv_free[i])
            t_d = vop(lambda v, i=i: v.tensor_add(out=dtv_sb[i][:, :],
                                                  in0=sp_a[:, :], in1=sp_b[:, :]),
                      waits=w2_)
            prev_td[0] = t_d
            prev_td[1] = t_d
            dtv_free[i] = S.dma("sync", s_dt[b, mi * 128:(mi + 1) * 128,
                                             c * NT:(c + 1) * NT],
                                dtv_sb[i][:, :], waits=[t_d], sem=f"q_zt{i}")
        return bf_toks


    def phase_a(b, c):
        S.barrier_all(exclude=("sync",))
        mat_tok, pump = ffn_prefetch()
        pump(4)
        wx_ = list(xin_free[0])
        xin_free[0] = []
        tok_x = S.dma("sync", x_in[:, :].rearrange("p (pc t1) -> p pc t1", pc=PC_D),
                      x_in_d[b].rearrange("(pc p) t1 -> p pc t1", p=128)
                      [:, :, c * NT:(c + 1) * NT], sem="q_x", waits=wx_)
        reg_free["y_acc"] = []
        reg_free["p_bf"] = []
        bf_toks, tok_lo = ln_full(x_in[:, :], [tok_x], want_lo=True)
        tok_wT = routing(bf_toks, tok_lo)
        y_toks = ffn(bf_toks, tok_wT, mat_tok, pump)
        x1_toks = []
        for m in range(PC_D):
            t_x1 = vop(lambda v, m=m: v.scalar_tensor_tensor(
                out=x_in[:, m * NT:(m + 1) * NT],
                in0=y_acc[:, m * NT:(m + 1) * NT], scalar=0.5,
                in1=x_in[:, m * NT:(m + 1) * NT],
                op0=ALU.mult, op1=ALU.add), waits=[y_toks[m]])
            x1_toks.append(t_x1)
        t_x1s = S.dma("sync", s_x1[b][:, c * NT:(c + 1) * NT].rearrange(
            "(pc p) t1 -> p pc t1", p=128),
            x_in[:, :].rearrange("p (pc t1) -> p pc t1", pc=PC_D),
            waits=x1_toks, sem="q_x1s")
        ln2_toks = mamba_pre(b, c, x1_toks)
        xin_free[0] = [t_x1s, ln2_toks[-1]]

    def phase_b(b):
        irow_free = [None, None]
        slice_free = [None, None, None]
        yg_free = [None, None]
        vh_free = [None, None]
        u_free = [None, None]
        dA_free_big = [None, None]
        bint_free = [None]
        prev_tyg = [None]
        st_toks = [None] * PC_DI
        for tq in range(NTQ):
            bc_toks = []
            for (src, dst_int, bank) in ((s_bi, Bint, 6), (s_ci, Cint, 7)):
                for s_i in range(DS):
                    i = s_i % 2
                    w = []
                    if irow_free[i] is not None:
                        w.append(irow_free[i])
                    t_r = S.dma("sync", irow_sb[i][:, :],
                                src[b, s_i, tq * TQ:(tq + 1) * TQ],
                                waits=w, sem=f"q_ir{i}")
                    wq = [t_r]
                    if bank_tok[bank] is not None:
                        wq.append(bank_tok[bank])
                    if bint_free[0] is not None:
                        wq.append(bint_free[0])
                    def fb(t, i=i, bank=bank):
                        return t.matmul(out=PS[bank][:, :], lhsT=onesr_sb[:, :],
                                        rhs=irow_sb[i][0:1, :],
                                        start=True, stop=True)
                    tkb = S.op("tensor", fb, waits=wq, inc="sP")
                    t_ev = act(dst_int[:, s_i * TQ:(s_i + 1) * TQ],
                               PS[bank][:, :], AF.Identity, waits=[tkb])
                    bank_tok[bank] = t_ev
                    irow_free[i] = t_ev
                    bc_toks.append(t_ev)
            pre = {}

            def emit_pre(dc):
                i = dc % 3
                w = []
                if slice_free[i] is not None:
                    w.append(slice_free[i])
                t_dt = S.dma("sync", dts_sb[i][:, :],
                             s_dt[b, dc * 128:(dc + 1) * 128, tq * TQ:(tq + 1) * TQ],
                             waits=w, sem=f"q_dts{i}")
                t_xc = S.dma("sync", xcs_sb[i][:, :],
                             s_xc[b, dc * 128:(dc + 1) * 128, tq * TQ:(tq + 1) * TQ],
                             sem=f"q_xcs{i}")
                t_sz = S.dma("sync", szs_sb[i][:, :],
                             s_sz[b, dc * 128:(dc + 1) * 128, tq * TQ:(tq + 1) * TQ],
                             sem=f"q_szs{i}")
                wu = [t_dt, t_xc]
                if u_free[dc % 2] is not None:
                    wu.append(u_free[dc % 2])
                t_u = pop_(lambda g, i=i, dc=dc: g.tensor_mul(
                    out=u_sb2[dc % 2][:, :], in0=dts_sb[i][:, :],
                    in1=xcs_sb[i][:, :]), waits=wu)
                dw = [t_dt, t_an]
                if dA_free_big[dc % 2] is not None:
                    dw.append(dA_free_big[dc % 2])
                t_dA = None
                for s_i in range(DS):
                    t_dA = act(dA_big[dc % 2][:, s_i * TQ:(s_i + 1) * TQ],
                               dts_sb[i][:, :], AF.Exp,
                               scale=aneg_sb[:, dc * DS + s_i: dc * DS + s_i + 1],
                               waits=dw if s_i == 0 else ())
                pre[dc] = (t_dt, t_xc, t_sz, t_u, t_dA)

            def emit_head(dc):
                i = dc % 3
                vh = vh_bufs[dc % 2]
                dAb = dA_big[dc % 2]
                t_dt, t_xc, t_sz, t_u, t_dA = pre.pop(dc)
                wv = [t_u] + bc_toks
                if vh_free[dc % 2] is not None:
                    wv.append(vh_free[dc % 2])
                t_v = vop(lambda v, vh=vh, dc=dc: v.tensor_tensor(
                    out=vh[:, :].rearrange("p (s1 t1) -> p s1 t1", s1=DS),
                    in0=u_sb2[dc % 2][:, :].unsqueeze(1).to_broadcast([128, DS, TQ]),
                    in1=Bint[:, :].rearrange("p (s1 t1) -> p s1 t1", s1=DS),
                    op=ALU.mult), waits=wv)
                u_free[dc % 2] = t_v
                seg0 = dAb[:, :].rearrange("p (s1 t1) -> p s1 t1", t1=TQ)[:, :, 0]
                if tq > 0:
                    t_f1 = vop(lambda v, seg0=seg0, dc=dc: v.tensor_mul(
                        out=sfix_sb[:, 0:16], in0=seg0,
                        in1=state_sb[:, dc * DS:(dc + 1) * DS]),
                        waits=[t_dA, st_toks[dc]])
                    t_f2 = vop(lambda v, vh=vh: v.tensor_add(
                        out=vh[:, :].rearrange("p (s1 t1) -> p s1 t1", t1=TQ)[:, :, 0],
                        in0=vh[:, :].rearrange("p (s1 t1) -> p s1 t1", t1=TQ)[:, :, 0],
                        in1=sfix_sb[:, 0:16]), waits=[t_f1, t_v])
                else:
                    t_f2 = t_v
                t_z0 = vop(lambda v, seg0=seg0: v.tensor_scalar(
                    out=seg0, in0=seg0, scalar1=0.0, scalar2=None, op0=ALU.mult),
                    waits=[t_dA] + ([t_f1] if tq > 0 else []))
                t_sc = vop(lambda v, vh=vh, dAb=dAb: v.tensor_tensor_scan(
                    out=vh[:, :], data0=dAb[:, :], data1=vh[:, :],
                    initial=0.0, op0=ALU.mult, op1=ALU.add),
                    waits=[t_z0, t_f2], selfsync=True)
                dA_free_big[dc % 2] = t_sc
                t_st = vop(lambda v, dc=dc, vh=vh: v.tensor_copy(
                    out=state_sb[:, dc * DS:(dc + 1) * DS],
                    in_=vh[:, :].rearrange("p (s1 t1) -> p s1 t1", t1=TQ)[:, :, TQ - 1]),
                    waits=[t_sc], selfsync=True)
                st_toks[dc] = t_st
                t_hc = pop_(lambda g, vh=vh: g.tensor_mul(out=vh[:, :],
                                                          in0=vh[:, :],
                                                          in1=Cint[:, :]),
                            waits=[t_st])
                head[dc] = (t_hc, t_sz, i)

            def emit_tail(dc):
                t_hc, t_sz, i = head.pop(dc)
                vh = vh_bufs[dc % 2]
                wy = [t_hc]
                if prev_tyg[0] is not None:
                    wy.append(prev_tyg[0])
                t_y = vop(lambda v, vh=vh: v.tensor_reduce(
                    out=yt_sb[:, :],
                    in_=vh[:, :].rearrange("p (s1 t1) -> p t1 s1", s1=DS),
                    axis=AX, op=ALU.add), waits=wy, selfsync=True)
                vh_free[dc % 2] = t_y
                bint_free[0] = t_y
                t_yd = vop(lambda v, i=i, dc=dc: v.scalar_tensor_tensor(
                    out=yt_sb[:, :], in0=xcs_sb[i][:, :],
                    scalar=dpar_sb[:, dc:dc + 1], in1=yt_sb[:, :],
                    op0=ALU.mult, op1=ALU.add), waits=[t_y])
                iy = dc % 2
                wyg = [t_yd, t_sz]
                if yg_free[iy] is not None:
                    wyg.append(yg_free[iy])
                t_yg = pop_(lambda g, iy=iy: g.tensor_mul(
                    out=ygb_sb[iy][:, :], in0=yt_sb[:, :], in1=szs_sb[i][:, :]),
                    waits=wyg)
                prev_tyg[0] = t_yg
                yg_free[iy] = S.dma("gpsimd", s_yg[b, dc * 128:(dc + 1) * 128,
                                                  tq * TQ:(tq + 1) * TQ],
                                    ygb_sb[iy][:, :], waits=[t_yg],
                                    sem=f"q_yg{iy}")
                slice_free[i] = t_yg

            head = {}
            emit_pre(0)
            for dc in range(PC_DI):
                if dc + 1 < PC_DI:
                    emit_pre(dc + 1)
                emit_head(dc)
                if dc > 0:
                    emit_tail(dc - 1)
            emit_tail(PC_DI - 1)

    def phase_c(b, c):
        S.barrier_all(exclude=("sync",))
        yg_v = av(0, 16 * K, BF16)
        x2_v = av(0, 16 * K, F32)
        mat_tok_c, pump_c = ffn_prefetch()
        wc_ = list(c_free[0])
        c_free[0] = []
        tok_yg = S.dma("sync", yg_v[:, :].rearrange("p (kc t1) -> p kc t1", kc=PC_DI),
                       s_yg[b].rearrange("(kc p) t1 -> p kc t1", p=128)
                       [:, :, c * NT:(c + 1) * NT], sem="q_ygl", waits=wc_)
        tok_x1 = S.dma("sync", x1c_v[:, :].rearrange("p (pc t1) -> p pc t1", pc=PC_D),
                       s_x1[b].rearrange("(pc p) t1 -> p pc t1", p=128)
                       [:, :, c * NT:(c + 1) * NT], sem="q_x1l")
        t_wo = [load_slot(0, s_wout[0:1024].rearrange("(kc p) m -> p kc m", p=128),
                          waits=ptok["wout"]),
                load_slot(1, s_wout[1024:2048].rearrange("(kc p) m -> p kc m", p=128),
                          waits=ptok["wout"])]
        mm_toks = []
        for m in range(PC_D):
            pieces = []
            for kc in range(PC_DI):
                sl = 0 if kc < 8 else 1
                pieces.append((aw_slot(sl).rearrange("p (kc m1) -> p kc m1", kc=8)
                               [:, kc % 8, m * 128:(m + 1) * 128],
                               yg_v[:, kc * NT:(kc + 1) * NT]))
            mm_toks.append(mm_group(PS[m][:, :], m, pieces,
                                    waits=t_wo + [tok_yg]))
        x2_toks = []
        for m in range(PC_D):
            t_b = vop(lambda v, m=m: v.scalar_tensor_tensor(
                out=x2_v[:, m * NT:(m + 1) * NT], in0=PS[m][:, :],
                scalar=bout_sb[:, m:m + 1], in1=x1c_v[:, m * NT:(m + 1) * NT],
                op0=ALU.add, op1=ALU.add),
                waits=[mm_toks[m], tok_x1, mm_toks[PC_D - 1]])
            bank_tok[m] = t_b
            x2_toks.append(t_b)
        slot_free[0] = x2_toks[-1]
        slot_free[1] = x2_toks[-1]
        pump_c(4)
        reg_free["y_acc"] = []
        reg_free["p_bf"] = []
        bf_toks, tok_lo = ln_full(x2_v[:, :], x2_toks, want_lo=True)
        tok_wT = routing(bf_toks, tok_lo)
        y_toks = ffn(bf_toks, tok_wT, mat_tok_c, pump_c)
        out_toks = []
        for m in range(PC_D):
            t_o = vop(lambda v, m=m: v.scalar_tensor_tensor(
                out=x2_v[:, m * NT:(m + 1) * NT],
                in0=y_acc[:, m * NT:(m + 1) * NT], scalar=0.5,
                in1=x2_v[:, m * NT:(m + 1) * NT],
                op0=ALU.mult, op1=ALU.add), waits=[y_toks[m]])
            out_toks.append(t_o)
        t_os = S.dma("sync", out_d[b].rearrange("(pc p) t1 -> p pc t1", p=128)
              [:, :, c * NT:(c + 1) * NT],
              x2_v[:, :].rearrange("p (pc t1) -> p pc t1", pc=PC_D),
              waits=out_toks, sem="qOut")
        c_free[0] = [t_os, y_toks[-1]]

    xin_free[0] = [prep_toks[0], prep_toks[1]]
    for b in range(NB):
        for c in range(NCH):
            phase_a(b, c)
    for b in range(NB):
        S.barrier_all()
        phase_b(b)
    S.barrier_all()
    for b in range(NB):
        for c in range(NCH):
            phase_c(b, c)
    S.barrier_all()

    S.emit(block)
    st.close()
    return nc


def _prep_inputs(inputs):
    f = np.float32

    def t2(a):
        return np.ascontiguousarray(np.asarray(a).T)

    ii = {}
    ii["wg"] = t2(inputs["L0_ffn_gate_w"])
    ii["win"] = t2(inputs["L0_attn_W_in"])
    ii["binp"] = np.ascontiguousarray(
        np.asarray(inputs["L0_attn_b_in"]).reshape(32, 128).T)
    ii["cw"] = np.ascontiguousarray(
        np.asarray(inputs["L0_attn_conv_w"]).reshape(2048, 4)
        .reshape(16, 128, 4).transpose(1, 0, 2).reshape(128, 64))
    ii["cb"] = np.ascontiguousarray(
        np.asarray(inputs["L0_attn_conv_b"]).reshape(16, 128).T)
    ii["wx"] = t2(inputs["L0_attn_W_x"])
    ii["bx"] = np.ascontiguousarray(np.asarray(inputs["L0_attn_b_x"]).reshape(96, 1))
    ii["bxc"] = np.ascontiguousarray(
        np.asarray(inputs["L0_attn_b_x"])[80:96].reshape(16, 1))
    ii["wdt"] = t2(inputs["L0_attn_W_dt"])
    ii["bdt"] = np.ascontiguousarray(
        np.asarray(inputs["L0_attn_b_dt"]).reshape(16, 128).T)
    ii["alog"] = np.ascontiguousarray(
        np.asarray(inputs["L0_attn_A_log"]).reshape(16, 128, 16)
        .transpose(1, 0, 2).reshape(128, 256))
    ii["dpar"] = np.ascontiguousarray(
        np.asarray(inputs["L0_attn_D_param"]).reshape(16, 128).T)
    ii["wout"] = t2(inputs["L0_attn_W_out"])
    ii["bout"] = np.ascontiguousarray(
        np.asarray(inputs["L0_attn_b_out"]).reshape(8, 128).T)
    for nm, w_, b_ in (("w1", "L0_ffn_w1", "L0_ffn_b1"),
                       ("w2", "L0_ffn_w2", "L0_ffn_b2"),
                       ("w3", "L0_ffn_w3", "L0_ffn_b3")):
        ii[nm] = np.ascontiguousarray(np.asarray(inputs[w_]).transpose(0, 2, 1))
        ii["b" + nm[1]] = np.ascontiguousarray(
            np.asarray(inputs[b_]).T.reshape(8, 128, 8).transpose(1, 0, 2)
            .reshape(128, 64))
    ii["sw1"] = t2(inputs["L0_ffn_sw1"])
    ii["sw2"] = t2(inputs["L0_ffn_sw2"])
    ii["sw3"] = t2(inputs["L0_ffn_sw3"])
    ii["ident"] = np.eye(128, dtype=f)
    ii["onesc"] = np.ones((128, 1), dtype=f)
    ii["onesr"] = np.ones((1, 128), dtype=f)
    ii["constv"] = np.tile(np.array([[1e-5, 1.0, -1.0, -0.5]], dtype=f), (128, 1))
    sel8 = np.zeros((8, 8 * 128), dtype=f)
    for e_ in range(8):
        sel8[e_, e_ * 128:(e_ + 1) * 128] = 1.0
    ii["sel8"] = sel8
    for k in ii:
        assert ii[k].dtype == np.float32, k
    return ii


def kernel(**inputs):
    x = np.asarray(inputs["L0_attn_x"])
    shared = _prep_inputs(inputs)
    nc = build_program()
    in_maps = []
    for core in range(NCORES):
        m = dict(shared)
        m["x"] = np.ascontiguousarray(
            x[core * NB:(core + 1) * NB].transpose(0, 2, 1))
        in_maps.append(m)
    res = run_bass_kernel_spmd(nc, in_maps, core_ids=list(range(NCORES)))
    outs = [np.asarray(res.results[core]["out"]).transpose(0, 2, 1)
            for core in range(NCORES)]
    return np.ascontiguousarray(np.concatenate(outs, axis=0))


if __name__ == "__main__":
    import reference as _r
    ins = {k: np.asarray(v) for k, v in _r.setup_inputs().items()}
    out = kernel(**ins)
    ref = np.asarray(_r.reference(**ins))
    rv = float(np.mean((out - ref) ** 2) / np.mean(ref ** 2))
    print("maxabs", np.abs(out - ref).max(), "relvar", rv)


# version 11
# speedup vs baseline: 1.1974x; 1.0365x over previous
import numpy as np
import concourse.bass as bass
import concourse.mybir as mybir
from concourse.bass_utils import run_bass_kernel_spmd
from contextlib import ExitStack

F32 = mybir.dt.float32
BF16 = mybir.dt.bfloat16
U8 = mybir.dt.uint8
AF = mybir.ActivationFunctionType
ALU = mybir.AluOpType
AX = mybir.AxisListType.X

NCORES = 8
NB = 2
T = 2048
D = 1024
DI = 2048
DS = 16
DTR = 64
NE = 8
ISH = 2048
NT = 512
NCH = T // NT
PC_D = D // 128
PC_DI = DI // 128
TQ = 512
NTQ = T // TQ
EPS = 1e-5

ENGS = ("sync", "tensor", "vector", "scalar", "gpsimd")
ENG_SEM = {"vector": "sV", "scalar": "sA", "gpsimd": "sO", "tensor": "sP"}
ALL_SEMS = tuple(["qOut", "qC", "qC2", "sP", "sV", "sA", "sO",
                  "qG0", "qG1", "qP0", "qP1", "q_x", "q_x1s", "q_bt",
                  "q_ygl", "q_x1l"] +
                 [f"q_sl{i}" for i in range(4)] +
                 [f"{p}{i}" for p in ("q_zt", "q_xcv", "q_ir", "q_yg")
                  for i in range(2)] +
                 [f"{p}{i}" for p in ("q_dts", "q_xcs", "q_szs")
                  for i in range(3)])


class Sched:

    def __init__(self):
        self.q = {e: [] for e in ENGS}
        self.cnt = {s: 0 for s in ALL_SEMS}
        self.lastwait = {e: {} for e in ENGS}
        self.sems = {}

    def _wait_list(self, eng, waits):
        out = []
        for w in waits:
            if w is None:
                continue
            s, c = w
            if c <= 0:
                continue
            assert c <= self.cnt[s], f"wait on future count {s}={c} > {self.cnt[s]}"
            if self.lastwait[eng].get(s, 0) < c:
                self.lastwait[eng][s] = c
                out.append((s, c))
        return out

    def op(self, eng, fn, waits=(), inc=None, incn=1, selfsync=False):
        if selfsync and inc is None:
            inc = ENG_SEM[eng]
        w = self._wait_list(eng, list(waits))
        tok = None
        if inc is not None:
            self.cnt[inc] += incn
            tok = (inc, self.cnt[inc])
        self.q[eng].append((w, fn, tok, incn if inc else 0, selfsync))
        if selfsync:
            self.lastwait[eng][tok[0]] = tok[1]
        return tok

    def dma(self, eng, out_ap, in_ap, waits=(), sem="qOut"):
        def fn(e, out_ap=out_ap, in_ap=in_ap):
            return e.dma_start(out=out_ap, in_=in_ap)
        return self.op(eng, fn, waits=waits, inc=sem, incn=16)

    def barrier_all(self, exclude=()):
        toks = [(s, self.cnt[s]) for s in ALL_SEMS]
        for e in ENGS:
            if e in exclude:
                continue
            w = self._wait_list(e, toks)
            if w:
                self.q[e].append((w, None, None, 0, False))

    def emit(self, block):
        def make_body(eng_name):
            def body(e):
                for (w, fn, tok, incn, selfsync) in self.q[eng_name]:
                    for s, c in w:
                        e.wait_ge(self.sems[s], c)
                    if fn is None:
                        continue
                    instr = fn(e)
                    if tok is not None:
                        instr.then_inc(self.sems[tok[0]], incn)
                    if selfsync:
                        e.wait_ge(self.sems[tok[0]], tok[1])
            return body
        block.sync(make_body("sync"))
        block.tensor(make_body("tensor"))
        block.vector(make_body("vector"))
        block.scalar(make_body("scalar"))
        block.gpsimd(make_body("gpsimd"))


def build_program(debug=False):
    nc = bass.Bass("TRN2", target_bir_lowering=False)
    S = Sched()

    def din(name, shape):
        return nc.dram_tensor(name, shape, F32, kind="ExternalInput").ap()

    x_in_d = din("x", [NB, D, T])
    wg_d = din("wg", [D, NE])
    win_d = din("win", [D, 2 * DI])
    bin_d = din("binp", [128, 32])
    cw_d = din("cw", [128, 64])
    cb_d = din("cb", [128, 16])
    wx_d = din("wx", [DI, 96])
    bx_d = din("bx", [96, 1])
    bxc_d = din("bxc", [16, 1])
    wdt_d = din("wdt", [DTR, DI])
    bdt_d = din("bdt", [128, 16])
    alog_d = din("alog", [128, 256])
    dpar_d = din("dpar", [128, 16])
    wout_d = din("wout", [DI, D])
    bout_d = din("bout", [128, 8])
    w1_d = din("w1", [NE, D, D])
    b1_d = din("b1", [128, 64])
    w2_d = din("w2", [NE, D, D])
    b2_d = din("b2", [128, 64])
    w3_d = din("w3", [NE, D, D])
    b3_d = din("b3", [128, 64])
    sw1_d = din("sw1", [D, ISH])
    sw2_d = din("sw2", [D, ISH])
    sw3_d = din("sw3", [ISH, D])
    ident_d = din("ident", [128, 128])
    onesc_d = din("onesc", [128, 1])
    onesr_d = din("onesr", [1, 128])
    sel8_d = din("sel8", [NE, NE * 128])
    constv_d = din("constv", [128, 4])

    out_d = nc.dram_tensor("out", [NB, D, T], F32, kind="ExternalOutput").ap()

    def scr(name, shape, dt):
        kind = "ExternalOutput" if (debug and name.startswith(("s_x", "s_dbg"))
                                    or debug and name in ("s_sz", "s_dt", "s_bi",
                                                          "s_ci", "s_yg")) else "Internal"
        return nc.dram_tensor(name, shape, dt, kind=kind).ap()

    s_win = scr("s_win", [D, 2 * DI], BF16)
    s_w1 = scr("s_w1", [NE, D, D], BF16)
    s_w2 = scr("s_w2", [NE, D, D], BF16)
    s_w3 = scr("s_w3", [NE, D, D], BF16)
    s_sw1 = scr("s_sw1", [D, ISH], BF16)
    s_sw2 = scr("s_sw2", [D, ISH], BF16)
    s_sw3 = scr("s_sw3", [ISH, D], BF16)
    s_wout = scr("s_wout", [DI, D], BF16)
    s_wx = scr("s_wx", [DI, 96], BF16)
    s_wdt = scr("s_wdt", [DTR, DI], BF16)
    s_x1 = scr("s_x1", [NB, D, T], F32)
    s_xc = scr("s_xc", [NB, DI, T], F32)
    s_sz = scr("s_sz", [NB, DI, T], F32)
    s_dt = scr("s_dt", [NB, DI, T], F32)
    s_bi = scr("s_bi", [NB, DS, T], F32)
    s_ci = scr("s_ci", [NB, DS, T], F32)
    s_yg = scr("s_yg", [NB, DI, T], BF16)

    st = ExitStack()
    e = st.enter_context

    ident_sb = e(nc.sbuf_tensor("ident_sb", [128, 128], F32))
    onesc_sb = e(nc.sbuf_tensor("onesc_sb", [128, 1], F32))
    onesr_sb = e(nc.sbuf_tensor("onesr_sb", [1, 128], F32))
    sel8_sb = e(nc.sbuf_tensor("sel8_sb", [NE, NE * 128], F32))
    bin_sb = e(nc.sbuf_tensor("bin_sb", [128, 32], F32))
    cw_sb = e(nc.sbuf_tensor("cw_sb", [128, 64], F32))
    cb_sb = e(nc.sbuf_tensor("cb_sb", [128, 16], F32))
    bx_sb = e(nc.sbuf_tensor("bx_sb", [96, 1], F32))
    bxc_sb = e(nc.sbuf_tensor("bxc_sb", [16, 1], F32))
    bdt_sb = e(nc.sbuf_tensor("bdt_sb", [128, 16], F32))
    alog_sb = e(nc.sbuf_tensor("alog_sb", [128, 256], F32))
    aneg_sb = e(nc.sbuf_tensor("aneg_sb", [128, 256], F32))
    dpar_sb = e(nc.sbuf_tensor("dpar_sb", [128, 16], F32))
    bout_sb = e(nc.sbuf_tensor("bout_sb", [128, 8], F32))
    b1_sb = e(nc.sbuf_tensor("b1_sb", [128, 64], F32))
    b2_sb = e(nc.sbuf_tensor("b2_sb", [128, 64], F32))
    b3_sb = e(nc.sbuf_tensor("b3_sb", [128, 64], F32))
    ggf_sb = e(nc.sbuf_tensor("ggf_sb", [128, 64], F32))
    ghi_sb = e(nc.sbuf_tensor("ghi_sb", [128, 64], BF16))
    glo_sb = e(nc.sbuf_tensor("glo_sb", [128, 64], BF16))
    wx_sb = e(nc.sbuf_tensor("wx_sb", [128, 16 * 96], BF16))
    wdt_sb = e(nc.sbuf_tensor("wdt_sb", [DTR, DI], BF16))
    halo_sb = e(nc.sbuf_tensor("halo_sb", [128, 48], F32))
    state_sb = e(nc.sbuf_tensor("state_sb", [128, 256], F32))
    mean_r = e(nc.sbuf_tensor("mean_r", [1, NT], F32))
    rstd_r = e(nc.sbuf_tensor("rstd_r", [1, NT], F32))
    srow_r = e(nc.sbuf_tensor("srow_r", [1, NT], F32))
    scores_sb = e(nc.sbuf_tensor("scores_sb", [128, 8], F32))
    m8_sb = e(nc.sbuf_tensor("m8_sb", [128, 8], F32))
    m2_sb = e(nc.sbuf_tensor("m2_sb", [128, 8], F32))
    zap_sb = e(nc.sbuf_tensor("zap_sb", [128, 8], F32))
    wsel_sb = e(nc.sbuf_tensor("wsel_sb", [128, 8], F32))
    nmax_sb = e(nc.sbuf_tensor("nmax_sb", [128, 1], F32))
    ssum_sb = e(nc.sbuf_tensor("ssum_sb", [128, 1], F32))
    btile_sb = e(nc.sbuf_tensor("btile_sb", [128, 16], F32))
    dtlow_bf = e(nc.sbuf_tensor("dtlow_bf", [DTR, NT], BF16))
    constv_sb = e(nc.sbuf_tensor("constv_sb", [128, 4], F32))

    AW = e(nc.sbuf_tensor("AW", [128, 4 * 16384], U8))
    AACT = e(nc.sbuf_tensor("AACT", [128, 118784], U8))

    def aw_slot(i):
        return AW[:, i * 16384:(i + 1) * 16384].bitcast(BF16)

    def av(off_b, nbytes, dt):
        return AACT[:, off_b:off_b + nbytes].bitcast(dt)

    K = 1024
    x_in = av(0, 16 * K, F32)
    x1c_v = av(16 * K, 16 * K, F32)
    xh_bf = av(16 * K, 8 * K, BF16)
    xh_lo = av(24 * K, 8 * K, BF16)
    y_acc = av(32 * K, 16 * K, F32)
    xs_bf = av(48 * K, 8 * K, BF16)
    h_bf = av(56 * K, 8 * K, BF16)
    p_bf = av(64 * K, 16 * K, BF16)
    xf_tmp = av(80 * K, 2 * K, F32)
    g1s_bf = av(80 * K, K, BF16)
    xsq = [av(82 * K, 2 * K, F32), av(84 * K, 2 * K, F32)]
    x2m = av(86 * K, 2 * K, F32)
    tmp_f = av(88 * K, 2 * K, F32)
    wbc_bufs = [av(90 * K, K, BF16), av(67 * K, K, BF16)]
    mask_bufs = [av(91 * K, 2 * K, F32), av(64 * K, 2 * K, F32)]
    sp_a = av(93 * K, 2 * K, F32)
    sp_b = av(95 * K, 2 * K, F32)
    cvb = [av(97 * K, 2072, F32), av(97 * K + 2080, 2072, F32)]
    zt_sb = [av(102 * K, 2 * K, F32), av(104 * K, 2 * K, F32)]
    xcv_sb = [av(106 * K, 2 * K, F32), av(108 * K, 2 * K, F32)]
    dtv_sb = zt_sb
    brow_sb = av(110 * K, 2 * K, F32)[0:16, :].bitcast(F32)
    crow_sb = av(112 * K, 2 * K, F32)[0:16, :].bitcast(F32)
    wT_sb = av(114 * K, 2 * K, F32)[0:8, :].bitcast(F32)
    dA_big = [av(0, 32 * K, F32), av(32 * K, 32 * K, F32)]
    dts_sb = [av(64 * K + 2 * K * j, 2 * K, F32) for j in range(3)]
    xcs_sb = [av(70 * K + 2 * K * j, 2 * K, F32) for j in range(3)]
    szs_sb = [av(76 * K + 2 * K * j, 2 * K, F32) for j in range(3)]
    u_sb2 = [av(82 * K + K * j, K, BF16) for j in range(2)]
    yt_sb = av(84 * K, 2 * K, F32)
    ygb_sb = [av(86 * K, K, BF16), av(87 * K, K, BF16)]
    sfix_sb = av(88 * K, 64, F32)
    irow_sb = [av(89 * K, 2 * K, F32)[0:1, :].bitcast(F32),
               av(91 * K, 2 * K, F32)[0:1, :].bitcast(F32)]
    Bint = aw_slot(0)
    Cint = aw_slot(1)
    vh_bufs = [aw_slot(2), aw_slot(3)]
    stg_sb = [av(0, 8 * K, BF16), av(8 * K, 8 * K, BF16)]

    PS = [e(nc.psum_tensor(f"PS{i}", [128, 512], F32)) for i in range(8)]
    bank_tok = [None] * 8

    for s in ALL_SEMS:
        S.sems[s] = e(nc.semaphore(s))
    block = e(nc.Block())

    def mm_group(out_view, bank, pieces, waits=(), tag=""):
        w = list(waits)
        if bank_tok[bank] is not None:
            w.append(bank_tok[bank])
        tok = None
        n = len(pieces)
        for i, (lh, rh) in enumerate(pieces):
            def fn(t, lh=lh, rh=rh, i=i, n=n, out_view=out_view):
                return t.matmul(out=out_view, lhsT=lh, rhs=rh,
                                start=(i == 0), stop=(i == n - 1))
            if i == n - 1:
                tok = S.op("tensor", fn, waits=w if i == 0 else (), inc="sP")
            else:
                S.op("tensor", fn, waits=w if i == 0 else ())
        return tok

    def act(out, in_, func, bias=None, scale=None, waits=()):
        def fn(s, out=out, in_=in_, func=func, bias=bias, scale=scale):
            kw = {}
            if bias is not None:
                kw["bias"] = bias
            if scale is not None:
                kw["scale"] = scale
            return s.activation(out=out, in_=in_, func=func, **kw)
        return S.op("scalar", fn, waits=waits, inc="sA")

    def vop(fn, waits=(), selfsync=False):
        return S.op("vector", fn, waits=waits, inc="sV", selfsync=selfsync)

    def pop_(fn, waits=()):
        return S.op("gpsimd", fn, waits=waits, inc="sO")

    slot_free = [None] * 4

    def load_slot(i, dram_ap_3d, waits=()):
        w = list(waits)
        if slot_free[i] is not None:
            w.append(slot_free[i])
        kc = dram_ap_3d.shape[1]
        m = dram_ap_3d.shape[2]
        sl = aw_slot(i)
        return S.dma("sync", sl[:, 0:kc * m].rearrange("p (kc m1) -> p kc m1", kc=kc),
                     dram_ap_3d, waits=w, sem=f"q_sl{i}")

    def expert_mat(dram, ei):
        return dram[ei].rearrange("(kc p) m -> p kc m", p=128)

    def win_view(sl):
        return s_win[sl * 256:(sl + 1) * 256].rearrange("(kc p) m -> p kc m", p=128)

    def xi_view(m):
        return AACT[:, 32768 + m * 2048: 32768 + (m + 1) * 2048].bitcast(F32)

    for dst, src in [(ident_sb, ident_d), (onesc_sb, onesc_d), (onesr_sb, onesr_d),
                     (sel8_sb, sel8_d), (bin_sb, bin_d), (cw_sb, cw_d),
                     (cb_sb, cb_d), (bx_sb, bx_d), (bxc_sb, bxc_d),
                     (bdt_sb, bdt_d), (alog_sb, alog_d), (dpar_sb, dpar_d),
                     (bout_sb, bout_d), (b1_sb, b1_d), (b2_sb, b2_d),
                     (b3_sb, b3_d), (constv_sb, constv_d)]:
        S.dma("sync", dst[:, :], src, sem="qC")
    for i_, v_ in enumerate([EPS, 1.0, -1.0, -0.5]):
        nc.const_aps.register(F32, v_, constv_sb[:, i_:i_ + 1])
    S.dma("sync", ggf_sb[:, :], wg_d.rearrange("(c p) e1 -> p c e1", p=128),
          sem="qC")
    tok_consts = ("qC", S.cnt["qC"])
    t_ghi = vop(lambda v: v.tensor_copy(out=ghi_sb[:, :], in_=ggf_sb[:, :]),
                waits=[tok_consts])
    t_glo = vop(lambda v: v.tensor_sub(out=glo_sb[:, :], in0=ggf_sb[:, :],
                                       in1=ghi_sb[:, :]), waits=[t_ghi])
    t_an = act(aneg_sb[:, :], alog_sb[:, :], AF.Exp, waits=[tok_consts])
    t_an = vop(lambda v: v.tensor_scalar(out=aneg_sb[:, :], in0=aneg_sb[:, :],
                                         scalar1=-1.0, scalar2=None, op0=ALU.mult),
               waits=[t_an])

    stg_free = [None, None]
    pctr = [0]

    def prep_one(df, ds_, r0, rows, cols):
        i = pctr[0] % 2
        pctr[0] += 1
        w = []
        if stg_free[i] is not None:
            w.append(stg_free[i])
        tok_c = S.dma("gpsimd", stg_sb[i][0:rows, 0:cols], df[r0:r0 + rows, :],
                      waits=w, sem=f"qG{i}")
        tok_s = S.dma("sync", ds_[r0:r0 + rows, :], stg_sb[i][0:rows, 0:cols],
                      waits=[tok_c], sem=f"qP{i}")
        stg_free[i] = tok_s

    ptok = {}

    def prep_done_tok():
        return [stg_free[0], stg_free[1]]

    for ei in range(NE):
        for nm, (df, ds_) in [("w1", (w1_d, s_w1)), ("w2", (w2_d, s_w2)),
                              ("w3", (w3_d, s_w3))]:
            for kc in range(8):
                prep_one(df[ei], ds_[ei], kc * 128, 128, D)
            ptok[(nm, ei)] = prep_done_tok()
    for nm, (df, ds_) in [("sw1", (sw1_d, s_sw1)), ("sw2", (sw2_d, s_sw2)),
                          ("win", (win_d, s_win)), ("sw3", (sw3_d, s_sw3)),
                          ("wout", (wout_d, s_wout)), ("wx", (wx_d, s_wx)),
                          ("wdt", (wdt_d, s_wdt))]:
        R = df.shape[0]
        r0 = 0
        while r0 < R:
            rows = min(128, R - r0)
            prep_one(df, ds_, r0, rows, df.shape[1])
            r0 += rows
        ptok[nm] = prep_done_tok()
    prep_toks = prep_done_tok()

    t_wx = S.dma("sync", wx_sb[:, :].rearrange("p (kc m1) -> p kc m1", kc=16),
                 s_wx.rearrange("(kc p) m -> p kc m", p=128), waits=ptok["wx"],
                 sem="qC2")
    S.dma("sync", wdt_sb[:, :], s_wdt, waits=ptok["wdt"], sem="qC2")
    t_wx = ("qC2", S.cnt["qC2"])
    t_wdt = t_wx

    def ln_full(src_view, src_toks, want_lo):
        ws = [t for t in src_toks if t is not None]
        w6 = list(ws)
        if bank_tok[6] is not None:
            w6.append(bank_tok[6])
        tok_sum = None
        for pc in range(PC_D):
            xb = src_view[:, pc * NT:(pc + 1) * NT]
            def fn(t, xb=xb, pc=pc):
                return t.matmul(out=PS[6][0:1, :], lhsT=onesc_sb[:, :], rhs=xb,
                                start=(pc == 0), stop=(pc == PC_D - 1))
            if pc == PC_D - 1:
                tok_sum = S.op("tensor", fn, waits=w6 if pc == 0 else (), inc="sP")
            else:
                S.op("tensor", fn, waits=w6 if pc == 0 else ())
        w7 = list(ws)
        if bank_tok[7] is not None:
            w7.append(bank_tok[7])
        sq_free = [None, None]
        tok_sq = None
        for pc in range(PC_D):
            xb = src_view[:, pc * NT:(pc + 1) * NT]
            wq = list(ws)
            if sq_free[pc % 2] is not None:
                wq.append(sq_free[pc % 2])
            t_s = act(xsq[pc % 2][:, :], xb, AF.Square, waits=wq)
            def fn(t, pc=pc):
                return t.matmul(out=PS[7][0:1, :], lhsT=onesc_sb[:, :],
                                rhs=xsq[pc % 2][:, :],
                                start=(pc == 0), stop=(pc == PC_D - 1))
            if pc == PC_D - 1:
                tok_sq = S.op("tensor", fn, waits=(w7 + [t_s]) if pc == 0 else [t_s],
                              inc="sP")
            else:
                tok_sq = S.op("tensor", fn, waits=(w7 + [t_s]) if pc == 0 else [t_s],
                              inc="sP")
            sq_free[pc % 2] = tok_sq
        t1 = vop(lambda v: v.tensor_scalar(out=mean_r[:, :], in0=PS[6][0:1, :],
                                           scalar1=1.0 / D, scalar2=None,
                                           op0=ALU.mult), waits=[tok_sum])
        t2 = vop(lambda v: v.tensor_scalar(out=srow_r[:, :], in0=PS[7][0:1, :],
                                           scalar1=1.0 / D, scalar2=None,
                                           op0=ALU.mult), waits=[tok_sq])
        t3 = vop(lambda v: v.tensor_mul(out=rstd_r[:, :], in0=mean_r[:, :],
                                        in1=mean_r[:, :]), waits=[t1])
        t4 = vop(lambda v: v.tensor_sub(out=srow_r[:, :], in0=srow_r[:, :],
                                        in1=rstd_r[:, :]), waits=[t2, t3])
        t5 = act(rstd_r[:, :], srow_r[:, :], AF.Ln, bias=float(EPS), waits=[t4])
        t6 = act(rstd_r[:, :], rstd_r[:, :], AF.Exp, scale=-0.5, waits=[t5])
        tok_mb = mm_group(PS[6][:, :], 6, [(onesr_sb[:, :], mean_r[:, :])],
                          waits=[t1])
        tok_rb = mm_group(PS[7][:, :], 7, [(onesr_sb[:, :], rstd_r[:, :])],
                          waits=[t6])
        toks_bf = []
        tok_lo = None
        for pc in range(PC_D):
            xb = src_view[:, pc * NT:(pc + 1) * NT]
            ta = vop(lambda v, xb=xb: v.tensor_sub(out=xf_tmp[:, :], in0=xb,
                                                   in1=PS[6][:, :]),
                     waits=[tok_mb])
            tb = vop(lambda v: v.tensor_mul(out=xf_tmp[:, :], in0=xf_tmp[:, :],
                                            in1=PS[7][:, :]), waits=[tok_rb, ta])
            tc = vop(lambda v, pc=pc: v.tensor_copy(
                out=xh_bf[:, pc * NT:(pc + 1) * NT], in_=xf_tmp[:, :]), waits=[tb])
            toks_bf.append(tc)
            if want_lo:
                tok_lo = vop(lambda v, pc=pc: v.tensor_sub(
                    out=xh_lo[:, pc * NT:(pc + 1) * NT], in0=xf_tmp[:, :],
                    in1=xh_bf[:, pc * NT:(pc + 1) * NT]), waits=[tc])
        bank_tok[6] = tok_lo if want_lo else toks_bf[-1]
        bank_tok[7] = bank_tok[6]
        return toks_bf, tok_lo

    def routing(bf_toks, tok_lo):
        tok_wT = None
        for tt in range(NT // 128):
            w = [bf_toks[-1], tok_lo, t_ghi, t_glo]
            if bank_tok[4] is not None:
                w.append(bank_tok[4])
            combos = []
            for pc in range(PC_D):
                lh_hi = xh_bf[:, pc * NT + tt * 128: pc * NT + (tt + 1) * 128]
                lh_lo = xh_lo[:, pc * NT + tt * 128: pc * NT + (tt + 1) * 128]
                combos.append((lh_hi, ghi_sb[:, pc * 8:(pc + 1) * 8]))
                combos.append((lh_hi, glo_sb[:, pc * 8:(pc + 1) * 8]))
                combos.append((lh_lo, ghi_sb[:, pc * 8:(pc + 1) * 8]))
            tokg = None
            for i, (lh, rh) in enumerate(combos):
                def fn(t, lh=lh, rh=rh, i=i, n=len(combos)):
                    return t.matmul(out=PS[4][:, 0:8], lhsT=lh, rhs=rh,
                                    start=(i == 0), stop=(i == n - 1))
                if i == len(combos) - 1:
                    tokg = S.op("tensor", fn, waits=w if i == 0 else (), inc="sP")
                else:
                    S.op("tensor", fn, waits=w if i == 0 else ())
            ta = vop(lambda v: v.reduce_max(out=nmax_sb[:, :], in_=PS[4][:, 0:8],
                                            axis=AX, negate=True),
                     waits=[tokg], selfsync=True)
            tb = act(scores_sb[:, :], PS[4][:, 0:8], AF.Exp, bias=nmax_sb[:, 0:1],
                     waits=[ta])
            bank_tok[4] = tb
            tc = vop(lambda v: v.reduce_sum(out=ssum_sb[:, :], in_=scores_sb[:, :],
                                            axis=AX), waits=[tb], selfsync=True)
            td = vop(lambda v: v.reciprocal(out=ssum_sb[:, :], in_=ssum_sb[:, :]),
                     waits=[tc], selfsync=True)
            te = vop(lambda v: v.tensor_scalar(out=scores_sb[:, :],
                                               in0=scores_sb[:, :],
                                               scalar1=ssum_sb[:, 0:1],
                                               scalar2=None, op0=ALU.mult),
                     waits=[td], selfsync=True)
            tf = vop(lambda v: v.max(out=m8_sb[:, :], in_=scores_sb[:, :]),
                     waits=[te], selfsync=True)
            tg = vop(lambda v: v.memset(m2_sb[:, :], -1.0), selfsync=True)
            th = vop(lambda v: v.tensor_copy(out=m2_sb[:, 0:2], in_=m8_sb[:, 0:2]),
                     waits=[tf, tg], selfsync=True)
            ti = vop(lambda v: v.match_replace(out=zap_sb[:, :],
                                               in_to_replace=m2_sb[:, :],
                                               in_values=scores_sb[:, :],
                                               imm_value=0.0),
                     waits=[th], selfsync=True)
            tj = vop(lambda v: v.tensor_sub(out=wsel_sb[:, :], in0=scores_sb[:, :],
                                            in1=zap_sb[:, :]),
                     waits=[ti], selfsync=True)
            w5 = [tj]
            if bank_tok[5] is not None:
                w5.append(bank_tok[5])
            def ftr(t):
                return t.transpose(out=PS[5][0:8, 0:128], in_=wsel_sb[:, :],
                                   identity=ident_sb[:, :])
            tk = S.op("tensor", ftr, waits=w5, inc="sP")
            tok_wT = vop(lambda v, tt=tt: v.tensor_copy(
                out=wT_sb[:, tt * 128:(tt + 1) * 128], in_=PS[5][0:8, 0:128]),
                waits=[tk])
            bank_tok[5] = tok_wT
        return tok_wT

    def ffn_prefetch():
        mats = []
        for ei in range(NE):
            mats += [("w1", ei), ("w2", ei), ("w3", ei)]
        dram_of = {"w1": s_w1, "w2": s_w2, "w3": s_w3}
        mat_tok = {}
        next_load = [0]

        def pump(n):
            while next_load[0] < len(mats) and next_load[0] < n:
                mi = next_load[0]
                kind, ei = mats[mi]
                sl = mi % 4
                mat_tok[(kind, ei)] = (sl, load_slot(
                    sl, expert_mat(dram_of[kind], ei), waits=ptok[(kind, ei)]))
                next_load[0] += 1
        return mat_tok, pump

    def ffn(bf_toks, tok_wT, mat_tok, pump):
        y_toks = [None] * PC_D
        xs_bufs = [xs_bf, xh_lo]
        xs_cons = [[], []]
        wbc_cons = [[], []]
        mask_cons = [[], []]
        pre_e = {}

        def expert_preamble(ei):
            par = ei % 2
            w6 = [tok_wT]
            if bank_tok[6] is not None:
                w6.append(bank_tok[6])
            def fbc(t, ei=ei):
                return t.matmul(out=PS[6][:, :],
                                lhsT=sel8_sb[:, ei * 128:(ei + 1) * 128],
                                rhs=wT_sb[:, :], start=True, stop=True)
            tbc = S.op("tensor", fbc, waits=w6, inc="sP")
            tw = act(wbc_bufs[par][:, :], PS[6][:, :], AF.Identity,
                     waits=[tbc] + wbc_cons[par])
            bank_tok[6] = tw
            tm = act(mask_bufs[par][:, :], wbc_bufs[par][:, :], AF.Sign,
                     waits=[tw] + mask_cons[par])
            xs_toks = []
            for pc in range(PC_D):
                xs_toks.append(vop(lambda v, pc=pc, par=par: v.tensor_mul(
                    out=xs_bufs[par][:, pc * NT:(pc + 1) * NT],
                    in0=xh_bf[:, pc * NT:(pc + 1) * NT],
                    in1=wbc_bufs[par][:, :]),
                    waits=[tw, bf_toks[pc]] + xs_cons[par]))
            pre_e[ei] = (par, tm, xs_toks)

        expert_preamble(0)
        for ei in range(NE):
            par, tok_mask, xs_toks = pre_e.pop(ei)
            xs_v = xs_bufs[par]
            mask_v = mask_bufs[par]
            sl1, tok_w1 = mat_tok[("w1", ei)]
            h_toks = []
            for m in range(PC_D):
                bank = m % 2
                pieces = [(aw_slot(sl1).rearrange("p (kc m1) -> p kc m1", kc=8)
                           [:, kc, m * 128:(m + 1) * 128],
                           xs_v[:, kc * NT:(kc + 1) * NT]) for kc in range(8)]
                tokm = mm_group(PS[bank][:, :], bank, pieces,
                                waits=[tok_w1] + xs_toks)
                t_h = act(h_bf[:, m * NT:(m + 1) * NT], PS[bank][:, :], AF.Silu,
                          bias=b1_sb[:, m * 8 + ei: m * 8 + ei + 1], waits=[tokm])
                bank_tok[bank] = t_h
                h_toks.append(t_h)
            if ei + 1 < NE:
                expert_preamble(ei + 1)
            sl2, tok_w2 = mat_tok[("w2", ei)]
            sl3, tok_w3 = mat_tok[("w3", ei)]
            x2m_toks = []
            tok3_last = None
            for m in range(PC_D):
                b2k = 2 + (m % 2)
                pieces2 = [(aw_slot(sl2).rearrange("p (kc m1) -> p kc m1", kc=8)
                            [:, kc, m * 128:(m + 1) * 128],
                            h_bf[:, kc * NT:(kc + 1) * NT]) for kc in range(8)]
                tok2 = mm_group(PS[b2k][:, :], b2k, pieces2,
                                waits=[tok_w2] + h_toks)
                t_x2 = vop(lambda v, b2k=b2k, m=m, ei=ei, mask_v=mask_v:
                           v.scalar_tensor_tensor(
                    out=x2m[:, :], in0=PS[b2k][:, :],
                    scalar=b2_sb[:, m * 8 + ei: m * 8 + ei + 1],
                    in1=mask_v[:, :], op0=ALU.add, op1=ALU.mult),
                    waits=[tok2, tok_mask])
                bank_tok[b2k] = t_x2
                x2m_toks.append(t_x2)
                b3k = 4 + (m % 2)
                pieces3 = [(aw_slot(sl3).rearrange("p (kc m1) -> p kc m1", kc=8)
                            [:, kc, m * 128:(m + 1) * 128],
                            xs_v[:, kc * NT:(kc + 1) * NT]) for kc in range(8)]
                tok3 = mm_group(PS[b3k][:, :], b3k, pieces3,
                                waits=[tok_w3] + xs_toks)
                tok3_last = tok3
                t_x3 = vop(lambda v, b3k=b3k, m=m, ei=ei: v.scalar_tensor_tensor(
                    out=tmp_f[:, :], in0=PS[b3k][:, :],
                    scalar=b3_sb[:, m * 8 + ei: m * 8 + ei + 1],
                    in1=x2m[:, :], op0=ALU.add, op1=ALU.mult),
                    waits=[tok3, t_x2] + ([y_toks[m - 1]] if m > 0 else
                                          ([y_toks[PC_D - 1]] if y_toks[PC_D - 1] else [])))
                bank_tok[b3k] = t_x3
                if ei == 0:
                    t_y = pop_(lambda g, m=m: g.tensor_copy(
                        out=y_acc[:, m * NT:(m + 1) * NT], in_=tmp_f[:, :]),
                        waits=[t_x3] + reg_free["y_acc"])
                else:
                    t_y = pop_(lambda g, m=m: g.tensor_add(
                        out=y_acc[:, m * NT:(m + 1) * NT],
                        in0=y_acc[:, m * NT:(m + 1) * NT], in1=tmp_f[:, :]),
                        waits=[t_x3, y_toks[m]])
                y_toks[m] = t_y
            xs_cons[par] = [tok3_last]
            wbc_cons[par] = [xs_toks[-1], tok_mask]
            mask_cons[par] = [x2m_toks[-1]]
            done_tok = y_toks[PC_D - 1]
            for (sl, _) in (mat_tok[("w1", ei)], mat_tok[("w2", ei)],
                            mat_tok[("w3", ei)]):
                slot_free[sl] = done_tok
            pump((ei + 1) * 3 + 4)
        def sw_half(dram, half):
            return dram[half * 512:(half + 1) * 512].rearrange(
                "(kc p) m -> p kc m", p=128)
        t_sw1 = [load_slot(0, sw_half(s_sw1, 0), waits=ptok["sw1"]),
                 load_slot(1, sw_half(s_sw1, 1), waits=ptok["sw1"])]
        t_sw2 = [load_slot(2, sw_half(s_sw2, 0), waits=ptok["sw2"]),
                 load_slot(3, sw_half(s_sw2, 1), waits=ptok["sw2"])]
        p_toks = []
        prev_tp = None
        for m in range(PC_DI):
            bank = m % 2
            pieces = []
            for kc in range(8):
                sl = 0 if kc < 4 else 1
                pieces.append((aw_slot(sl).rearrange("p (kc m1) -> p kc m1", kc=4)
                               [:, kc % 4, m * 128:(m + 1) * 128],
                               xh_bf[:, kc * NT:(kc + 1) * NT]))
            tok1 = mm_group(PS[bank][:, :], bank, pieces,
                            waits=t_sw1 + [bf_toks[-1]])
            wg1 = [tok1]
            if prev_tp is not None:
                wg1.append(prev_tp)
            t_g1 = act(g1s_bf[:, :], PS[bank][:, :], AF.Silu, waits=wg1)
            bank_tok[bank] = t_g1
            bank2 = 2 + (m % 2)
            pieces = []
            for kc in range(8):
                sl = 2 if kc < 4 else 3
                pieces.append((aw_slot(sl).rearrange("p (kc m1) -> p kc m1", kc=4)
                               [:, kc % 4, m * 128:(m + 1) * 128],
                               xh_bf[:, kc * NT:(kc + 1) * NT]))
            tok2 = mm_group(PS[bank2][:, :], bank2, pieces,
                            waits=t_sw2 + [bf_toks[-1]])
            t_p = vop(lambda v, m=m, bank2=bank2: v.tensor_mul(
                out=p_bf[:, m * NT:(m + 1) * NT], in0=g1s_bf[:, :],
                in1=PS[bank2][:, :]),
                waits=[tok2, t_g1] + reg_free["p_bf"])
            bank_tok[bank2] = t_p
            prev_tp = t_p
            p_toks.append(t_p)
        slot_free[0] = p_toks[-1]
        slot_free[1] = p_toks[-1]
        t_sw3 = [load_slot(0, s_sw3[0:1024].rearrange("(kc p) m -> p kc m", p=128),
                           waits=ptok["sw3"]),
                 load_slot(1, s_sw3[1024:2048].rearrange("(kc p) m -> p kc m", p=128),
                           waits=ptok["sw3"])]
        for m in range(PC_D):
            bank = 4 + (m % 2)
            pieces = []
            for kc in range(16):
                sl = 0 if kc < 8 else 1
                pieces.append((aw_slot(sl).rearrange("p (kc m1) -> p kc m1", kc=8)
                               [:, kc % 8, m * 128:(m + 1) * 128],
                               p_bf[:, kc * NT:(kc + 1) * NT]))
            tokz = mm_group(PS[bank][:, :], bank, pieces, waits=t_sw3 + p_toks)
            t_y = vop(lambda v, m=m, bank=bank: v.tensor_add(
                out=y_acc[:, m * NT:(m + 1) * NT],
                in0=y_acc[:, m * NT:(m + 1) * NT], in1=PS[bank][:, :]),
                waits=[tokz, y_toks[m]])
            bank_tok[bank] = t_y
            y_toks[m] = t_y
        slot_free[0] = y_toks[-1]
        slot_free[1] = y_toks[-1]
        slot_free[2] = p_toks[-1]
        slot_free[3] = p_toks[-1]
        reg_free["p_bf"] = []
        reg_free["y_acc"] = []
        return y_toks

    reg_free = {"y_acc": [], "p_bf": []}
    halo_toks = [None] * PC_DI
    btile_free = [None]
    xin_free = [[]]
    c_free = [[]]
    dbg_braw = None
    if debug:
        dbg_braw = [scr("s_dbgB", [NB, 16, T], F32), scr("s_dbgC", [NB, 16, T], F32)]

    def mamba_pre(b, c, x1_toks):
        bf_toks, _ = ln_full(x_in[:, :], x1_toks, want_lo=False)
        ln2_bf = bf_toks
        t_win = [load_slot(sl, win_view(sl), waits=ptok["win"]) for sl in range(4)]
        xi_toks = [None] * PC_DI
        zt_free = [None, None]
        sl_of = [0, 0, 1, 1, 2, 2, 3, 3]
        for m in range(32):
            bank = m % 2
            pieces = []
            for kc in range(8):
                sl = sl_of[kc]
                pieces.append((aw_slot(sl).rearrange("p (kc m1) -> p kc m1", kc=2)
                               [:, kc % 2, m * 128:(m + 1) * 128],
                               xh_bf[:, kc * NT:(kc + 1) * NT]))
            tokm = mm_group(PS[bank][:, :], bank, pieces,
                            waits=t_win + [bf_toks[-1]])
            if m < PC_DI:
                t_xi = act(xi_view(m), PS[bank][:, :], AF.Identity,
                           bias=bin_sb[:, m:m + 1], waits=[tokm])
                bank_tok[bank] = t_xi
                xi_toks[m] = t_xi
            else:
                mz = m - PC_DI
                i = mz % 2
                w = [tokm]
                if zt_free[i] is not None:
                    w.append(zt_free[i])
                t_z = act(zt_sb[i][:, :], PS[bank][:, :], AF.Silu,
                          bias=bin_sb[:, m:m + 1], waits=w)
                bank_tok[bank] = t_z
                zt_free[i] = S.dma("sync", s_sz[b, mz * 128:(mz + 1) * 128,
                                               c * NT:(c + 1) * NT],
                                   zt_sb[i][:, :], waits=[t_z], sem=f"q_zt{i}")
        conv_bf_toks = []
        xcv_free = [None, None]
        sp_free = [None, None]
        for mi in range(PC_DI):
            even = (mi % 2 == 0)
            opf = vop
            cb_i = cvb[0] if even else cvb[1]
            sp_i = sp_a if even else sp_b
            t0 = opf(lambda v, mi=mi, cb_i=cb_i: v.tensor_copy(
                out=cb_i[:, 3:3 + NT], in_=xi_view(mi)), waits=[xi_toks[mi]])
            if c == 0:
                t1_ = opf(lambda v, cb_i=cb_i: v.memset(cb_i[:, 0:3], 0.0),
                          waits=[t0])
            else:
                t1_ = opf(lambda v, mi=mi, cb_i=cb_i: v.tensor_copy(
                    out=cb_i[:, 0:3], in_=halo_sb[:, mi * 3:(mi + 1) * 3]),
                    waits=[t0, halo_toks[mi]])
            w2_ = [t1_]
            if sp_free[mi % 2] is not None:
                w2_.append(sp_free[mi % 2])
            t2_ = opf(lambda v, mi=mi, cb_i=cb_i, sp_i=sp_i: v.tensor_scalar(
                out=sp_i[:, :], in0=cb_i[:, 0:NT],
                scalar1=cw_sb[:, mi * 4:mi * 4 + 1], scalar2=None, op0=ALU.mult),
                waits=w2_)
            for j in range(1, 4):
                t2_ = opf(lambda v, mi=mi, j=j, cb_i=cb_i, sp_i=sp_i:
                          v.scalar_tensor_tensor(
                              out=sp_i[:, :], in0=cb_i[:, j:j + NT],
                              scalar=cw_sb[:, mi * 4 + j:mi * 4 + j + 1],
                              in1=sp_i[:, :], op0=ALU.mult, op1=ALU.add),
                          waits=[t2_])
            halo_toks[mi] = opf(lambda v, mi=mi, cb_i=cb_i: v.tensor_copy(
                out=halo_sb[:, mi * 3:(mi + 1) * 3], in_=cb_i[:, NT:NT + 3]),
                waits=[t1_])
            i = mi % 2
            w = [t2_]
            if xcv_free[i] is not None:
                w.extend(xcv_free[i])
            t_cv = act(xcv_sb[i][:, :], sp_i[:, :], AF.Silu,
                       bias=cb_sb[:, mi:mi + 1], waits=w)
            sp_free[mi % 2] = t_cv
            tok_st = S.dma("sync", s_xc[b, mi * 128:(mi + 1) * 128,
                                        c * NT:(c + 1) * NT],
                           xcv_sb[i][:, :], waits=[t_cv], sem=f"q_xcv{i}")
            t_bf = vop(lambda v, mi=mi, i=i: v.tensor_copy(
                out=p_bf[:, mi * NT:(mi + 1) * NT], in_=xcv_sb[i][:, :]),
                waits=[t_cv])
            conv_bf_toks.append(t_bf)
            xcv_free[i] = [tok_st, t_bf]
        grp_defs = [(0, 64, 4, bx_sb[0:64, 0:1], dtlow_bf[:, :]),
                    (64, 80, 5, brow_sb[:, :], None),
                    (80, 96, 6, crow_sb[:, :], None)]
        row_toks = {}
        for (c0, c1, bidx, arg1, arg2) in grp_defs:
            w = conv_bf_toks + [t_wx]
            if bank_tok[bidx] is not None:
                w.append(bank_tok[bidx])
            tokd = None
            for kc in range(PC_DI):
                def fn(t, kc=kc, c0=c0, c1=c1, bidx=bidx):
                    return t.matmul(out=PS[bidx][0:c1 - c0, :],
                                    lhsT=wx_sb[:, :].rearrange(
                                        "p (kc m1) -> p kc m1", kc=16)[:, kc, c0:c1],
                                    rhs=p_bf[:, kc * NT:(kc + 1) * NT],
                                    start=(kc == 0), stop=(kc == PC_DI - 1))
                if kc == PC_DI - 1:
                    tokd = S.op("tensor", fn, waits=w if kc == 0 else (), inc="sP")
                else:
                    S.op("tensor", fn, waits=w if kc == 0 else ())
            if c0 == 0:
                t_e = act(dtlow_bf[:, :], PS[bidx][0:64, :], AF.Identity,
                          bias=bx_sb[0:64, 0:1], waits=[tokd])
            elif c0 == 64:
                t_e = act(brow_sb[:, :], PS[bidx][0:16, :], AF.Identity,
                          bias=bx_sb[64:80, 0:1], waits=[tokd])
            else:
                t_e = act(crow_sb[:, :], PS[bidx][0:16, :], AF.Identity,
                          bias=bxc_sb[0:16, 0:1], waits=[tokd])
            bank_tok[bidx] = t_e
            row_toks[c0] = t_e
        for (rows_sb, dst, base) in ((brow_sb, s_bi, 64), (crow_sb, s_ci, 80)):
            S.dma("sync", dst[b, :, c * NT:(c + 1) * NT], rows_sb[:, :],
                  waits=[row_toks[base]], sem="q_bt")
        dtv_free = list(zt_free)
        prev_td = [None, None]
        for mi in range(PC_DI):
            bank = mi % 2
            w = [row_toks[0], t_wdt]
            if bank_tok[bank] is not None:
                w.append(bank_tok[bank])
            def fdt(t, mi=mi, bank=bank):
                return t.matmul(out=PS[bank][:, :],
                                lhsT=wdt_sb[:, mi * 128:(mi + 1) * 128],
                                rhs=dtlow_bf[:, :], start=True, stop=True)
            tokd = S.op("tensor", fdt, waits=w, inc="sP")
            wa = [tokd]
            if prev_td[0] is not None:
                wa.append(prev_td[0])
            t_a = act(sp_a[:, :], PS[bank][:, :], AF.Abs,
                      bias=bdt_sb[:, mi:mi + 1], waits=wa)
            wr = [t_a]
            if prev_td[1] is not None:
                wr.append(prev_td[1])
            t_r = act(sp_b[:, :], PS[bank][:, :], AF.Relu,
                      bias=bdt_sb[:, mi:mi + 1], waits=wr)
            bank_tok[bank] = t_r
            t_e1 = act(sp_a[:, :], sp_a[:, :], AF.Exp, scale=-1.0, waits=[t_r])
            t_e2 = act(sp_a[:, :], sp_a[:, :], AF.Ln, bias=1.0, waits=[t_e1])
            i = mi % 2
            w2_ = [t_e2]
            if dtv_free[i] is not None:
                w2_.append(dtv_free[i])
            t_d = vop(lambda v, i=i: v.tensor_add(out=dtv_sb[i][:, :],
                                                  in0=sp_a[:, :], in1=sp_b[:, :]),
                      waits=w2_)
            prev_td[0] = t_d
            prev_td[1] = t_d
            dtv_free[i] = S.dma("sync", s_dt[b, mi * 128:(mi + 1) * 128,
                                             c * NT:(c + 1) * NT],
                                dtv_sb[i][:, :], waits=[t_d], sem=f"q_zt{i}")
        return bf_toks


    def phase_a(b, c):
        S.barrier_all(exclude=("sync",))
        mat_tok, pump = ffn_prefetch()
        pump(4)
        wx_ = list(xin_free[0])
        xin_free[0] = []
        tok_x = S.dma("sync", x_in[:, :].rearrange("p (pc t1) -> p pc t1", pc=PC_D),
                      x_in_d[b].rearrange("(pc p) t1 -> p pc t1", p=128)
                      [:, :, c * NT:(c + 1) * NT], sem="q_x", waits=wx_)
        reg_free["y_acc"] = []
        reg_free["p_bf"] = []
        bf_toks, tok_lo = ln_full(x_in[:, :], [tok_x], want_lo=True)
        tok_wT = routing(bf_toks, tok_lo)
        y_toks = ffn(bf_toks, tok_wT, mat_tok, pump)
        x1_toks = []
        for m in range(PC_D):
            t_x1 = vop(lambda v, m=m: v.scalar_tensor_tensor(
                out=x_in[:, m * NT:(m + 1) * NT],
                in0=y_acc[:, m * NT:(m + 1) * NT], scalar=0.5,
                in1=x_in[:, m * NT:(m + 1) * NT],
                op0=ALU.mult, op1=ALU.add), waits=[y_toks[m]])
            x1_toks.append(t_x1)
        t_x1s = S.dma("sync", s_x1[b][:, c * NT:(c + 1) * NT].rearrange(
            "(pc p) t1 -> p pc t1", p=128),
            x_in[:, :].rearrange("p (pc t1) -> p pc t1", pc=PC_D),
            waits=x1_toks, sem="q_x1s")
        ln2_toks = mamba_pre(b, c, x1_toks)
        xin_free[0] = [t_x1s, ln2_toks[-1]]

    def phase_b(b):
        irow_free = [None, None]
        slice_free = [None, None, None]
        yg_free = [None, None]
        vh_free = [None, None]
        u_free = [None, None]
        dA_free_big = [None, None]
        bint_free = [None]
        prev_tyg = [None]
        st_toks = [None] * PC_DI
        for tq in range(NTQ):
            bc_toks = []
            for (src, dst_int, bank) in ((s_bi, Bint, 6), (s_ci, Cint, 7)):
                for s_i in range(DS):
                    i = s_i % 2
                    w = []
                    if irow_free[i] is not None:
                        w.append(irow_free[i])
                    t_r = S.dma("sync", irow_sb[i][:, :],
                                src[b, s_i, tq * TQ:(tq + 1) * TQ],
                                waits=w, sem=f"q_ir{i}")
                    wq = [t_r]
                    if bank_tok[bank] is not None:
                        wq.append(bank_tok[bank])
                    if bint_free[0] is not None:
                        wq.append(bint_free[0])
                    def fb(t, i=i, bank=bank):
                        return t.matmul(out=PS[bank][:, :], lhsT=onesr_sb[:, :],
                                        rhs=irow_sb[i][0:1, :],
                                        start=True, stop=True)
                    tkb = S.op("tensor", fb, waits=wq, inc="sP")
                    t_ev = act(dst_int[:, s_i * TQ:(s_i + 1) * TQ],
                               PS[bank][:, :], AF.Identity, waits=[tkb])
                    bank_tok[bank] = t_ev
                    irow_free[i] = t_ev
                    bc_toks.append(t_ev)
            pre = {}

            def emit_pre(dc):
                i = dc % 3
                w = []
                if slice_free[i] is not None:
                    w.append(slice_free[i])
                t_dt = S.dma("sync", dts_sb[i][:, :],
                             s_dt[b, dc * 128:(dc + 1) * 128, tq * TQ:(tq + 1) * TQ],
                             waits=w, sem=f"q_dts{i}")
                t_xc = S.dma("sync", xcs_sb[i][:, :],
                             s_xc[b, dc * 128:(dc + 1) * 128, tq * TQ:(tq + 1) * TQ],
                             sem=f"q_xcs{i}")
                t_sz = S.dma("sync", szs_sb[i][:, :],
                             s_sz[b, dc * 128:(dc + 1) * 128, tq * TQ:(tq + 1) * TQ],
                             sem=f"q_szs{i}")
                wu = [t_dt, t_xc]
                if u_free[dc % 2] is not None:
                    wu.append(u_free[dc % 2])
                t_u = pop_(lambda g, i=i, dc=dc: g.tensor_mul(
                    out=u_sb2[dc % 2][:, :], in0=dts_sb[i][:, :],
                    in1=xcs_sb[i][:, :]), waits=wu)
                dw = [t_dt, t_an]
                if dA_free_big[dc % 2] is not None:
                    dw.append(dA_free_big[dc % 2])
                t_dA = None
                for s_i in range(DS):
                    t_dA = act(dA_big[dc % 2][:, s_i * TQ:(s_i + 1) * TQ],
                               dts_sb[i][:, :], AF.Exp,
                               scale=aneg_sb[:, dc * DS + s_i: dc * DS + s_i + 1],
                               waits=dw if s_i == 0 else ())
                pre[dc] = (t_dt, t_xc, t_sz, t_u, t_dA)

            def emit_head(dc):
                i = dc % 3
                vh = vh_bufs[dc % 2]
                dAb = dA_big[dc % 2]
                t_dt, t_xc, t_sz, t_u, t_dA = pre.pop(dc)
                wv = [t_u] + bc_toks
                if vh_free[dc % 2] is not None:
                    wv.append(vh_free[dc % 2])
                t_v = vop(lambda v, vh=vh, dc=dc: v.tensor_tensor(
                    out=vh[:, :].rearrange("p (s1 t1) -> p s1 t1", s1=DS),
                    in0=u_sb2[dc % 2][:, :].unsqueeze(1).to_broadcast([128, DS, TQ]),
                    in1=Bint[:, :].rearrange("p (s1 t1) -> p s1 t1", s1=DS),
                    op=ALU.mult), waits=wv)
                u_free[dc % 2] = t_v
                seg0 = dAb[:, :].rearrange("p (s1 t1) -> p s1 t1", t1=TQ)[:, :, 0]
                if tq > 0:
                    t_f1 = vop(lambda v, seg0=seg0, dc=dc: v.tensor_mul(
                        out=sfix_sb[:, 0:16], in0=seg0,
                        in1=state_sb[:, dc * DS:(dc + 1) * DS]),
                        waits=[t_dA, st_toks[dc]])
                    t_f2 = vop(lambda v, vh=vh: v.tensor_add(
                        out=vh[:, :].rearrange("p (s1 t1) -> p s1 t1", t1=TQ)[:, :, 0],
                        in0=vh[:, :].rearrange("p (s1 t1) -> p s1 t1", t1=TQ)[:, :, 0],
                        in1=sfix_sb[:, 0:16]), waits=[t_f1, t_v])
                else:
                    t_f2 = t_v
                t_z0 = vop(lambda v, seg0=seg0: v.tensor_scalar(
                    out=seg0, in0=seg0, scalar1=0.0, scalar2=None, op0=ALU.mult),
                    waits=[t_dA] + ([t_f1] if tq > 0 else []))
                t_sc = vop(lambda v, vh=vh, dAb=dAb: v.tensor_tensor_scan(
                    out=vh[:, :], data0=dAb[:, :], data1=vh[:, :],
                    initial=0.0, op0=ALU.mult, op1=ALU.add),
                    waits=[t_z0, t_f2], selfsync=True)
                dA_free_big[dc % 2] = t_sc
                t_st = vop(lambda v, dc=dc, vh=vh: v.tensor_copy(
                    out=state_sb[:, dc * DS:(dc + 1) * DS],
                    in_=vh[:, :].rearrange("p (s1 t1) -> p s1 t1", t1=TQ)[:, :, TQ - 1]),
                    waits=[t_sc], selfsync=True)
                st_toks[dc] = t_st
                t_hc = pop_(lambda g, vh=vh: g.tensor_mul(out=vh[:, :],
                                                          in0=vh[:, :],
                                                          in1=Cint[:, :]),
                            waits=[t_st])
                head[dc] = (t_hc, t_sz, i)

            def emit_tail(dc):
                t_hc, t_sz, i = head.pop(dc)
                vh = vh_bufs[dc % 2]
                wy = [t_hc]
                if prev_tyg[0] is not None:
                    wy.append(prev_tyg[0])
                t_y = vop(lambda v, vh=vh: v.tensor_reduce(
                    out=yt_sb[:, :],
                    in_=vh[:, :].rearrange("p (s1 t1) -> p t1 s1", s1=DS),
                    axis=AX, op=ALU.add), waits=wy, selfsync=True)
                vh_free[dc % 2] = t_y
                bint_free[0] = t_y
                t_yd = vop(lambda v, i=i, dc=dc: v.scalar_tensor_tensor(
                    out=yt_sb[:, :], in0=xcs_sb[i][:, :],
                    scalar=dpar_sb[:, dc:dc + 1], in1=yt_sb[:, :],
                    op0=ALU.mult, op1=ALU.add), waits=[t_y])
                iy = dc % 2
                wyg = [t_yd, t_sz]
                if yg_free[iy] is not None:
                    wyg.append(yg_free[iy])
                t_yg = pop_(lambda g, iy=iy: g.tensor_mul(
                    out=ygb_sb[iy][:, :], in0=yt_sb[:, :], in1=szs_sb[i][:, :]),
                    waits=wyg)
                prev_tyg[0] = t_yg
                yg_free[iy] = S.dma("gpsimd", s_yg[b, dc * 128:(dc + 1) * 128,
                                                  tq * TQ:(tq + 1) * TQ],
                                    ygb_sb[iy][:, :], waits=[t_yg],
                                    sem=f"q_yg{iy}")
                slice_free[i] = t_yg

            head = {}
            emit_pre(0)
            for dc in range(PC_DI):
                if dc + 1 < PC_DI:
                    emit_pre(dc + 1)
                emit_head(dc)
                if dc > 0:
                    emit_tail(dc - 1)
            emit_tail(PC_DI - 1)

    def phase_c(b, c):
        S.barrier_all(exclude=("sync",))
        yg_v = av(0, 16 * K, BF16)
        x2_v = av(0, 16 * K, F32)
        mat_tok_c, pump_c = ffn_prefetch()
        wc_ = list(c_free[0])
        c_free[0] = []
        tok_yg = S.dma("sync", yg_v[:, :].rearrange("p (kc t1) -> p kc t1", kc=PC_DI),
                       s_yg[b].rearrange("(kc p) t1 -> p kc t1", p=128)
                       [:, :, c * NT:(c + 1) * NT], sem="q_ygl", waits=wc_)
        tok_x1 = S.dma("sync", x1c_v[:, :].rearrange("p (pc t1) -> p pc t1", pc=PC_D),
                       s_x1[b].rearrange("(pc p) t1 -> p pc t1", p=128)
                       [:, :, c * NT:(c + 1) * NT], sem="q_x1l")
        t_wo = [load_slot(0, s_wout[0:1024].rearrange("(kc p) m -> p kc m", p=128),
                          waits=ptok["wout"]),
                load_slot(1, s_wout[1024:2048].rearrange("(kc p) m -> p kc m", p=128),
                          waits=ptok["wout"])]
        mm_toks = []
        for m in range(PC_D):
            pieces = []
            for kc in range(PC_DI):
                sl = 0 if kc < 8 else 1
                pieces.append((aw_slot(sl).rearrange("p (kc m1) -> p kc m1", kc=8)
                               [:, kc % 8, m * 128:(m + 1) * 128],
                               yg_v[:, kc * NT:(kc + 1) * NT]))
            mm_toks.append(mm_group(PS[m][:, :], m, pieces,
                                    waits=t_wo + [tok_yg]))
        x2_toks = []
        for m in range(PC_D):
            t_b = vop(lambda v, m=m: v.scalar_tensor_tensor(
                out=x2_v[:, m * NT:(m + 1) * NT], in0=PS[m][:, :],
                scalar=bout_sb[:, m:m + 1], in1=x1c_v[:, m * NT:(m + 1) * NT],
                op0=ALU.add, op1=ALU.add),
                waits=[mm_toks[m], tok_x1, mm_toks[PC_D - 1]])
            bank_tok[m] = t_b
            x2_toks.append(t_b)
        slot_free[0] = x2_toks[-1]
        slot_free[1] = x2_toks[-1]
        pump_c(4)
        reg_free["y_acc"] = []
        reg_free["p_bf"] = []
        bf_toks, tok_lo = ln_full(x2_v[:, :], x2_toks, want_lo=True)
        tok_wT = routing(bf_toks, tok_lo)
        y_toks = ffn(bf_toks, tok_wT, mat_tok_c, pump_c)
        out_toks = []
        for m in range(PC_D):
            t_o = vop(lambda v, m=m: v.scalar_tensor_tensor(
                out=x2_v[:, m * NT:(m + 1) * NT],
                in0=y_acc[:, m * NT:(m + 1) * NT], scalar=0.5,
                in1=x2_v[:, m * NT:(m + 1) * NT],
                op0=ALU.mult, op1=ALU.add), waits=[y_toks[m]])
            out_toks.append(t_o)
        t_os = S.dma("sync", out_d[b].rearrange("(pc p) t1 -> p pc t1", p=128)
              [:, :, c * NT:(c + 1) * NT],
              x2_v[:, :].rearrange("p (pc t1) -> p pc t1", pc=PC_D),
              waits=out_toks, sem="qOut")
        c_free[0] = [t_os, y_toks[-1]]

    xin_free[0] = [prep_toks[0], prep_toks[1]]
    for b in range(NB):
        for c in range(NCH):
            phase_a(b, c)
    for b in range(NB):
        S.barrier_all()
        phase_b(b)
    S.barrier_all()
    for b in range(NB):
        for c in range(NCH):
            phase_c(b, c)
    S.barrier_all()

    S.emit(block)
    st.close()
    return nc


def _prep_inputs(inputs):
    f = np.float32

    def t2(a):
        return np.ascontiguousarray(np.asarray(a).T)

    ii = {}
    ii["wg"] = t2(inputs["L0_ffn_gate_w"])
    ii["win"] = t2(inputs["L0_attn_W_in"])
    ii["binp"] = np.ascontiguousarray(
        np.asarray(inputs["L0_attn_b_in"]).reshape(32, 128).T)
    ii["cw"] = np.ascontiguousarray(
        np.asarray(inputs["L0_attn_conv_w"]).reshape(2048, 4)
        .reshape(16, 128, 4).transpose(1, 0, 2).reshape(128, 64))
    ii["cb"] = np.ascontiguousarray(
        np.asarray(inputs["L0_attn_conv_b"]).reshape(16, 128).T)
    ii["wx"] = t2(inputs["L0_attn_W_x"])
    ii["bx"] = np.ascontiguousarray(np.asarray(inputs["L0_attn_b_x"]).reshape(96, 1))
    ii["bxc"] = np.ascontiguousarray(
        np.asarray(inputs["L0_attn_b_x"])[80:96].reshape(16, 1))
    ii["wdt"] = t2(inputs["L0_attn_W_dt"])
    ii["bdt"] = np.ascontiguousarray(
        np.asarray(inputs["L0_attn_b_dt"]).reshape(16, 128).T)
    ii["alog"] = np.ascontiguousarray(
        np.asarray(inputs["L0_attn_A_log"]).reshape(16, 128, 16)
        .transpose(1, 0, 2).reshape(128, 256))
    ii["dpar"] = np.ascontiguousarray(
        np.asarray(inputs["L0_attn_D_param"]).reshape(16, 128).T)
    ii["wout"] = t2(inputs["L0_attn_W_out"])
    ii["bout"] = np.ascontiguousarray(
        np.asarray(inputs["L0_attn_b_out"]).reshape(8, 128).T)
    for nm, w_, b_ in (("w1", "L0_ffn_w1", "L0_ffn_b1"),
                       ("w2", "L0_ffn_w2", "L0_ffn_b2"),
                       ("w3", "L0_ffn_w3", "L0_ffn_b3")):
        ii[nm] = np.ascontiguousarray(np.asarray(inputs[w_]).transpose(0, 2, 1))
        ii["b" + nm[1]] = np.ascontiguousarray(
            np.asarray(inputs[b_]).T.reshape(8, 128, 8).transpose(1, 0, 2)
            .reshape(128, 64))
    ii["sw1"] = t2(inputs["L0_ffn_sw1"])
    ii["sw2"] = t2(inputs["L0_ffn_sw2"])
    ii["sw3"] = t2(inputs["L0_ffn_sw3"])
    ii["ident"] = np.eye(128, dtype=f)
    ii["onesc"] = np.ones((128, 1), dtype=f)
    ii["onesr"] = np.ones((1, 128), dtype=f)
    ii["constv"] = np.tile(np.array([[1e-5, 1.0, -1.0, -0.5]], dtype=f), (128, 1))
    sel8 = np.zeros((8, 8 * 128), dtype=f)
    for e_ in range(8):
        sel8[e_, e_ * 128:(e_ + 1) * 128] = 1.0
    ii["sel8"] = sel8
    for k in ii:
        assert ii[k].dtype == np.float32, k
    return ii


def kernel(**inputs):
    x = np.asarray(inputs["L0_attn_x"])
    shared = _prep_inputs(inputs)
    nc = build_program()
    in_maps = []
    for core in range(NCORES):
        m = dict(shared)
        m["x"] = np.ascontiguousarray(
            x[core * NB:(core + 1) * NB].transpose(0, 2, 1))
        in_maps.append(m)
    res = run_bass_kernel_spmd(nc, in_maps, core_ids=list(range(NCORES)))
    outs = [np.asarray(res.results[core]["out"]).transpose(0, 2, 1)
            for core in range(NCORES)]
    return np.ascontiguousarray(np.concatenate(outs, axis=0))


if __name__ == "__main__":
    import reference as _r
    ins = {k: np.asarray(v) for k, v in _r.setup_inputs().items()}
    out = kernel(**ins)
    ref = np.asarray(_r.reference(**ins))
    rv = float(np.mean((out - ref) ** 2) / np.mean(ref ** 2))
    print("maxabs", np.abs(out - ref).max(), "relvar", rv)
